# version 04
# baseline (speedup 1.0000x reference)
import numpy as np
import math
from contextlib import ExitStack

import concourse.bass as bass
import concourse.mybir as mybir
from concourse.bass_utils import run_bass_kernel_spmd
from concourse.alu_op_type import AluOpType
from concourse import library_config

F32 = mybir.dt.float32
BF16 = mybir.dt.bfloat16
AF = mybir.ActivationFunctionType
AX = mybir.AxisListType

NCORES = 8
B = 2
S = 16384
D = 1024
NKC = D // 128

L0DEPTH = 6
HEADS = 16
DH = 64
WIN = 256
DI = 2730
W1OUT = 2 * DI
QK_SCALE = 8.0

NEXP = 8
RH = 1024
SHID = 4096
L1_SCALE = 1.0 / 32.0
UB_SHIFT = 30.0

NEG = -3.0e38

TOWN = 4096
HALO = 2048
T = TOWN + HALO
ENG_ATTR = {"t": "tensor", "v": "vector", "s": "scalar", "g": "gpsimd", "y": "sync"}


def set_config(town=None, halo=None, seq=None, l0depth=None):
    global TOWN, HALO, T, S, L0DEPTH
    if town is not None:
        TOWN = town
    if halo is not None:
        HALO = halo
    if seq is not None:
        S = seq
    if l0depth is not None:
        L0DEPTH = l0depth
    T = TOWN + HALO


class Ctx:
    ENGS = ("t", "v", "s", "g", "y")

    def __init__(self, nc, stack):
        self.nc = nc
        self.cnt = {e: 0 for e in self.ENGS}
        self.pend = {e: [] for e in self.ENGS}
        self.marks = {e: [] for e in self.ENGS}
        self.hwm = {}
        self.writer = {}
        self.readers = {}
        self.undrained = {e: set() for e in self.ENGS}
        self.unread = {e: set() for e in self.ENGS}
        self.sem = {e: stack.enter_context(nc.semaphore(f"sem_{e}"))
                    for e in self.ENGS + ("gd",)}
        self.cnt["gd"] = 0
        self.need = {"y": set(), "gd": set()}
        self.flushed = {"y": 0, "gd": 0}

    def _ticket(self, eng):
        if eng == "gd":
            return self.cnt["gd"]
        if self.pend[eng] and self.marks[eng]:
            last = None
            for rec in reversed(self.pend[eng]):
                if rec[0] is not None:
                    last = rec
                    break
            if last is not None and not last[4]:
                last[4] = True
                self.cnt[eng] += last[3]
            tick = self.cnt[eng]
            for _, ws, rs in self.marks[eng]:
                for b in ws:
                    w = self.writer.get(b)
                    if w is not None and w[0] == eng and w[1] is None:
                        self.writer[b] = (eng, tick, w[2])
                for b in rs:
                    r = self.readers.get(b, {}).get(eng)
                    if r is not None and r[0] is None:
                        self.readers[b][eng] = (tick, r[1])
            self.marks[eng] = []
        return self.cnt[eng]

    def op(self, eng, fn, reads=(), writes=(), dma=False, coll=False):
        waits = []
        drain = False

        def tick_of(psk, t):
            v = t if t is not None else self._ticket(psk)
            if psk in ("y", "gd") and v > self.flushed[psk]:
                self.need.setdefault(psk, set()).add(v)
            return v

        own = lambda sk: sk[0] == eng
        for b in reads:
            w = self.writer.get(b)
            if w is None:
                continue
            wsk, wtick, wdma = w
            if own(wsk) and not wdma:
                if b in self.undrained[eng]:
                    drain = True
            else:
                waits.append((wsk, tick_of(wsk, wtick)))
        for b in writes:
            for rsk, (rtick, rdma) in list(self.readers.get(b, {}).items()):
                if not own(rsk) or rdma:
                    waits.append((rsk, tick_of(rsk, rtick)))
                elif b in self.unread[eng]:
                    drain = True
            w = self.writer.get(b)
            if w is not None:
                wsk, wtick, wdma = w
                if not own(wsk) or wdma:
                    waits.append((wsk, tick_of(wsk, wtick)))
        fwaits = []
        for peng, tick in waits:
            if peng in ("y", "gd") and tick <= self.flushed[peng]:
                continue
            key = (eng, peng)
            if self.hwm.get(key, 0) < tick:
                self.hwm[key] = tick
                fwaits.append((peng, tick))
        if drain:
            self.undrained[eng].clear()
            self.unread[eng].clear()
        quant = 16 if dma else 1
        sealed = dma or coll
        semkey = "gd" if (dma and eng == "g") else eng
        self.pend[eng].append([fn, fwaits, drain, quant, sealed, semkey])
        if sealed:
            self.cnt[semkey] += quant
            tick = self.cnt[semkey]
        else:
            tick = None
        for b in writes:
            self.writer[b] = (semkey, tick, dma)
            self.readers.pop(b, None)
            if not dma:
                self.undrained[eng].add(b)
        for b in reads:
            self.readers.setdefault(b, {})[semkey] = (tick, dma)
            if not dma:
                self.unread[eng].add(b)
        if not sealed:
            self.marks[eng].append((len(self.pend[eng]) - 1, tuple(writes), tuple(reads)))

    def dma(self, q, out, in_, reads=(), writes=(), **kw):
        self.op(q, lambda E: E.dma_start(out=out, in_=in_, **kw),
                reads=reads, writes=writes, dma=True)

    def barrier_wait_all(self, eng):
        for p in self.ENGS + ("gd",):
            if p == eng:
                continue
            t = self._ticket(p)
            key = (eng, p)
            if t and self.hwm.get(key, 0) < t:
                self.hwm[key] = t
                self.pend[eng].append([None, [(p, t)], False, 1, False, eng])
                self.marks[eng].append((len(self.pend[eng]) - 1, (), ()))

    def flush(self, block):
        for e in self.ENGS:
            self._ticket(e)
        sem = self.sem
        post = {}
        for e in ("y", "g"):
            tot = dict(self.flushed)
            for idx, rec in enumerate(self.pend[e]):
                sk = rec[5]
                if sk in ("y", "gd") and rec[4] and rec[0] is not None:
                    tot[sk] = tot.get(sk, 0) + rec[3]
                    if tot[sk] in self.need.get(sk, ()):
                        post[(e, idx)] = (sk, tot[sk])
            for sk in ("y", "gd"):
                if sk in tot:
                    self.flushed[sk] = tot[sk]
        for e in self.ENGS:
            ops = self.pend[e]
            if not ops:
                continue
            def run(engobj, ops=ops, e=e, post=post):
                for idx, (fn, fwaits, drain, quant, inc, semkey) in enumerate(ops):
                    for psk, tick in fwaits:
                        engobj.wait_ge(sem[psk], tick)
                    if drain:
                        engobj.drain()
                    if fn is None:
                        if inc:
                            engobj.nop().then_inc(sem[semkey], quant)
                    else:
                        inst = fn(engobj)
                        if inc:
                            inst.then_inc(sem[semkey], quant)
                    pw = post.get((e, idx))
                    if pw is not None:
                        engobj.wait_ge(sem[pw[0]], pw[1])
            getattr(block, ENG_ATTR[e])(run)
            self.pend[e] = []
            self.undrained[e].clear()
            self.unread[e].clear()


def cdiv(a, b):
    return (a + b - 1) // b


def host_inputs(core, full):
    b, r = core // 4, core % 4
    f32 = np.float32
    d = {}
    x = np.asarray(full["L0_attn_x"])
    start = r * TOWN
    lo = start - HALO
    sl = np.zeros((T, D), f32)
    src_lo = max(lo, 0)
    sl[src_lo - lo:, :] = x[b, src_lo:start + TOWN, :]
    d["xin"] = np.ascontiguousarray(sl.T)

    pos = np.clip(np.arange(lo, start + TOWN), 0, S - 1).astype(np.float64)
    half = np.arange(0, DH, 2, dtype=np.float64)
    inv_freq = 1.0 / 10000.0 ** (half / DH)
    freqs = pos[:, None] * inv_freq[None, :]
    base_scale = (half + 0.4 * DH) / (1.4 * DH)
    power = (pos - S // 2) / (WIN // 2)
    scale = base_scale[None, :] ** power[:, None]
    cos, sin = np.cos(freqs), np.sin(freqs)
    def build(scl):
        A32 = (cos * scl).T
        B32 = (sin * scl).T
        A64 = np.concatenate([A32, A32], 0)
        D64 = np.concatenate([B32, -B32], 0)
        A = np.concatenate([A64, A64], 0).astype(f32)
        Dt = np.concatenate([D64, D64], 0).astype(f32)
        return np.ascontiguousarray(A), np.ascontiguousarray(Dt)
    d["tAq"], d["tDq"] = build(scale)
    d["tAk"], d["tDk"] = build(QK_SCALE / scale)

    for nm, w in (("wq", "L0_attn_Wq"), ("wk", "L0_attn_Wk"), ("wv", "L0_attn_Wv")):
        d[f"l0a_{nm}T"] = np.ascontiguousarray(np.asarray(full[w]).T)
    d["l0a_bq"] = np.ascontiguousarray(np.asarray(full["L0_attn_bq"]).astype(f32).reshape(NKC, 128).T)
    d["l0a_bk"] = np.ascontiguousarray(np.asarray(full["L0_attn_bk"]).astype(f32).reshape(NKC, 128).T)
    d["l0a_bv"] = np.asarray(full["L0_attn_bv"]).reshape(1, D).astype(f32)

    d["l0f_wqkv"] = np.ascontiguousarray(np.asarray(full["L0_ffn_Wqkv"]))
    d["l0f_wo"] = np.ascontiguousarray(np.asarray(full["L0_ffn_Wo"]))
    d["l0f_w1"] = np.ascontiguousarray(np.asarray(full["L0_ffn_W1"]))
    d["l0f_w2"] = np.ascontiguousarray(np.asarray(full["L0_ffn_W2"]))
    pc = lambda a: np.ascontiguousarray(np.asarray(a).astype(f32).reshape(-1, NKC, 128).transpose(0, 2, 1))
    d["l0f_g"] = pc(full["L0_ffn_ln_g"])
    d["l0f_b"] = pc(full["L0_ffn_ln_b"])
    d["l0f_b2"] = pc(full["L0_ffn_b2"])
    b1 = np.asarray(full["L0_ffn_b1"]).astype(f32)
    NJ = cdiv(DI, 128)
    b1c = np.zeros((L0DEPTH, 128, 2 * NJ), f32)
    for j in range(NJ):
        sz = min(128, DI - j * 128)
        b1c[:, :sz, 2 * j] = b1[:, j * 128:j * 128 + sz]
        b1c[:, :sz, 2 * j + 1] = b1[:, DI + j * 128:DI + j * 128 + sz]
    d["l0f_b1c"] = b1c
    def selput(sc):
        out = np.zeros((L0DEPTH, 16, D), f32)
        for h in range(16):
            out[:, h, DH * h:DH * (h + 1)] = sc
        return out
    d["l0f_selq"] = selput(np.asarray(full["L0_ffn_q_scale"]).astype(f32))
    d["l0f_selk"] = selput(np.asarray(full["L0_ffn_k_scale"]).astype(f32))

    for i in (1, 2, 3, 4):
        d[f"l1a_w{i}"] = np.ascontiguousarray(np.asarray(full[f"L1_attn_W{i}"]))
        d[f"l1a_b{i}"] = np.ascontiguousarray(np.asarray(full[f"L1_attn_b{i}"]).astype(f32).reshape(NKC, 128).T)
    d["l1a_b3r"] = np.asarray(full["L1_attn_b3"]).reshape(1, D).astype(f32)

    d["l1f_w12T"] = np.ascontiguousarray(np.asarray(full["L1_ffn_w12"]).transpose(0, 2, 1))
    d["l1f_w3T"] = np.ascontiguousarray(np.asarray(full["L1_ffn_w3"]).transpose(0, 2, 1))
    d["l1f_sw1T"] = np.ascontiguousarray(np.asarray(full["L1_ffn_shared_w1"]).T)
    d["l1f_sw2T"] = np.ascontiguousarray(np.asarray(full["L1_ffn_shared_w2"]).T)
    d["l1f_sw3T"] = np.ascontiguousarray(np.asarray(full["L1_ffn_shared_w3"]).T)
    d["l1f_rwT"] = np.ascontiguousarray(np.asarray(full["L1_ffn_router_w"]).T.astype(f32))
    d["l1f_eb"] = np.asarray(full["L1_ffn_expert_bias"]).reshape(NEXP, 1).astype(f32)

    d["c_ident"] = np.eye(128, dtype=f32)
    sw = np.zeros((128, 128), f32)
    for i_ in range(128):
        sw[i_, (i_ // 64) * 64 + ((i_ % 64) + 32) % 64] = 1.0
    d["c_swap"] = sw
    d["c_ones"] = np.ones((128, 128), f32)
    ar = np.arange(128, dtype=f32)
    d["c_ms1"] = np.stack([ar, ar + 128.0], 1)
    d["c_me1"] = d["c_ms1"] + 257.0
    ms2 = np.zeros((128, 4), f32)
    me2 = np.zeros((128, 4), f32)
    for kc in range(4):
        kj = 128.0 * kc + ar
        ms2[:, kc] = np.maximum(0.0, kj - 256.0)
        me2[:, kc] = np.minimum(256.0, kj + 1.0)
    d["c_ms2"], d["c_me2"] = ms2, me2
    NEGM = np.float32(-1.0e30)
    mk1 = np.zeros((4, 128, 512), f32)
    for fw in range(2):
        for qc in range(2):
            qi = 128 * qc + np.arange(128)[:, None]
            kj = np.arange(512)[None, :]
            if fw:
                bad = (kj < 256) | (kj > 256 + qi)
            else:
                bad = (kj < qi) | (kj > qi + 256)
            mk1[2 * fw + qc][bad] = NEGM
    d["c_mk1"] = mk1
    mk2 = np.zeros((4, 128, 256), f32)
    for kt in range(4):
        kj = 128 * kt + np.arange(128)[:, None]
        qi = np.arange(256)[None, :]
        bad = (qi < kj - 256) | (qi > kj)
        mk2[kt][bad] = NEGM
    d["c_mk2"] = mk2
    bd = np.zeros((128, NKC, 16), f32)
    for c in range(NKC):
        bd[0:64, c, 2 * c] = 1.0
        bd[64:128, c, 2 * c + 1] = 1.0
    d["c_bdsel"] = bd.reshape(128, NKC * 16)
    return d


def np_f32(x):
    return np.ascontiguousarray(np.asarray(x, dtype=np.float32))


def build_program():
    nc = bass.Bass("TRN2", target_bir_lowering=False)
    TT = 512
    NT = T // TT
    NTO = TOWN // TT
    NJ = cdiv(DI, 128)
    NCH = TOWN // 1024

    dt = lambda n, s, d, k="ExternalInput": nc.dram_tensor(n, s, d, kind=k).ap()
    xin = dt("xin", [D, T], F32)
    tA = {"q": dt("tAq", [128, T], F32), "k": dt("tAk", [128, T], F32)}
    tD = {"q": dt("tDq", [128, T], F32), "k": dt("tDk", [128, T], F32)}
    l0a_wT = {n: dt(f"l0a_w{n}T", [D, D], F32) for n in "qkv"}
    l0a_bq = dt("l0a_bq", [128, NKC], F32)
    l0a_bk = dt("l0a_bk", [128, NKC], F32)
    l0a_bv = dt("l0a_bv", [1, D], F32)
    l0f_wqkv = dt("l0f_wqkv", [L0DEPTH, D, 3 * D], F32)
    l0f_wo = dt("l0f_wo", [L0DEPTH, D, D], F32)
    l0f_w1 = dt("l0f_w1", [L0DEPTH, D, W1OUT], F32)
    l0f_w2 = dt("l0f_w2", [L0DEPTH, DI, D], F32)
    l0f_g = dt("l0f_g", [L0DEPTH, 128, NKC], F32)
    l0f_b = dt("l0f_b", [L0DEPTH, 128, NKC], F32)
    l0f_b1c = dt("l0f_b1c", [L0DEPTH, 128, 2 * NJ], F32)
    l0f_b2 = dt("l0f_b2", [L0DEPTH, 128, NKC], F32)
    l0f_selq = dt("l0f_selq", [L0DEPTH, 16, D], F32)
    l0f_selk = dt("l0f_selk", [L0DEPTH, 16, D], F32)
    l1a_w = {i: dt(f"l1a_w{i}", [D, D], F32) for i in (1, 2, 3, 4)}
    l1a_b = {i: dt(f"l1a_b{i}", [128, NKC], F32) for i in (1, 2, 3, 4)}
    l1a_b3r = dt("l1a_b3r", [1, D], F32)
    l1f_w12T = dt("l1f_w12T", [NEXP, D, 2 * RH], F32)
    l1f_w3T = dt("l1f_w3T", [NEXP, RH, D], F32)
    l1f_sw1T = dt("l1f_sw1T", [D, SHID], F32)
    l1f_sw2T = dt("l1f_sw2T", [D, SHID], F32)
    l1f_sw3T = dt("l1f_sw3T", [SHID, D], F32)
    l1f_rwT = dt("l1f_rwT", [D, NEXP], F32)
    l1f_eb = dt("l1f_eb", [NEXP, 1], F32)
    c_ident = dt("c_ident", [128, 128], F32)
    c_swap = dt("c_swap", [128, 128], F32)
    c_ones = dt("c_ones", [128, 128], F32)
    c_ms1 = dt("c_ms1", [128, 2], F32)
    c_me1 = dt("c_me1", [128, 2], F32)
    c_ms2 = dt("c_ms2", [128, 4], F32)
    c_me2 = dt("c_me2", [128, 4], F32)
    c_bdsel = dt("c_bdsel", [128, NKC * 16], F32)
    c_mk1 = dt("c_mk1", [4, 128, 512], F32)
    c_mk2 = dt("c_mk2", [4, 128, 256], F32)
    out = dt("out", [D, TOWN], F32, k="ExternalOutput")
    ya = dt("ya", [D, T], F32, k="Internal")
    yb = dt("yb", [D, T], F32, k="Internal")
    a0h = dt("a0h", [D, TOWN], F32, k="Internal")
    x1h = dt("x1h", [D, TOWN], F32, k="Internal")
    x2ph = dt("x2ph", [D, TOWN], F32, k="Internal")
    xn1h = dt("xn1h", [D, TOWN], BF16, k="Internal")
    agk_in = dt("agk_in", [D, TOWN], BF16, k="Internal")
    agk_out = dt("agk_out", [4 * D, TOWN], BF16, k="Internal")
    agv_in = dt("agv_in", [TOWN, D], BF16, k="Internal")
    agv_out = dt("agv_out", [4 * TOWN, D], BF16, k="Internal")
    mk_in = dt("mk_in", [1, 1], F32, k="Internal")
    mscr = dt("mscr", [64 * 128], BF16, k="Internal")
    gatesd = dt("gatesd", [NEXP, TOWN], F32, k="Internal")
    lscr = dt("lscr", [32 * 256], BF16, k="Internal")
    lscr2 = dt("lscr2", [16 * 512], F32, k="Internal")
    augd = dt("augd", [1, TOWN], BF16, k="Internal")
    qd = dt("qd", [D, TOWN], BF16, k="Internal")
    mk_out = dt("mk_out", [1, 1], F32, k="Internal")

    def fm(dram, c0, w, nk=NKC):
        return dram[:, c0:c0 + w].rearrange("(c p) t -> p c t", c=nk)

    uid = [0]

    def sb(stk, shape, dtyp):
        uid[0] += 1
        return stk.enter_context(nc.sbuf_tensor(f"b{uid[0]}", shape, dtyp))

    def ps(stk, shape, dtyp=F32):
        uid[0] += 1
        return stk.enter_context(nc.psum_tensor(f"p{uid[0]}", shape, dtyp))

    with ExitStack() as big:
        ctx = Ctx(nc, big)
        rg = [[0, 1, 2, 3], [4, 5, 6, 7]]

        ident = sb(big, [128, 128], F32)
        identb = sb(big, [128, 128], BF16)
        swpb = sb(big, [128, 128], BF16)
        onesb = sb(big, [128, 128], BF16)
        onesfr = sb(big, [1, 128], F32)
        bdsel = sb(big, [128, NKC * 16], BF16)
        mk1 = sb(big, [128, 4, 512], BF16)
        mk2 = sb(big, [128, 4, 256], BF16)

        def phase(fn):
            fn()
            ctx.barrier_wait_all("y")
            ctx.pend["y"].append([None, [("y", ctx.cnt["y"])], False, 1, False, "y"])
            ctx.pend["g"].append([None, [("gd", ctx.cnt["gd"])], False, 1, False, "g"])
            with nc.Block() as blk:
                ctx.flush(blk)

        def ph_consts():
            ctx.dma("y", ident[:], c_ident, writes=["ident"])
            ctx.dma("g", identb[:], c_ident, writes=["identb"])
            ctx.dma("g", swpb[:], c_swap, writes=["swpb"])
            ctx.dma("g", onesb[:], c_ones, writes=["onesb"])
            ctx.dma("y", onesfr[:], c_ones[0:1, :], writes=["onesfr"])
            ctx.dma("g", bdsel[:], c_bdsel, writes=["bdsel"])
            ctx.dma("g", mk1[:], c_mk1.rearrange("v p x -> p v x"), writes=["mk1"])
            ctx.dma("g", mk2[:], c_mk2.rearrange("v p x -> p v x"), writes=["mk2"])
        phase(ph_consts)

        def ph_outer_ln():
            with ExitStack() as stk:
                ysb = sb(stk, [128, NKC, TT], F32)
                xb = sb(stk, [128, NKC, TT], BF16)
                xf = sb(stk, [128, NKC, TT], F32)
                sq = sb(stk, [128, TT], BF16)
                rowb = sb(stk, [1, 2 * TT], F32)
                pum = ps(stk, [1, TT])
                pq2 = ps(stk, [1, TT])
                pbc = ps(stk, [128, 1024])
                for i in range(NT):
                    c0 = i * TT
                    if True:
                        ctx.dma("g", xb[:], fm(xin, c0, TT), writes=[xb.name])
                        ctx.dma("y", xf[:], fm(xin, c0, TT), writes=[xf.name])
                        for kc in range(NKC):
                            ctx.op("s", lambda E, kc=kc: E.activation(sq[:], xb[:, kc, :], AF.Square),
                                   reads=[xb.name], writes=[sq.name])
                            ctx.op("t", lambda E, kc=kc: E.matmul(
                                pum[:], onesb[:, 0:1], xb[:, kc, :],
                                start=(kc == 0), stop=(kc == NKC - 1), skip_group_check=True),
                                reads=[xb.name, "onesb"], writes=[pum.name])
                            ctx.op("t", lambda E, kc=kc: E.matmul(
                                pq2[:], onesb[:, 0:1], sq[:],
                                start=(kc == 0), stop=(kc == NKC - 1), skip_group_check=True),
                                reads=[sq.name, "onesb"], writes=[pq2.name])
                        mu = rowb[0:1, 0:TT]
                        var = rowb[0:1, TT:2 * TT]
                        rn = rowb.name
                        ctx.op("v", lambda E: E.tensor_scalar(mu, pum[:], 1.0 / D, None, op0=AluOpType.mult),
                               reads=[pum.name], writes=[rn])
                        ctx.op("v", lambda E: E.scalar_tensor_tensor(
                            var, mu, 1.0, mu, op0=AluOpType.mult, op1=AluOpType.mult),
                            reads=[rn], writes=[rn])
                        ctx.op("v", lambda E: E.scalar_tensor_tensor(
                            var, pq2[:], 1.0 / D, var, op0=AluOpType.mult, op1=AluOpType.subtract),
                            reads=[pq2.name, rn], writes=[rn])
                        ctx.op("v", lambda E: E.tensor_scalar(var, var, 1e-5, None, op0=AluOpType.add),
                               reads=[rn], writes=[rn])
                        ctx.op("v", lambda E: E.reciprocal(var, var), reads=[rn], writes=[rn])
                        ctx.op("s", lambda E: E.activation(var, var, AF.Sqrt),
                               reads=[rn], writes=[rn])
                        ctx.op("v", lambda E: E.scalar_tensor_tensor(
                            mu, mu, -1.0, var, op0=AluOpType.mult, op1=AluOpType.mult),
                            reads=[rn], writes=[rn])
                        ctx.op("t", lambda E: E.matmul(
                            pbc[:, 0:512], onesfr[0:1, 0:128], var,
                            start=True, stop=True, skip_group_check=True),
                            reads=["onesfr", rn], writes=[pbc.name])
                        ctx.op("t", lambda E: E.matmul(
                            pbc[:, 512:1024], onesfr[0:1, 0:128], mu,
                            start=True, stop=True, skip_group_check=True),
                            reads=["onesfr", rn], writes=[pbc.name])
                        for kc in range(NKC):
                            ctx.op("v", lambda E, kc=kc: E.tensor_tensor(
                                ysb[:, kc, :], xf[:, kc, :], pbc[:, 0:512], op=AluOpType.mult),
                                reads=[xf.name, pbc.name], writes=[ysb.name])
                            ctx.op("v", lambda E, kc=kc: E.tensor_tensor(
                                ysb[:, kc, :], ysb[:, kc, :], pbc[:, 512:1024], op=AluOpType.add),
                                reads=[ysb.name, pbc.name], writes=[ysb.name])
                        ctx.dma("y", fm(ya, c0, TT), ysb[:], reads=[ysb.name], writes=["ya"])
        phase(ph_outer_ln)



        def ph_l0_attn():
            with ExitStack() as stk:
                wq = sb(stk, [128, NKC, D], BF16)
                wk = sb(stk, [128, NKC, D], BF16)
                wv = sb(stk, [128, NKC, D], BF16)
                bqs = sb(stk, [128, NKC], F32)
                bks = sb(stk, [128, NKC], F32)
                bvr = sb(stk, [1, D], BF16)
                ctx.dma("g", wq[:], l0a_wT["q"].rearrange("(c p) e -> p c e", c=NKC), writes=[wq.name])
                ctx.dma("g", wk[:], l0a_wT["k"].rearrange("(c p) e -> p c e", c=NKC), writes=[wk.name])
                ctx.dma("g", wv[:], l0a_wT["v"].rearrange("(c p) e -> p c e", c=NKC), writes=[wv.name])
                ctx.dma("y", bqs[:], l0a_bq, writes=[bqs.name])
                ctx.dma("y", bks[:], l0a_bk, writes=[bks.name])
                ctx.dma("g", bvr[:], l0a_bv, writes=[bvr.name])
                cs = stk
                xc = sb(cs, [128, NKC, 1024], BF16)
                qb = sb(cs, [128, NKC, 1024], BF16)
                kb = sb(cs, [128, NKC, 1024], BF16)
                vb = sb(cs, [128, NKC, 1024], BF16)
                mcols = sb(cs, [128, 16], F32)
                mc8 = sb(cs, [128, 8], F32)
                mrows = sb(cs, [8, 128], BF16)
                mflat = sb(cs, [1, 8 * 128], BF16)
                scr = sb(cs, [128, 512], F32)
                st2 = sb(cs, [128, 256], F32)
                pt = sb(cs, [128, 8, 256], BF16)
                lsb = sb(cs, [1, 256], F32)
                rbb = sb(cs, [128, 256], F32)
                a0sb = sb(cs, [128, NKC, 256], F32)
                pq = ps(cs, [128, 512])
                ps1 = pq
                pmx = ps(cs, [8, 512])
                pmt = pmx[0:8, 0:128]
                plv = pmx[0:1, 128:384]
                pstA = ps(cs, [128, 512])
                pstB = ps(cs, [128, 512])
                pst2 = (pstA, pstB)
                pav = ps(cs, [128, NKC, 256])
                for ch in range(TOWN // 1024):
                    if True:
                        c0 = HALO + 1024 * ch
                        ctx.dma("g", xc[:], fm(ya, c0, 1024), reads=["ya"], writes=[xc.name])
                        for (wsb, bias, dst) in ((wq, bqs, qb), (wk, bks, kb)):
                            for oc in range(NKC):
                                for nt in range(2):
                                    for kc in range(NKC):
                                        ctx.op("t", lambda E, oc=oc, nt=nt, kc=kc, wsb=wsb: E.matmul(
                                            pq[:], wsb[:, kc, oc * 128:oc * 128 + 128],
                                            xc[:, kc, nt * 512:nt * 512 + 512],
                                            start=(kc == 0), stop=(kc == NKC - 1), skip_group_check=True),
                                            reads=[wsb.name, xc.name], writes=[pq.name])
                                    ctx.op("s", lambda E, oc=oc, nt=nt, bias=bias, dst=dst: E.activation(
                                        dst[:, oc, nt * 512:nt * 512 + 512], pq[:], AF.Identity,
                                        bias=bias[:, oc:oc + 1]),
                                        reads=[pq.name, bias.name], writes=[dst.name])
                        for tch in range(8):
                            for nt in range(2):
                                for kc in range(NKC):
                                    ctx.op("t", lambda E, tch=tch, nt=nt, kc=kc: E.matmul(
                                        pq[:], xc[:, kc, tch * 128:tch * 128 + 128],
                                        wv[:, kc, nt * 512:nt * 512 + 512],
                                        start=(kc == 0), stop=False, skip_group_check=True),
                                        reads=[wv.name, xc.name], writes=[pq.name])
                                ctx.op("t", lambda E, nt=nt: E.matmul(
                                    pq[:], onesb[0:1, 0:128], bvr[0:1, nt * 512:nt * 512 + 512],
                                    start=False, stop=True, skip_group_check=True),
                                    reads=["onesb", bvr.name], writes=[pq.name])
                                ctx.op("s", lambda E, tch=tch, nt=nt: E.activation(
                                    vb[:, tch, nt * 512:nt * 512 + 512], pq[:], AF.Copy),
                                    reads=[pq.name], writes=[vb.name])
                        for qc in range(8):
                            for kt in range(2):
                                for ec in range(NKC):
                                    ctx.op("t", lambda E, qc=qc, kt=kt, ec=ec: E.matmul(
                                        ps1[:], qb[:, ec, qc * 128:qc * 128 + 128],
                                        kb[:, ec, kt * 512:kt * 512 + 512],
                                        start=(ec == 0), stop=(ec == NKC - 1), skip_group_check=True),
                                        reads=[qb.name, kb.name], writes=[ps1.name])
                                ctx.op("v", lambda E, qc=qc, kt=kt: E.reduce_max(
                                    mcols[:, 2 * qc + kt:2 * qc + kt + 1], ps1[:], axis=AX.X),
                                    reads=[ps1.name], writes=[mcols.name])
                        ctx.op("v", lambda E: E.tensor_tensor(
                            mc8[:], mcols[:].rearrange("p (a b) -> p a b", b=2)[:, :, 0],
                            mcols[:].rearrange("p (a b) -> p a b", b=2)[:, :, 1],
                            op=AluOpType.max),
                            reads=[mcols.name], writes=[mc8.name])
                        ctx.op("t", lambda E: E.transpose(pmt, mc8[:], ident[:]),
                               reads=[mc8.name, "ident"], writes=[pmx.name])
                        ctx.op("s", lambda E: E.activation(mrows[:], pmt, AF.Identity, scale=-1.0),
                               reads=[pmx.name], writes=[mrows.name])
                        ctx.dma("y", mscr[0:8 * 128].rearrange("(a b) -> a b", b=128), mrows[:],
                                reads=[mrows.name], writes=["mscr"])
                        ctx.dma("y", mflat[:], mscr[0:8 * 128].rearrange("(a b) -> a b", a=1),
                                reads=["mscr"], writes=[mflat.name])
                        for qq in range(4):
                            for kt in range(8):
                                pS = pst2[kt % 2]
                                for ec in range(NKC):
                                    ctx.op("t", lambda E, qq=qq, kt=kt, ec=ec, pS=pS: E.matmul(
                                        pS[:, 0:256], kb[:, ec, kt * 128:kt * 128 + 128],
                                        qb[:, ec, qq * 256:qq * 256 + 256],
                                        start=(ec == 0), stop=False, skip_group_check=True),
                                        reads=[qb.name, kb.name], writes=[pS.name])
                                for half in range(2):
                                    ctx.op("t", lambda E, qq=qq, half=half, pS=pS: E.matmul(
                                        pS[:, half * 128:half * 128 + 128],
                                        onesb[0:1, 0:128],
                                        mflat[0:1, 128 * (2 * qq + half):128 * (2 * qq + half) + 128],
                                        start=False, stop=(half == 1), skip_group_check=True),
                                        reads=["onesb", mflat.name], writes=[pS.name])
                                ctx.op("s", lambda E, kt=kt, pS=pS: E.activation(
                                    pt[:, kt, :], pS[:, 0:256], AF.Exp),
                                    reads=[pS.name], writes=[pt.name])
                                ctx.op("t", lambda E, kt=kt: E.matmul(
                                    plv, onesb[:, 0:1], pt[:, kt, :],
                                    start=(kt == 0), stop=(kt == 7), skip_group_check=True),
                                    reads=["onesb", pt.name], writes=[pmx.name])
                            ctx.op("v", lambda E: E.tensor_copy(lsb[:], plv),
                                   reads=[pmx.name], writes=[lsb.name])
                            ctx.op("v", lambda E: E.reciprocal(lsb[:], lsb[:]),
                                   reads=[lsb.name], writes=[lsb.name])
                            ctx.dma("y", lscr2[0:256].rearrange("(a b) -> a b", a=1), lsb[:],
                                    reads=[lsb.name], writes=["lscr2"])
                            ctx.dma("y", rbb[:],
                                    lscr2[0:256].rearrange("(a b) -> a b", a=1)
                                    .broadcast_to([128, 256]),
                                    reads=["lscr2"], writes=[rbb.name])
                            for dc in range(NKC):
                                for kt in range(8):
                                    ctx.op("t", lambda E, kt=kt, dc=dc: E.matmul(
                                        pav[:, dc, :], vb[:, kt, dc * 128:dc * 128 + 128],
                                        pt[:, kt, :], start=(kt == 0), stop=(kt == 7), skip_group_check=True),
                                        reads=[vb.name, pt.name], writes=[pav.name])
                                ctx.op("v", lambda E, dc=dc: E.scalar_tensor_tensor(
                                    a0sb[:, dc, :], pav[:, dc, :], 1.0, rbb[:],
                                    op0=AluOpType.mult, op1=AluOpType.mult),
                                    reads=[pav.name, rbb.name], writes=[a0sb.name])
                            ctx.dma("y", fm(a0h, 1024 * ch + 256 * qq, 256), a0sb[:],
                                    reads=[a0sb.name], writes=["a0h"])
        phase(ph_l0_attn)

        TTW = 512
        NTI = T // TTW

        def ph_inner_attn(l, src, dst, sname, dname):
            with ExitStack() as stk:
                wqkv = sb(stk, [128, NKC, 3 * D], BF16)
                wo = sb(stk, [128, NKC, D], BF16)
                selq = sb(stk, [16, D], F32)
                selk = sb(stk, [16, D], F32)
                ctx.dma("g", wqkv[:], l0f_wqkv[l].rearrange("(c p) e -> p c e", c=NKC),
                        writes=[wqkv.name])
                ctx.dma("g", wo[:], l0f_wo[l].rearrange("(c p) e -> p c e", c=NKC),
                        writes=[wo.name])
                ctx.dma("y", selq[:], l0f_selq[l], writes=[selq.name])
                ctx.dma("y", selk[:], l0f_selk[l], writes=[selk.name])
                vony = sb(stk, [128, 6, HEADS * 65], BF16)
                vony_v = vony[:].rearrange("p c (h u) -> p c h u", u=65)
                ctx.op("v", lambda E: E.memset(
                    vony[:].rearrange("p c (h u) -> p c h u", u=65)[:, :, :, 64:65], 1.0),
                    writes=[vony.name])
                ybf = sb(stk, [128, NKC, 769], BF16)
                yf32 = sb(stk, [128, NKC, TTW], F32)
                tq_a = sb(stk, [128, TTW], BF16)
                tq_d = sb(stk, [128, TTW], BF16)
                tk_a = sb(stk, [128, 768], BF16)
                tk_d = sb(stk, [128, 768], BF16)
                qrot = sb(stk, [128, NKC, TTW], BF16)
                krot = sb(stk, [128, NKC, 768], BF16)
                qraw = sb(stk, [128, NKC, 768], BF16)
                scrb = sb(stk, [128, 768], BF16)
                sqq = sb(stk, [128, 768], BF16)
                nsr = sb(stk, [16, 768], F32)
                tt1 = sb(stk, [128, 768], BF16)
                tt2 = sb(stk, [128, 768], BF16)
                mcols = sb(stk, [128, 64], F32)
                mrows = sb(stk, [64, 128], BF16)
                mflat = sb(stk, [1, 64 * 128], BF16)
                scr = sb(stk, [128, 512], F32)
                st2 = sb(stk, [128, 256], F32)
                ptb = sb(stk, [128, 2, 256], BF16)
                lrow = sb(stk, [1, 32 * 256], BF16)
                lmat2 = sb(stk, [16, TTW], F32)
                rbb = sb(stk, [128, TTW], F32)
                attc = qraw[:].rearrange("p c t -> p c t")[:, :, 0:TTW]
                attc_name = qraw.name
                yo = None
                pq = ps(stk, [128, 512])
                pun = ps(stk, [128, 1024])
                psq16 = pun[0:16, 0:768]
                prb = pun[0:128, 0:512]
                pmt = pun[0:64, 0:128]
                pstA = ps(stk, [128, 512])
                pstB = ps(stk, [128, 512])
                pst2 = (pstA, pstB)
                pavA = ps(stk, [65, 512])
                pavB = ps(stk, [65, 512])
                pswp = ps(stk, [128, 512])
                pav2 = (pavA, pavB)

                def proj_qk(i, which):
                    w = TTW if which == "q" else 768
                    dstb = qrot if which == "q" else krot
                    sel = selq if which == "q" else selk
                    ta = tq_a if which == "q" else tk_a
                    td = tq_d if which == "q" else tk_d
                    off = 1024 if which == "k" else 0
                    u0 = 257 if which == "q" else 1
                    nts = [(0, 512)] if which == "q" else [(0, 512), (512, 256)]
                    for oc in range(NKC):
                        for (n0, nw) in nts:
                            for kc in range(NKC):
                                lo = (u0 if kc < 4 else u0 - 1) + n0
                                ctx.op("t", lambda E, oc=oc, kc=kc, lo=lo, off=off, nw=nw: E.matmul(
                                    pq[:, 0:nw], wqkv[:, kc, off + oc * 128:off + oc * 128 + 128],
                                    ybf[:, kc, lo:lo + nw],
                                    start=(kc == 0), stop=(kc == NKC - 1), skip_group_check=True),
                                    reads=[wqkv.name, ybf.name], writes=[pq.name])
                            ctx.op("s", lambda E, n0=n0, nw=nw: E.activation(
                                sqq[:, n0:n0 + nw], pq[:, 0:nw], AF.Square),
                                reads=[pq.name], writes=[sqq.name])
                            ctx.op("s", lambda E, oc=oc, n0=n0, nw=nw: E.activation(
                                qraw[:, oc, n0:n0 + nw], pq[:, 0:nw], AF.Copy),
                                reads=[pq.name], writes=[qraw.name])
                            ctx.op("t", lambda E, oc=oc, n0=n0, nw=nw: E.matmul(
                                psq16[:, n0:n0 + nw], bdsel[:, oc * 16:oc * 16 + 16],
                                sqq[:, n0:n0 + nw],
                                start=(oc == 0), stop=(oc == NKC - 1), skip_group_check=True),
                                reads=[sqq.name, "bdsel"], writes=[pun.name])
                    ctx.op("s", lambda E, w=w: E.activation(nsr[:, 0:w], psq16[:, 0:w], AF.Sqrt),
                           reads=[pun.name], writes=[nsr.name])
                    ctx.op("v", lambda E, w=w: E.tensor_scalar_max(nsr[:, 0:w], nsr[:, 0:w], 1e-12),
                           reads=[nsr.name], writes=[nsr.name])
                    ctx.op("v", lambda E, w=w: E.reciprocal(nsr[:, 0:w], nsr[:, 0:w]),
                           reads=[nsr.name], writes=[nsr.name])
                    for oc in range(NKC):
                        for (n0, nw) in nts:
                            ctx.op("t", lambda E, oc=oc, n0=n0, nw=nw, sel=sel: E.matmul(
                                prb[:, 0:nw], sel[:, oc * 128:oc * 128 + 128], nsr[:, n0:n0 + nw],
                                start=True, stop=True, skip_group_check=True),
                                reads=[sel.name, nsr.name], writes=[pun.name])
                            ctx.op("v", lambda E, oc=oc, n0=n0, nw=nw: E.tensor_tensor(
                                tt1[:, n0:n0 + nw], qraw[:, oc, n0:n0 + nw], prb[:, 0:nw],
                                op=AluOpType.mult),
                                reads=[qraw.name, pun.name], writes=[tt1.name])
                        ctx.op("v", lambda E, w=w, ta=ta: E.tensor_tensor(
                            tt2[:, 0:w], tt1[:, 0:w], ta[:, 0:w], op=AluOpType.mult),
                            reads=[tt1.name, ta.name], writes=[tt2.name])
                        ctx.op("v", lambda E, w=w, td=td: E.tensor_tensor(
                            scrb[:, 0:w], tt1[:, 0:w], td[:, 0:w], op=AluOpType.mult),
                            reads=[tt1.name, td.name], writes=[scrb.name])
                        for (n0, nw) in nts:
                            ctx.op("t", lambda E, n0=n0, nw=nw: E.matmul(
                                pswp[:, 0:nw], identb[:], tt2[:, n0:n0 + nw],
                                start=True, stop=False, skip_group_check=True),
                                reads=["identb", tt2.name], writes=[pswp.name])
                            ctx.op("t", lambda E, n0=n0, nw=nw: E.matmul(
                                pswp[:, 0:nw], swpb[:], scrb[:, n0:n0 + nw],
                                start=False, stop=True, skip_group_check=True),
                                reads=["swpb", scrb.name], writes=[pswp.name])
                            ctx.op("s", lambda E, oc=oc, n0=n0, nw=nw, dstb=dstb: E.activation(
                                dstb[:, oc, n0:n0 + nw], pswp[:, 0:nw], AF.Copy),
                                reads=[pswp.name], writes=[dstb.name])

                for i in range(NTI):
                    c0 = i * TTW
                    if i == 0:
                        ctx.op("v", lambda E: E.memset(ybf[:, :, 0:257], 0.0),
                               writes=[ybf.name])
                        ctx.dma("g", ybf[:, :, 257:769], fm(src, 0, TTW),
                                reads=[sname], writes=[ybf.name])
                        ctx.op("v", lambda E: E.memset(tk_a[:, 0:256], 0.0), writes=[tk_a.name])
                        ctx.op("v", lambda E: E.memset(tk_d[:, 0:256], 0.0), writes=[tk_d.name])
                        ctx.dma("g", tk_a[:, 256:768], tA["k"][:, 0:TTW], writes=[tk_a.name])
                        ctx.dma("g", tk_d[:, 256:768], tD["k"][:, 0:TTW], writes=[tk_d.name])
                    else:
                        ctx.dma("g", ybf[:], fm(src, c0 - 257, 769),
                                reads=[sname], writes=[ybf.name])
                        ctx.dma("g", tk_a[:], tA["k"][:, c0 - 256:c0 + 512], writes=[tk_a.name])
                        ctx.dma("g", tk_d[:], tD["k"][:, c0 - 256:c0 + 512], writes=[tk_d.name])
                    ctx.dma("y", yf32[:], fm(src, c0, TTW), reads=[sname], writes=[yf32.name])
                    ctx.dma("g", tq_a[:], tA["q"][:, c0:c0 + TTW], writes=[tq_a.name])
                    ctx.dma("g", tq_d[:], tD["q"][:, c0:c0 + TTW], writes=[tq_d.name])
                    proj_qk(i, "q")
                    proj_qk(i, "k")
                    for tch in range(6):
                        for nt in range(2):
                            for kc in range(NKC):
                                lo = (1 if kc < 4 else 0) + 128 * tch
                                ctx.op("t", lambda E, tch=tch, nt=nt, kc=kc, lo=lo: E.matmul(
                                    pq[:], ybf[:, kc, lo:lo + 128],
                                    wqkv[:, kc, 2048 + nt * 512:2048 + nt * 512 + 512],
                                    start=(kc == 0), stop=(kc == NKC - 1), skip_group_check=True),
                                    reads=[wqkv.name, ybf.name], writes=[pq.name])
                            ctx.op("s", lambda E, tch=tch, nt=nt: E.activation(
                                vony_v[:, tch, 8 * nt:8 * nt + 8, 0:64],
                                pq[:].rearrange("p (h u) -> p h u", u=64), AF.Copy),
                                reads=[pq.name], writes=[vony.name])
                    for h in range(HEADS):
                        hp, hc = 64 * (h % 2), h // 2
                        for w in range(2):
                            fw = (i == 0 and w == 0)
                            j2 = (h * 2 + w) * 2
                            for qc in range(2):
                                ctx.op("t", lambda E, h=h, w=w, qc=qc, hp=hp, hc=hc: E.matmul(
                                    pq[:], qrot[hp:hp + 64, hc, 256 * w + 128 * qc:256 * w + 128 * qc + 128],
                                    krot[hp:hp + 64, hc, 256 * w:256 * w + 512],
                                    start=True, stop=True, skip_group_check=True),
                                    reads=[qrot.name, krot.name], writes=[pq.name])
                                mv = 2 * (1 if fw else 0) + qc
                                ctx.op("v", lambda E, mv=mv: E.tensor_tensor(
                                    scr[:, 0:512], pq[:], mk1[:, mv, :], op=AluOpType.add),
                                    reads=[pq.name, "mk1"], writes=[scr.name])
                                ctx.op("v", lambda E, qc=qc, j2=j2: E.reduce_max(
                                    mcols[:, j2 + qc:j2 + qc + 1], scr[:, 0:512], axis=AX.X),
                                    reads=[scr.name], writes=[mcols.name])
                    ctx.op("t", lambda E: E.transpose(pmt, mcols[:], ident[:]),
                           reads=[mcols.name, "ident"], writes=[pun.name])
                    ctx.op("s", lambda E: E.activation(mrows[:], pmt, AF.Identity, scale=-1.0),
                           reads=[pun.name], writes=[mrows.name])
                    ctx.dma("y", mscr.rearrange("(a b) -> a b", b=128), mrows[:],
                            reads=[mrows.name], writes=["mscr"])
                    ctx.dma("y", mflat[:], mscr.rearrange("(a b) -> a b", a=1),
                            reads=["mscr"], writes=[mflat.name])
                    for h in range(HEADS):
                        hp, hc = 64 * (h % 2), h // 2
                        for w in range(2):
                            fw = (i == 0 and w == 0)
                            j2 = (h * 2 + w) * 2
                            pA = pav2[(h * 2 + w) % 2]
                            kts = range(2, 4) if fw else range(4)
                            first_kt = 2 if fw else 0
                            for kt in kts:
                                pS = pst2[kt % 2]
                                ctx.op("t", lambda E, h=h, w=w, kt=kt, hp=hp, hc=hc, pS=pS: E.matmul(
                                    pS[:, 0:256], krot[hp:hp + 64, hc, 256 * w + 128 * kt:256 * w + 128 * kt + 128],
                                    qrot[hp:hp + 64, hc, 256 * w:256 * w + 256],
                                    start=True, stop=False, skip_group_check=True),
                                    reads=[qrot.name, krot.name], writes=[pS.name])
                                for half in range(2):
                                    ctx.op("t", lambda E, j2=j2, half=half, pS=pS: E.matmul(
                                        pS[:, half * 128:half * 128 + 128],
                                        onesb[0:1, 0:128],
                                        mflat[0:1, 128 * (j2 + half):128 * (j2 + half) + 128],
                                        start=False, stop=(half == 1), skip_group_check=True),
                                        reads=["onesb", mflat.name], writes=[pS.name])
                                ctx.op("v", lambda E, kt=kt, pS=pS: E.tensor_tensor(
                                    st2[:], pS[:, 0:256], mk2[:, kt, :], op=AluOpType.add),
                                    reads=[pS.name, "mk2"], writes=[st2.name])
                                ctx.op("s", lambda E, kt=kt: E.activation(
                                    ptb[:, kt % 2, :], st2[:], AF.Exp),
                                    reads=[st2.name], writes=[ptb.name])
                                ctx.op("t", lambda E, h=h, w=w, kt=kt, pA=pA, first_kt=first_kt: E.matmul(
                                    pA[:, 0:256], vony[:, 2 * w + kt, 65 * h:65 * h + 65],
                                    ptb[:, kt % 2, :],
                                    start=(kt == first_kt), stop=(kt == 3), skip_group_check=True),
                                    reads=[vony.name, ptb.name], writes=[pA.name])
                            ctx.op("v", lambda E, h=h, w=w, pA=pA: E.tensor_copy(
                                lrow[0:1, 256 * (h * 2 + w):256 * (h * 2 + w) + 256],
                                pA[64:65, 0:256]),
                                reads=[pA.name], writes=[lrow.name])
                            ctx.op("s", lambda E, h=h, w=w, hp=hp, hc=hc, pA=pA: E.activation(
                                attc[hp:hp + 64, hc, 256 * w:256 * w + 256], pA[0:64, 0:256], AF.Copy),
                                reads=[pA.name], writes=[attc_name])
                    ctx.dma("y", lscr.rearrange("(a b) -> a b", a=1), lrow[:],
                            reads=[lrow.name], writes=["lscr"])
                    ctx.dma("g", lmat2[:],
                            lscr.rearrange("(h x) -> h x", h=16),
                            reads=["lscr"], writes=[lmat2.name])
                    ctx.op("v", lambda E: E.reciprocal(lmat2[:], lmat2[:]),
                           reads=[lmat2.name], writes=[lmat2.name])
                    ctx.dma("y", lscr2.rearrange("(h x) -> h x", h=16), lmat2[:],
                            reads=[lmat2.name], writes=["lscr2"])
                    for c in range(NKC):
                        lv = lscr2.rearrange("(h x) -> h x", h=16)
                        ctx.dma("y", rbb[0:64, :],
                                lv[2 * c:2 * c + 1, :].broadcast_to([64, TTW]),
                                reads=["lscr2"], writes=[rbb.name])
                        ctx.dma("y", rbb[64:128, :],
                                lv[2 * c + 1:2 * c + 2, :].broadcast_to([64, TTW]),
                                reads=["lscr2"], writes=[rbb.name])
                        ctx.op("v", lambda E, c=c: E.tensor_tensor(
                            attc[:, c, :], attc[:, c, :], rbb[:], op=AluOpType.mult),
                            reads=[attc_name, rbb.name], writes=[attc_name])
                    for oc in range(NKC):
                        for ic in range(NKC):
                            ctx.op("t", lambda E, oc=oc, ic=ic: E.matmul(
                                pq[:], wo[:, ic, oc * 128:oc * 128 + 128], attc[:, ic, :],
                                start=(ic == 0), stop=(ic == NKC - 1), skip_group_check=True),
                                reads=[wo.name, attc_name], writes=[pq.name])
                        ctx.op("v", lambda E, oc=oc: E.scalar_tensor_tensor(
                            yf32[:, oc, :], pq[:], 1.0, yf32[:, oc, :],
                            op0=AluOpType.mult, op1=AluOpType.add),
                            reads=[pq.name, yf32.name], writes=[yf32.name])
                    ctx.dma("y", fm(dst, c0, TTW), yf32[:], reads=[yf32.name], writes=[dname])

        def ph_inner_ffn(l, src, dst, sname, dname):
            NJ = cdiv(DI, 128)
            with ExitStack() as stk:
                w1 = sb(stk, [128, NKC, W1OUT], BF16)
                w2 = sb(stk, [128, NJ, D], BF16)
                b1c = sb(stk, [128, 2 * NJ], F32)
                b2c = sb(stk, [128, NKC], F32)
                gc = sb(stk, [128, NKC], F32)
                bc = sb(stk, [128, NKC], F32)
                ctx.dma("g", w1[:], l0f_w1[l].rearrange("(c p) e -> p c e", c=NKC),
                        writes=[w1.name])
                for j in range(NJ):
                    sz = min(128, DI - 128 * j)
                    ctx.dma("g", w2[0:sz, j, :], l0f_w2[l][128 * j:128 * j + sz, :],
                            writes=[w2.name])
                ctx.dma("y", b1c[:], l0f_b1c[l], writes=[b1c.name])
                ctx.dma("y", b2c[:], l0f_b2[l], writes=[b2c.name])
                ctx.dma("y", gc[:], l0f_g[l], writes=[gc.name])
                ctx.dma("y", bc[:], l0f_b[l], writes=[bc.name])
                ybf = sb(stk, [128, NKC, 513], BF16)
                yf32 = sb(stk, [128, NKC, TTW], F32)
                sq = sb(stk, [128, TTW], BF16)
                rowb = sb(stk, [1, 2 * TTW], F32)
                t1 = sb(stk, [128, TTW], F32)
                xnb = sb(stk, [128, NKC, TTW], BF16)
                pbc = ps(stk, [128, 1024])
                gel = sb(stk, [128, TTW], BF16)
                ab = sb(stk, [128, TTW], BF16)
                glu = sb(stk, [128, NJ, TTW], BF16)
                pa = ps(stk, [128, 512])
                pg = ps(stk, [128, 512])
                pum = ps(stk, [1, 512])
                pq2 = ps(stk, [1, 512])
                for i in range(NTI):
                    c0 = i * TTW
                    if i == 0:
                        ctx.op("v", lambda E: E.memset(ybf[:, :, 0:1], 0.0), writes=[ybf.name])
                        ctx.dma("g", ybf[:, :, 1:513], fm(src, 0, TTW),
                                reads=[sname], writes=[ybf.name])
                    else:
                        ctx.dma("g", ybf[:], fm(src, c0 - 1, 513),
                                reads=[sname], writes=[ybf.name])
                    ctx.dma("y", yf32[:], fm(src, c0, TTW), reads=[sname], writes=[yf32.name])
                    for kc in range(NKC):
                        lo = 1 if kc < 4 else 0
                        ctx.op("s", lambda E, kc=kc, lo=lo: E.activation(
                            sq[:], ybf[:, kc, lo:lo + TTW], AF.Square),
                            reads=[ybf.name], writes=[sq.name])
                        ctx.op("t", lambda E, kc=kc, lo=lo: E.matmul(
                            pum[:], onesb[:, 0:1], ybf[:, kc, lo:lo + TTW],
                            start=(kc == 0), stop=(kc == NKC - 1), skip_group_check=True),
                            reads=[ybf.name, "onesb"], writes=[pum.name])
                        ctx.op("t", lambda E, kc=kc: E.matmul(
                            pq2[:], onesb[:, 0:1], sq[:],
                            start=(kc == 0), stop=(kc == NKC - 1), skip_group_check=True),
                            reads=[sq.name, "onesb"], writes=[pq2.name])
                    mu = rowb[0:1, 0:TTW]
                    var = rowb[0:1, TTW:2 * TTW]
                    rn = rowb.name
                    ctx.op("v", lambda E: E.tensor_scalar(mu, pum[:], 1.0 / D, None, op0=AluOpType.mult),
                           reads=[pum.name], writes=[rn])
                    ctx.op("v", lambda E: E.scalar_tensor_tensor(
                        var, mu, 1.0, mu, op0=AluOpType.mult, op1=AluOpType.mult),
                        reads=[rn], writes=[rn])
                    ctx.op("v", lambda E: E.scalar_tensor_tensor(
                        var, pq2[:], 1.0 / D, var, op0=AluOpType.mult, op1=AluOpType.subtract),
                        reads=[pq2.name, rn], writes=[rn])
                    ctx.op("v", lambda E: E.tensor_scalar(var, var, 1e-5, None, op0=AluOpType.add),
                           reads=[rn], writes=[rn])
                    ctx.op("v", lambda E: E.reciprocal(var, var), reads=[rn], writes=[rn])
                    ctx.op("s", lambda E: E.activation(var, var, AF.Sqrt), reads=[rn], writes=[rn])
                    ctx.op("v", lambda E: E.scalar_tensor_tensor(
                        mu, mu, -1.0, var, op0=AluOpType.mult, op1=AluOpType.mult),
                        reads=[rn], writes=[rn])
                    ctx.op("t", lambda E: E.matmul(
                        pbc[:, 0:512], onesfr[0:1, 0:128], var,
                        start=True, stop=True, skip_group_check=True),
                        reads=["onesfr", rn], writes=[pbc.name])
                    ctx.op("t", lambda E: E.matmul(
                        pbc[:, 512:1024], onesfr[0:1, 0:128], mu,
                        start=True, stop=True, skip_group_check=True),
                        reads=["onesfr", rn], writes=[pbc.name])
                    for kc in range(NKC):
                        lo = 1 if kc < 4 else 0
                        ctx.op("v", lambda E, kc=kc, lo=lo: E.tensor_tensor(
                            t1[:], ybf[:, kc, lo:lo + TTW], pbc[:, 0:512], op=AluOpType.mult),
                            reads=[ybf.name, pbc.name], writes=[t1.name])
                        ctx.op("v", lambda E, kc=kc: E.tensor_tensor(
                            t1[:], t1[:], pbc[:, 512:1024], op=AluOpType.add),
                            reads=[t1.name, pbc.name], writes=[t1.name])
                        ctx.op("v", lambda E, kc=kc: E.tensor_scalar(
                            xnb[:, kc, :], t1[:], gc[:, kc:kc + 1], bc[:, kc:kc + 1],
                            op0=AluOpType.mult, op1=AluOpType.add),
                            reads=[t1.name, gc.name, bc.name], writes=[xnb.name])
                    for j in range(NJ):
                        sz = min(128, DI - 128 * j)
                        for kc in range(NKC):
                            ctx.op("t", lambda E, j=j, kc=kc, sz=sz: E.matmul(
                                pa[0:sz, :], w1[:, kc, 128 * j:128 * j + sz], xnb[:, kc, :],
                                start=(kc == 0), stop=(kc == NKC - 1), skip_group_check=True),
                                reads=[w1.name, xnb.name], writes=[pa.name])
                            ctx.op("t", lambda E, j=j, kc=kc, sz=sz: E.matmul(
                                pg[0:sz, :], w1[:, kc, DI + 128 * j:DI + 128 * j + sz], xnb[:, kc, :],
                                start=(kc == 0), stop=(kc == NKC - 1), skip_group_check=True),
                                reads=[w1.name, xnb.name], writes=[pg.name])
                        ctx.op("s", lambda E, j=j, sz=sz: E.activation(
                            gel[0:sz, :], pg[0:sz, :], AF.Gelu, bias=b1c[0:sz, 2 * j + 1:2 * j + 2]),
                            reads=[pg.name, b1c.name], writes=[gel.name])
                        ctx.op("s", lambda E, j=j, sz=sz: E.activation(
                            ab[0:sz, :], pa[0:sz, :], AF.Identity, bias=b1c[0:sz, 2 * j:2 * j + 1]),
                            reads=[pa.name, b1c.name], writes=[ab.name])
                        ctx.op("v", lambda E, j=j, sz=sz: E.tensor_tensor(
                            glu[0:sz, j, :], ab[0:sz, :], gel[0:sz, :], op=AluOpType.mult),
                            reads=[ab.name, gel.name], writes=[glu.name])
                    for oc in range(NKC):
                        for j in range(NJ):
                            sz = min(128, DI - 128 * j)
                            ctx.op("t", lambda E, oc=oc, j=j, sz=sz: E.matmul(
                                pa[:], w2[0:sz, j, oc * 128:oc * 128 + 128], glu[0:sz, j, :],
                                start=(j == 0), stop=(j == NJ - 1), skip_group_check=True),
                                reads=[w2.name, glu.name], writes=[pa.name])
                        ctx.op("v", lambda E, oc=oc: E.scalar_tensor_tensor(
                            yf32[:, oc, :], pa[:], b2c[:, oc:oc + 1], yf32[:, oc, :],
                            op0=AluOpType.add, op1=AluOpType.add),
                            reads=[pa.name, b2c.name, yf32.name], writes=[yf32.name])
                    ctx.dma("y", fm(dst, c0, TTW), yf32[:], reads=[yf32.name], writes=[dname])

        bufs = [(ya, "ya"), (yb, "yb")]
        cur = 0
        for l in range(L0DEPTH):
            src, sname = bufs[cur]
            dst, dname = bufs[1 - cur]
            phase(lambda l=l, src=src, dst=dst, sn=sname, dn=dname: ph_inner_attn(l, src, dst, sn, dn))
            cur = 1 - cur
            src, sname = bufs[cur]
            dst, dname = bufs[1 - cur]
            phase(lambda l=l, src=src, dst=dst, sn=sname, dn=dname: ph_inner_ffn(l, src, dst, sn, dn))
            cur = 1 - cur
        yfin, yfin_name = bufs[cur]

        NTO = TOWN // TTW
        mks = sb(big, [1, 1], F32)

        def ph_x1():
            with ExitStack() as stk:
                rwT = sb(stk, [128, NKC, NEXP], F32)
                ctx.dma("y", rwT[:], l1f_rwT.rearrange("(c p) e -> p c e", c=NKC), writes=[rwT.name])
                xi = sb(stk, [128, NKC, TTW], F32)
                a0t = sb(stk, [128, NKC, TTW], F32)
                yt = sb(stk, [128, NKC, TTW], F32)
                xb = sb(stk, [128, NKC, TTW], BF16)
                sq = sb(stk, [128, TTW], BF16)
                rowb = sb(stk, [1, 2 * TTW], F32)
                t1 = sb(stk, [128, TTW], F32)
                xnf = sb(stk, [128, NKC, TTW], F32)
                pbc = ps(stk, [128, 1024])
                xnb = sb(stk, [128, NKC, TTW], BF16)
                sgt = sb(stk, [128, 4, NEXP], F32)
                sgbt = sb(stk, [128, NEXP], F32)
                m1c = sb(stk, [128, 1], F32)
                mskt = sb(stk, [128, NEXP], F32)
                s2t = sb(stk, [128, NEXP], F32)
                gtt = sb(stk, [128, NEXP], F32)
                gout = sb(stk, [NEXP, TTW], F32)
                ebrep = sb(stk, [128, NEXP], F32)
                pum = ps(stk, [1, 512])
                pq2 = ps(stk, [1, 512])
                prt = ps(stk, [128, 512])
                pgt = ps(stk, [NEXP, 512])
                ctx.dma("y", ebrep[:],
                        l1f_eb.rearrange("e 1 -> 1 e").broadcast_to([128, NEXP]),
                        writes=[ebrep.name])
                for i in range(NTO):
                    c0 = i * TTW
                    if True:
                        ctx.dma("y", xi[:], fm(xin, HALO + c0, TTW), writes=[xi.name])
                        ctx.dma("y", a0t[:], fm(a0h, c0, TTW), reads=["a0h"], writes=[a0t.name])
                        ctx.dma("y", yt[:], fm(yfin, HALO + c0, TTW), reads=[yfin_name], writes=[yt.name])
                        for kc in range(NKC):
                            ctx.op("v", lambda E, kc=kc: E.tensor_tensor(
                                xi[:, kc, :], xi[:, kc, :], a0t[:, kc, :], op=AluOpType.add),
                                reads=[xi.name, a0t.name], writes=[xi.name])
                            ctx.op("v", lambda E, kc=kc: E.tensor_tensor(
                                xi[:, kc, :], xi[:, kc, :], yt[:, kc, :], op=AluOpType.add),
                                reads=[xi.name, yt.name], writes=[xi.name])
                            ctx.op("s", lambda E, kc=kc: E.activation(
                                xb[:, kc, :], xi[:, kc, :], AF.Copy),
                                reads=[xi.name], writes=[xb.name])
                        ctx.dma("y", fm(x1h, c0, TTW), xi[:], reads=[xi.name], writes=["x1h"])
                        for kc in range(NKC):
                            ctx.op("s", lambda E, kc=kc: E.activation(sq[:], xb[:, kc, :], AF.Square),
                                   reads=[xb.name], writes=[sq.name])
                            ctx.op("t", lambda E, kc=kc: E.matmul(
                                pum[:], onesb[:, 0:1], xb[:, kc, :],
                                start=(kc == 0), stop=(kc == NKC - 1), skip_group_check=True),
                                reads=[xb.name, "onesb"], writes=[pum.name])
                            ctx.op("t", lambda E, kc=kc: E.matmul(
                                pq2[:], onesb[:, 0:1], sq[:],
                                start=(kc == 0), stop=(kc == NKC - 1), skip_group_check=True),
                                reads=[sq.name, "onesb"], writes=[pq2.name])
                        mu = rowb[0:1, 0:TTW]
                        var = rowb[0:1, TTW:2 * TTW]
                        rn = rowb.name
                        ctx.op("v", lambda E: E.tensor_scalar(mu, pum[:], 1.0 / D, None, op0=AluOpType.mult),
                               reads=[pum.name], writes=[rn])
                        ctx.op("v", lambda E: E.scalar_tensor_tensor(
                            var, mu, 1.0, mu, op0=AluOpType.mult, op1=AluOpType.mult),
                            reads=[rn], writes=[rn])
                        ctx.op("v", lambda E: E.scalar_tensor_tensor(
                            var, pq2[:], 1.0 / D, var, op0=AluOpType.mult, op1=AluOpType.subtract),
                            reads=[pq2.name, rn], writes=[rn])
                        ctx.op("v", lambda E: E.tensor_scalar(var, var, 1e-5, None, op0=AluOpType.add),
                               reads=[rn], writes=[rn])
                        ctx.op("v", lambda E: E.reciprocal(var, var), reads=[rn], writes=[rn])
                        ctx.op("s", lambda E: E.activation(var, var, AF.Sqrt), reads=[rn], writes=[rn])
                        ctx.op("v", lambda E: E.scalar_tensor_tensor(
                            mu, mu, -1.0, var, op0=AluOpType.mult, op1=AluOpType.mult),
                            reads=[rn], writes=[rn])
                        ctx.op("t", lambda E: E.matmul(
                            pbc[:, 0:512], onesfr[0:1, 0:128], var,
                            start=True, stop=True, skip_group_check=True),
                            reads=["onesfr", rn], writes=[pbc.name])
                        ctx.op("t", lambda E: E.matmul(
                            pbc[:, 512:1024], onesfr[0:1, 0:128], mu,
                            start=True, stop=True, skip_group_check=True),
                            reads=["onesfr", rn], writes=[pbc.name])
                        for kc in range(NKC):
                            ctx.op("v", lambda E, kc=kc: E.tensor_tensor(
                                t1[:], xi[:, kc, :], pbc[:, 0:512], op=AluOpType.mult),
                                reads=[xi.name, pbc.name], writes=[t1.name])
                            ctx.op("v", lambda E, kc=kc: E.tensor_tensor(
                                xnf[:, kc, :], t1[:], pbc[:, 512:1024], op=AluOpType.add),
                                reads=[t1.name, pbc.name], writes=[xnf.name])
                            ctx.op("s", lambda E, kc=kc: E.activation(
                                xnb[:, kc, :], xnf[:, kc, :], AF.Copy),
                                reads=[xnf.name], writes=[xnb.name])
                        ctx.dma("y", fm(xn1h, c0, TTW), xnb[:], reads=[xnb.name], writes=["xn1h"])
                        for tc in range(4):
                            for kc in range(NKC):
                                ctx.op("t", lambda E, tc=tc, kc=kc: E.matmul(
                                    prt[:, 0:NEXP], xnf[:, kc, tc * 128:tc * 128 + 128],
                                    rwT[:, kc, :],
                                    start=(kc == 0), stop=(kc == NKC - 1), skip_group_check=True),
                                    reads=[rwT.name, xnf.name], writes=[prt.name])
                            ctx.op("s", lambda E, tc=tc: E.activation(
                                sgt[:, tc, :], prt[:, 0:NEXP], AF.Sigmoid),
                                reads=[prt.name], writes=[sgt.name])
                        for tc in range(4):
                            ctx.op("v", lambda E, tc=tc: E.tensor_tensor(
                                sgbt[:], sgt[:, tc, :], ebrep[:], op=AluOpType.add),
                                reads=[sgt.name, ebrep.name], writes=[sgbt.name])
                            ctx.op("v", lambda E: E.reduce_max(m1c[:], sgbt[:], axis=AX.X),
                                   reads=[sgbt.name], writes=[m1c.name])
                            ctx.op("v", lambda E: E.tensor_scalar(
                                mskt[:], sgbt[:], m1c[:], None, op0=AluOpType.is_ge),
                                reads=[sgbt.name, m1c.name], writes=[mskt.name])
                            ctx.op("v", lambda E: E.scalar_tensor_tensor(
                                s2t[:], mskt[:], -1e30, sgbt[:],
                                op0=AluOpType.mult, op1=AluOpType.add),
                                reads=[mskt.name, sgbt.name], writes=[s2t.name])
                            ctx.op("v", lambda E: E.reduce_max(m1c[:], s2t[:], axis=AX.X),
                                   reads=[s2t.name], writes=[m1c.name])
                            ctx.op("v", lambda E: E.tensor_scalar(
                                mskt[:], sgbt[:], m1c[:], None, op0=AluOpType.is_ge),
                                reads=[sgbt.name, m1c.name], writes=[mskt.name])
                            ctx.op("v", lambda E, tc=tc: E.tensor_tensor(
                                gtt[:], sgt[:, tc, :], mskt[:], op=AluOpType.mult),
                                reads=[sgt.name, mskt.name], writes=[gtt.name])
                            ctx.op("t", lambda E, tc=tc: E.transpose(
                                pgt[0:NEXP, 0:128], gtt[:], ident[:]),
                                reads=[gtt.name, "ident"], writes=[pgt.name])
                            ctx.op("s", lambda E, tc=tc: E.activation(
                                gout[:, tc * 128:tc * 128 + 128], pgt[0:NEXP, 0:128], AF.Copy),
                                reads=[pgt.name], writes=[gout.name])
                        ctx.dma("y", gatesd[:, c0:c0 + TTW], gout[:],
                                reads=[gout.name], writes=["gatesd"])
        phase(ph_x1)


        def ph_l1kv():
            with ExitStack() as stk:
                w1s = sb(stk, [128, NKC, D], BF16)
                w2s = sb(stk, [128, NKC, D], BF16)
                w3s = sb(stk, [128, NKC, D], BF16)
                b1c = sb(stk, [128, NKC], F32)
                b2c = sb(stk, [128, NKC], F32)
                b3r = sb(stk, [1, D], BF16)
                mkrun = sb(stk, [1, 1], F32)
                t11 = sb(stk, [1, 1], F32)
                nkrow = sb(stk, [1, TTW], F32)
                ctx.dma("g", w1s[:], l1a_w[1].rearrange("(c p) e -> p c e", c=NKC), writes=[w1s.name])
                ctx.dma("g", w2s[:], l1a_w[2].rearrange("(c p) e -> p c e", c=NKC), writes=[w2s.name])
                ctx.dma("g", w3s[:], l1a_w[3].rearrange("(c p) e -> p c e", c=NKC), writes=[w3s.name])
                ctx.dma("y", b1c[:], l1a_b[1], writes=[b1c.name])
                ctx.dma("y", b2c[:], l1a_b[2], writes=[b2c.name])
                ctx.dma("g", b3r[:], l1a_b3r, writes=[b3r.name])
                ctx.op("v", lambda E: E.memset(mkrun[:], NEG), writes=[mkrun.name])
                xnt = sb(stk, [128, NKC, TTW], BF16)
                kfb = sb(stk, [128, TTW], BF16)
                sqk = sb(stk, [128, TTW], BF16)
                vtb = sb(stk, [128, 4, D], BF16)
                pk = ps(stk, [128, 512])
                pnk = ps(stk, [1, 512])
                for i in range(NTO):
                    c0 = i * TTW
                    if True:
                        ctx.dma("y", xnt[:], fm(xn1h, c0, TTW), reads=["xn1h"], writes=[xnt.name])
                        for oc in range(NKC):
                            for kc in range(NKC):
                                ctx.op("t", lambda E, oc=oc, kc=kc: E.matmul(
                                    pk[:], w1s[:, kc, oc * 128:oc * 128 + 128], xnt[:, kc, :],
                                    start=(kc == 0), stop=(kc == NKC - 1), skip_group_check=True),
                                    reads=[w1s.name, xnt.name], writes=[pk.name])
                            ctx.op("s", lambda E, oc=oc: E.activation(
                                kfb[:], pk[:], AF.Identity, bias=b1c[:, oc:oc + 1]),
                                reads=[pk.name, b1c.name], writes=[kfb.name])
                            ctx.dma("y", qd[oc * 128:oc * 128 + 128, c0:c0 + TTW], kfb[:],
                                    reads=[kfb.name], writes=["qd"])
                        for oc in range(NKC):
                            for kc in range(NKC):
                                ctx.op("t", lambda E, oc=oc, kc=kc: E.matmul(
                                    pk[:], w2s[:, kc, oc * 128:oc * 128 + 128], xnt[:, kc, :],
                                    start=(kc == 0), stop=(kc == NKC - 1), skip_group_check=True),
                                    reads=[w2s.name, xnt.name], writes=[pk.name])
                            ctx.op("s", lambda E, oc=oc: E.activation(
                                kfb[:], pk[:], AF.Identity, bias=b2c[:, oc:oc + 1]),
                                reads=[pk.name, b2c.name], writes=[kfb.name])
                            ctx.dma("y", agk_in[oc * 128:oc * 128 + 128, c0:c0 + TTW], kfb[:],
                                    reads=[kfb.name], writes=["agk_in"])
                            ctx.op("s", lambda E, oc=oc: E.activation(sqk[:], kfb[:], AF.Square),
                                   reads=[kfb.name], writes=[sqk.name])
                            ctx.op("t", lambda E, oc=oc: E.matmul(
                                pnk[:], onesb[:, 0:1], sqk[:],
                                start=(oc == 0), stop=(oc == NKC - 1), skip_group_check=True),
                                reads=[sqk.name, "onesb"], writes=[pnk.name])
                        ctx.op("v", lambda E: E.tensor_copy(nkrow[:], pnk[:]),
                               reads=[pnk.name], writes=[nkrow.name])
                        ctx.op("v", lambda E: E.reduce_max(t11[:], nkrow[:], axis=AX.X),
                               reads=[nkrow.name], writes=[t11.name])
                        ctx.op("v", lambda E: E.tensor_tensor(
                            mkrun[:], mkrun[:], t11[:], op=AluOpType.max),
                            reads=[mkrun.name, t11.name], writes=[mkrun.name])
                        for tch in range(4):
                            for nt in range(2):
                                for kc in range(NKC):
                                    ctx.op("t", lambda E, tch=tch, nt=nt, kc=kc: E.matmul(
                                        pk[:], xnt[:, kc, tch * 128:tch * 128 + 128],
                                        w3s[:, kc, nt * 512:nt * 512 + 512],
                                        start=(kc == 0), stop=False, skip_group_check=True),
                                        reads=[w3s.name, xnt.name], writes=[pk.name])
                                ctx.op("t", lambda E, nt=nt: E.matmul(
                                    pk[:], onesb[0:1, 0:128], b3r[0:1, nt * 512:nt * 512 + 512],
                                    start=False, stop=True, skip_group_check=True),
                                    reads=["onesb", b3r.name], writes=[pk.name])
                                ctx.op("s", lambda E, tch=tch, nt=nt: E.activation(
                                    vtb[:, tch, nt * 512:nt * 512 + 512], pk[:], AF.Copy),
                                    reads=[pk.name], writes=[vtb.name])
                        for tch in range(4):
                            ctx.dma("y", agv_in[c0 + tch * 128:c0 + tch * 128 + 128, :],
                                    vtb[:, tch, :], reads=[vtb.name], writes=["agv_in"])
                ctx.op("s", lambda E: E.activation(mkrun[:], mkrun[:], AF.Sqrt),
                       reads=[mkrun.name], writes=[mkrun.name])
                ctx.dma("y", mk_in, mkrun[:], reads=[mkrun.name], writes=["mk_in"])
        phase(ph_l1kv)

        def ph_coll():
            ctx.op("g", lambda E: E.collective_compute(
                "AllGather", mybir.AluOpType.bypass, replica_groups=rg,
                ins=[agk_in], outs=[agk_out]),
                reads=["agk_in"], writes=["agk_out"], coll=True)
            ctx.op("g", lambda E: E.collective_compute(
                "AllGather", mybir.AluOpType.bypass, replica_groups=rg,
                ins=[agv_in], outs=[agv_out]),
                reads=["agv_in"], writes=["agv_out"], coll=True)
            ctx.op("g", lambda E: E.collective_compute(
                "AllReduce", mybir.AluOpType.max, replica_groups=rg,
                ins=[mk_in], outs=[mk_out]),
                reads=["mk_in"], writes=["mk_out"], coll=True)
        phase(ph_coll)

        def ph_ub():
            with ExitStack() as stk:
                sqq = sb(stk, [128, TTW], BF16)
                qt = sb(stk, [128, NKC, TTW], BF16)
                nq = sb(stk, [1, TTW], F32)
                augsb = sb(stk, [1, TTW], BF16)
                c960 = sb(stk, [1, TTW], F32)
                pnq = ps(stk, [1, 512])
                ctx.dma("y", mks[:], mk_out, reads=["mk_out"], writes=["mks"])
                ctx.op("v", lambda E: E.memset(c960[:], 32.0 * UB_SHIFT), writes=[c960.name])
                for ct in range(NTO):
                    c0 = ct * TTW
                    ctx.dma("y", qt[:], fm(qd, c0, TTW), reads=["qd"], writes=[qt.name])
                    for oc in range(NKC):
                        ctx.op("s", lambda E, oc=oc: E.activation(
                            sqq[:], qt[:, oc, :], AF.Square),
                            reads=[qt.name], writes=[sqq.name])
                        ctx.op("t", lambda E, oc=oc: E.matmul(
                            pnq[:], onesb[:, 0:1], sqq[:],
                            start=(oc == 0), stop=(oc == NKC - 1), skip_group_check=True),
                            reads=[sqq.name, "onesb"], writes=[pnq.name])
                    ctx.op("s", lambda E: E.activation(nq[:], pnq[:], AF.Sqrt),
                           reads=[pnq.name], writes=[nq.name])
                    ctx.op("v", lambda E: E.scalar_tensor_tensor(
                        nq[:], nq[:], mks[0:1, 0:1], c960[:],
                        op0=AluOpType.mult, op1=AluOpType.subtract),
                        reads=[nq.name, "mks", c960.name], writes=[nq.name])
                    ctx.op("s", lambda E, c0=c0: E.activation(
                        augsb[:], nq[:], AF.Identity, scale=-1.0),
                        reads=[nq.name], writes=[augsb.name])
                    ctx.dma("y", augd[0:1, c0:c0 + TTW], augsb[:],
                            reads=[augsb.name], writes=["augd"])
        phase(ph_ub)

        def ph_l1attn():
            QH = TOWN // 2
            NQB = QH // 256
            NKG = S // 1024
            with ExitStack() as stk:
                w4s = sb(stk, [128, NKC, D], BF16)
                b4c = sb(stk, [128, NKC], F32)
                ctx.dma("g", w4s[:], l1a_w[4].rearrange("(c p) e -> p c e", c=NKC), writes=[w4s.name])
                ctx.dma("y", b4c[:], l1a_b[4], writes=[b4c.name])
                acc = sb(stk, [128, NKC, QH], F32)
                lacc = sb(stk, [1, QH], F32)
                lrb = sb(stk, [128, QH], F32)
                att1 = sb(stk, [128, NKC, TTW], BF16)
                qbuf = sb(stk, [128, NKC, 256], BF16)
                ptq = sb(stk, [128, 8, 256], BF16)
                kgt = sb(stk, [128, NKC, 1024], BF16)
                vgt = sb(stk, [128, NKC, 1024], BF16)
                x1t = sb(stk, [128, NKC, TTW], F32)
                augh = sb(stk, [1, QH], BF16)
                pstA = ps(stk, [128, 512])
                pstB = ps(stk, [128, 512])
                pav = ps(stk, [128, NKC, 256])
                plq = ps(stk, [1, 512])
                x2sb = x1t
                pst2 = (pstA, pstB)
                for qh in range(2):
                    q0 = qh * QH
                    if True:
                        ctx.op("v", lambda E: E.memset(acc[:], 0.0), writes=[acc.name])
                        ctx.dma("y", augh[:], augd[0:1, q0:q0 + QH],
                                reads=["augd"], writes=[augh.name])
                        ctx.op("v", lambda E: E.memset(lacc[:], 0.0), writes=[lacc.name])
                        for kg in range(NKG):
                            rblk = (1024 * kg) // TOWN
                            coff = 1024 * kg - TOWN * rblk
                            ctx.dma("g", kgt[:],
                                    agk_out[D * rblk:D * (rblk + 1), coff:coff + 1024]
                                    .rearrange("(c p) t -> p c t", c=NKC),
                                    reads=["agk_out"], writes=[kgt.name])
                            ctx.dma("g", vgt[:],
                                    agv_out[1024 * kg:1024 * kg + 1024, :]
                                    .rearrange("(c p) e -> p c e", c=NKC),
                                    reads=["agv_out"], writes=[vgt.name])
                            for qb in range(NQB):
                                qc0 = q0 + qb * 256
                                ctx.dma("y", qbuf[:], fm(qd, qc0, 256),
                                        reads=["qd"], writes=[qbuf.name])
                                for kt in range(8):
                                    pS = pst2[kt % 2]
                                    for ec in range(NKC):
                                        ctx.op("t", lambda E, kt=kt, ec=ec, pS=pS: E.matmul(
                                            pS[:, 0:256], kgt[:, ec, kt * 128:kt * 128 + 128],
                                            qbuf[:, ec, :],
                                            start=(ec == 0), stop=False, skip_group_check=True),
                                            reads=[kgt.name, qbuf.name], writes=[pS.name])
                                    ctx.op("t", lambda E, qc0=qc0, q0=q0, pS=pS: E.matmul(
                                        pS[:, 0:256], onesb[0:1, 0:128],
                                        augh[0:1, qc0 - q0:qc0 - q0 + 256],
                                        start=False, stop=True, skip_group_check=True),
                                        reads=["onesb", augh.name], writes=[pS.name])
                                    ctx.op("s", lambda E, kt=kt, pS=pS: E.activation(
                                        ptq[:, kt, :], pS[:, 0:256], AF.Exp, scale=L1_SCALE),
                                        reads=[pS.name], writes=[ptq.name])
                                    ctx.op("t", lambda E, kt=kt: E.matmul(
                                        plq[0:1, 0:256], onesb[:, 0:1], ptq[:, kt, :],
                                        start=(kt == 0), stop=(kt == 7), skip_group_check=True),
                                        reads=["onesb", ptq.name], writes=[plq.name])
                                for dc in range(NKC):
                                    for kt in range(8):
                                        ctx.op("t", lambda E, kt=kt, dc=dc: E.matmul(
                                            pav[:, dc, :], vgt[:, kt, dc * 128:dc * 128 + 128],
                                            ptq[:, kt, :], start=(kt == 0), stop=(kt == 7), skip_group_check=True),
                                            reads=[vgt.name, ptq.name], writes=[pav.name])
                                    ctx.op("v", lambda E, dc=dc, qb=qb: E.scalar_tensor_tensor(
                                        acc[:, dc, qb * 256:qb * 256 + 256], pav[:, dc, :], 1.0,
                                        acc[:, dc, qb * 256:qb * 256 + 256],
                                        op0=AluOpType.mult, op1=AluOpType.add),
                                        reads=[pav.name, acc.name], writes=[acc.name])
                                qq = qb * 256
                                ctx.op("v", lambda E, qq=qq: E.scalar_tensor_tensor(
                                    lacc[0:1, qq:qq + 256], plq[0:1, 0:256], 1.0, lacc[0:1, qq:qq + 256],
                                    op0=AluOpType.mult, op1=AluOpType.add),
                                    reads=[plq.name, lacc.name], writes=[lacc.name])
                        ctx.op("v", lambda E: E.reciprocal(lacc[:], lacc[:]),
                               reads=[lacc.name], writes=[lacc.name])
                        ctx.dma("y", lscr2[0:QH].rearrange("(a b) -> a b", a=1), lacc[:],
                                reads=[lacc.name], writes=["lscr2"])
                        ctx.dma("y", lrb[:],
                                lscr2[0:QH].rearrange("(a b) -> a b", a=1)
                                .broadcast_to([128, QH]),
                                reads=["lscr2"], writes=[lrb.name])
                        for ct in range(QH // TTW):
                            c0 = q0 + ct * TTW
                            for dc in range(NKC):
                                ctx.op("v", lambda E, dc=dc, ct=ct: E.tensor_tensor(
                                    att1[:, dc, :], acc[:, dc, ct * TTW:ct * TTW + TTW],
                                    lrb[:, ct * TTW:ct * TTW + TTW], op=AluOpType.mult),
                                    reads=[acc.name, lrb.name], writes=[att1.name])
                            ctx.dma("y", x1t[:], fm(x1h, c0, TTW), reads=["x1h"], writes=[x1t.name])
                            for oc in range(NKC):
                                for ic in range(NKC):
                                    ctx.op("t", lambda E, oc=oc, ic=ic: E.matmul(
                                        pst2[0][:, 0:512],
                                        w4s[:, ic, oc * 128:oc * 128 + 128],
                                        att1[:, ic, :],
                                        start=(ic == 0), stop=(ic == NKC - 1), skip_group_check=True),
                                        reads=[w4s.name, att1.name], writes=[pstA.name])
                                ctx.op("v", lambda E, oc=oc: E.scalar_tensor_tensor(
                                    x2sb[:, oc, :], pst2[0][:, 0:512], b4c[:, oc:oc + 1],
                                    x1t[:, oc, :], op0=AluOpType.add, op1=AluOpType.add),
                                    reads=[pstA.name, b4c.name, x1t.name], writes=[x1t.name])
                            for oc in range(NKC):
                                ctx.dma("y", x2ph[oc * 128:oc * 128 + 128, c0:c0 + TTW],
                                        x2sb[:, oc, :], reads=[x1t.name], writes=[f"x2ph{oc}"])
        phase(ph_l1attn)

        def ph_moe():
            NTG = 2
            TG = TOWN // NTG
            NTT = TG // TTW
            parts = [("sq", q) for q in range(SHID // RH)] + [("e", e) for e in range(NEXP)]
            with ExitStack() as stk:
                pht = ps(stk, [128, NTT, 512])
                wsA = sb(stk, [128, NKC, 2 * RH], BF16)
                w3p = sb(stk, [128, NKC, D], BF16)
                xng = sb(stk, [128, NKC, TG], BF16)
                h1s = sb(stk, [128, NTT, 512], BF16)
                glu8 = sb(stk, [128, NKC, NTT, 512], BF16)
                osb = sb(stk, [128, 512], F32)
                gbuf = sb(stk, [128, NTT, 512], F32)
                for (pk_, pidx) in parts:
                    if True:
                        if pk_ == "sq":
                            wa = wsA[:, :, 0:RH]
                            wb = wsA[:, :, RH:2 * RH]
                            ctx.dma("g", wa, l1f_sw1T[:, RH * pidx:RH * (pidx + 1)]
                                    .rearrange("(c p) e -> p c e", c=NKC), writes=[wsA.name])
                            ctx.dma("g", wb, l1f_sw2T[:, RH * pidx:RH * (pidx + 1)]
                                    .rearrange("(c p) e -> p c e", c=NKC), writes=[wsA.name])
                            ctx.dma("g", w3p[:], l1f_sw3T[RH * pidx:RH * (pidx + 1), :]
                                    .rearrange("(c p) e -> p c e", c=NKC), writes=[w3p.name])
                        else:
                            wa = wsA[:]
                            wb = None
                            ctx.dma("g", wsA[:], l1f_w12T[pidx]
                                    .rearrange("(c p) e -> p c e", c=NKC), writes=[wsA.name])
                            ctx.dma("g", w3p[:], l1f_w3T[pidx]
                                    .rearrange("(c p) e -> p c e", c=NKC), writes=[w3p.name])
                        for tg in range(NTG):
                            if True:
                                ctx.dma("y", xng[:], fm(xn1h, tg * TG, TG),
                                        reads=["xn1h"], writes=[xng.name])
                                if pk_ == "e":
                                    for t in range(NTT):
                                        ctx.dma("y", gbuf[:, t, :],
                                                gatesd[pidx:pidx + 1,
                                                       tg * TG + t * 512:tg * TG + t * 512 + 512]
                                                .broadcast_to([128, 512]),
                                                reads=["gatesd"], writes=[gbuf.name])
                                for oc in range(NKC):
                                    for kc in range(NKC):
                                        for t in range(NTT):
                                            ctx.op("t", lambda E, oc=oc, kc=kc, t=t, wa=wa: E.matmul(
                                                pht[:, t, :], wa[:, kc, oc * 128:oc * 128 + 128],
                                                xng[:, kc, t * 512:t * 512 + 512],
                                                start=(kc == 0), stop=(kc == NKC - 1),
                                                skip_group_check=True),
                                                reads=[wsA.name, xng.name], writes=[pht.name])
                                    for t in range(NTT):
                                        ctx.op("s", lambda E, t=t: E.activation(
                                            h1s[:, t, :], pht[:, t, :], AF.Silu),
                                            reads=[pht.name], writes=[h1s.name])
                                    woff = RH if pk_ == "e" else 0
                                    wsec = wa if pk_ == "e" else wb
                                    for kc in range(NKC):
                                        for t in range(NTT):
                                            ctx.op("t", lambda E, oc=oc, kc=kc, t=t, wsec=wsec, woff=woff: E.matmul(
                                                pht[:, t, :], wsec[:, kc, woff + oc * 128:woff + oc * 128 + 128],
                                                xng[:, kc, t * 512:t * 512 + 512],
                                                start=(kc == 0), stop=(kc == NKC - 1),
                                                skip_group_check=True),
                                                reads=[wsA.name, xng.name], writes=[pht.name])
                                    for t in range(NTT):
                                        ctx.op("v", lambda E, oc=oc, t=t: E.scalar_tensor_tensor(
                                            glu8[:, oc, t, :], pht[:, t, :], 1.0, h1s[:, t, :],
                                            op0=AluOpType.mult, op1=AluOpType.mult),
                                            reads=[pht.name, h1s.name], writes=[glu8.name])
                                        if pk_ == "e":
                                            ctx.op("v", lambda E, oc=oc, t=t: E.tensor_tensor(
                                                glu8[:, oc, t, :], glu8[:, oc, t, :], gbuf[:, t, :],
                                                op=AluOpType.mult),
                                                reads=[glu8.name, gbuf.name], writes=[glu8.name])
                                for oc2 in range(NKC):
                                    for hcc in range(NKC):
                                        for t in range(NTT):
                                            ctx.op("t", lambda E, oc2=oc2, hcc=hcc, t=t: E.matmul(
                                                pht[:, t, :], w3p[:, hcc, oc2 * 128:oc2 * 128 + 128],
                                                glu8[:, hcc, t, :],
                                                start=(hcc == 0), stop=(hcc == NKC - 1),
                                                skip_group_check=True),
                                                reads=[w3p.name, glu8.name], writes=[pht.name])
                                    for t in range(NTT):
                                        ctx.op("v", lambda E, t=t: E.tensor_copy(
                                            osb[:], pht[:, t, :]),
                                            reads=[pht.name], writes=[osb.name])
                                        c0 = tg * TG + t * 512
                                        ctx.dma("g", x2ph[oc2 * 128:oc2 * 128 + 128, c0:c0 + TTW],
                                                osb[:], reads=[osb.name], writes=[f"x2ph{oc2}"],
                                                accum_op=AluOpType.add)
        phase(ph_moe)

        def ph_out():
            with ExitStack() as stk:
                ot = sb(stk, [128, NKC, TTW], F32)
                for i in range(NTO):
                    c0 = i * TTW
                    ctx.dma("y", ot[:], fm(x2ph, c0, TTW),
                            reads=[f"x2ph{oc}" for oc in range(NKC)], writes=[ot.name])
                    ctx.dma("y", fm(out, c0, TTW), ot[:], reads=[ot.name], writes=["out"])
            t = ctx.cnt["y"]
            ctx.pend["y"].append([None, [("y", t)], False, 1, False, "y"])
            ctx.marks["y"].append((len(ctx.pend["y"]) - 1, (), ()))
            ctx.hwm[("y", "y")] = t
        phase(ph_out)

    return nc


def kernel(**inputs):
    nc = build_program()
    in_maps = [host_inputs(c, inputs) for c in range(NCORES)]
    res = run_bass_kernel_spmd(nc, in_maps, core_ids=list(range(NCORES)))
    outp = np.zeros((B, S, D), np.float32)
    for c in range(NCORES):
        b, r = c // 4, c % 4
        outp[b, r * TOWN:(r + 1) * TOWN, :] = np.asarray(res.results[c]["out"]).T
    return outp


# version 07 (final)
# speedup vs baseline: 1.0136x; 1.0136x over previous
import numpy as np
import math
from contextlib import ExitStack

import concourse.bass as bass
import concourse.mybir as mybir
from concourse.bass_utils import run_bass_kernel_spmd
from concourse.alu_op_type import AluOpType
from concourse import library_config

F32 = mybir.dt.float32
BF16 = mybir.dt.bfloat16
AF = mybir.ActivationFunctionType
AX = mybir.AxisListType

NCORES = 8
B = 2
S = 16384
D = 1024
NKC = D // 128

L0DEPTH = 6
HEADS = 16
DH = 64
WIN = 256
DI = 2730
W1OUT = 2 * DI
QK_SCALE = 8.0

NEXP = 8
RH = 1024
SHID = 4096
L1_SCALE = 1.0 / 32.0
UB_SHIFT = 30.0

NEG = -3.0e38

TOWN = 4096
HALO = 2048
T = TOWN + HALO
ENG_ATTR = {"t": "tensor", "v": "vector", "s": "scalar", "g": "gpsimd", "y": "sync"}


def set_config(town=None, halo=None, seq=None, l0depth=None):
    global TOWN, HALO, T, S, L0DEPTH
    if town is not None:
        TOWN = town
    if halo is not None:
        HALO = halo
    if seq is not None:
        S = seq
    if l0depth is not None:
        L0DEPTH = l0depth
    T = TOWN + HALO


class Ctx:
    ENGS = ("t", "v", "s", "g", "y")

    def __init__(self, nc, stack):
        self.nc = nc
        self.cnt = {e: 0 for e in self.ENGS}
        self.pend = {e: [] for e in self.ENGS}
        self.marks = {e: [] for e in self.ENGS}
        self.hwm = {}
        self.writer = {}
        self.readers = {}
        self.undrained = {e: set() for e in self.ENGS}
        self.unread = {e: set() for e in self.ENGS}
        self.sem = {e: stack.enter_context(nc.semaphore(f"sem_{e}"))
                    for e in self.ENGS + ("gd",)}
        self.cnt["gd"] = 0
        self.need = {"y": set(), "gd": set()}
        self.flushed = {"y": 0, "gd": 0}

    def _ticket(self, eng):
        if eng == "gd":
            return self.cnt["gd"]
        if self.pend[eng] and self.marks[eng]:
            last = None
            for rec in reversed(self.pend[eng]):
                if rec[0] is not None:
                    last = rec
                    break
            if last is not None and not last[4]:
                last[4] = True
                self.cnt[eng] += last[3]
            tick = self.cnt[eng]
            for _, ws, rs in self.marks[eng]:
                for b in ws:
                    w = self.writer.get(b)
                    if w is not None and w[0] == eng and w[1] is None:
                        self.writer[b] = (eng, tick, w[2])
                for b in rs:
                    r = self.readers.get(b, {}).get(eng)
                    if r is not None and r[0] is None:
                        self.readers[b][eng] = (tick, r[1])
            self.marks[eng] = []
        return self.cnt[eng]

    def op(self, eng, fn, reads=(), writes=(), dma=False, coll=False):
        waits = []
        drain = False

        def tick_of(psk, t):
            v = t if t is not None else self._ticket(psk)
            if psk in ("y", "gd") and v > self.flushed[psk]:
                self.need.setdefault(psk, set()).add(v)
            return v

        own = lambda sk: sk[0] == eng
        for b in reads:
            w = self.writer.get(b)
            if w is None:
                continue
            wsk, wtick, wdma = w
            if own(wsk) and not wdma:
                if b in self.undrained[eng]:
                    drain = True
            else:
                waits.append((wsk, tick_of(wsk, wtick)))
        for b in writes:
            for rsk, (rtick, rdma) in list(self.readers.get(b, {}).items()):
                if not own(rsk) or rdma:
                    waits.append((rsk, tick_of(rsk, rtick)))
                elif b in self.unread[eng]:
                    drain = True
            w = self.writer.get(b)
            if w is not None:
                wsk, wtick, wdma = w
                if not own(wsk) or wdma:
                    waits.append((wsk, tick_of(wsk, wtick)))
        fwaits = []
        for peng, tick in waits:
            if peng in ("y", "gd") and tick <= self.flushed[peng]:
                continue
            key = (eng, peng)
            if self.hwm.get(key, 0) < tick:
                self.hwm[key] = tick
                fwaits.append((peng, tick))
        if drain:
            self.undrained[eng].clear()
            self.unread[eng].clear()
        quant = 16 if dma else 1
        sealed = dma or coll
        semkey = "gd" if (dma and eng == "g") else eng
        self.pend[eng].append([fn, fwaits, drain, quant, sealed, semkey])
        if sealed:
            self.cnt[semkey] += quant
            tick = self.cnt[semkey]
        else:
            tick = None
        for b in writes:
            self.writer[b] = (semkey, tick, dma)
            self.readers.pop(b, None)
            if not dma:
                self.undrained[eng].add(b)
        for b in reads:
            self.readers.setdefault(b, {})[semkey] = (tick, dma)
            if not dma:
                self.unread[eng].add(b)
        if not sealed:
            self.marks[eng].append((len(self.pend[eng]) - 1, tuple(writes), tuple(reads)))

    def dma(self, q, out, in_, reads=(), writes=(), **kw):
        self.op(q, lambda E: E.dma_start(out=out, in_=in_, **kw),
                reads=reads, writes=writes, dma=True)

    def barrier_wait_all(self, eng):
        for p in self.ENGS + ("gd",):
            if p == eng:
                continue
            t = self._ticket(p)
            key = (eng, p)
            if t and self.hwm.get(key, 0) < t:
                self.hwm[key] = t
                self.pend[eng].append([None, [(p, t)], False, 1, False, eng])
                self.marks[eng].append((len(self.pend[eng]) - 1, (), ()))

    def flush(self, block):
        for e in self.ENGS:
            self._ticket(e)
        sem = self.sem
        post = {}
        for e in ("y", "g"):
            tot = dict(self.flushed)
            for idx, rec in enumerate(self.pend[e]):
                sk = rec[5]
                if sk in ("y", "gd") and rec[4] and rec[0] is not None:
                    tot[sk] = tot.get(sk, 0) + rec[3]
                    if tot[sk] in self.need.get(sk, ()):
                        post[(e, idx)] = (sk, tot[sk])
            for sk in ("y", "gd"):
                if sk in tot:
                    self.flushed[sk] = tot[sk]
        for e in self.ENGS:
            ops = self.pend[e]
            if not ops:
                continue
            def run(engobj, ops=ops, e=e, post=post):
                for idx, (fn, fwaits, drain, quant, inc, semkey) in enumerate(ops):
                    for psk, tick in fwaits:
                        engobj.wait_ge(sem[psk], tick)
                    if drain:
                        engobj.drain()
                    if fn is None:
                        if inc:
                            engobj.nop().then_inc(sem[semkey], quant)
                    else:
                        inst = fn(engobj)
                        if inc:
                            inst.then_inc(sem[semkey], quant)
                    pw = post.get((e, idx))
                    if pw is not None:
                        engobj.wait_ge(sem[pw[0]], pw[1])
            getattr(block, ENG_ATTR[e])(run)
            self.pend[e] = []
            self.undrained[e].clear()
            self.unread[e].clear()


def cdiv(a, b):
    return (a + b - 1) // b


def host_inputs(core, full):
    b, r = core // 4, core % 4
    f32 = np.float32
    d = {}
    x = np.asarray(full["L0_attn_x"])
    start = r * TOWN
    lo = start - HALO
    sl = np.zeros((T, D), f32)
    src_lo = max(lo, 0)
    sl[src_lo - lo:, :] = x[b, src_lo:start + TOWN, :]
    d["xin"] = np.ascontiguousarray(sl.T)

    pos = np.clip(np.arange(lo, start + TOWN), 0, S - 1).astype(np.float64)
    half = np.arange(0, DH, 2, dtype=np.float64)
    inv_freq = 1.0 / 10000.0 ** (half / DH)
    freqs = pos[:, None] * inv_freq[None, :]
    base_scale = (half + 0.4 * DH) / (1.4 * DH)
    power = (pos - S // 2) / (WIN // 2)
    scale = base_scale[None, :] ** power[:, None]
    cos, sin = np.cos(freqs), np.sin(freqs)
    def build(scl):
        A32 = (cos * scl).T
        B32 = (sin * scl).T
        A64 = np.concatenate([A32, A32], 0)
        D64 = np.concatenate([B32, -B32], 0)
        A = np.concatenate([A64, A64], 0).astype(f32)
        Dt = np.concatenate([D64, D64], 0).astype(f32)
        return np.ascontiguousarray(A), np.ascontiguousarray(Dt)
    d["tAq"], d["tDq"] = build(scale)
    d["tAk"], d["tDk"] = build(QK_SCALE / scale)

    for nm, w in (("wq", "L0_attn_Wq"), ("wk", "L0_attn_Wk"), ("wv", "L0_attn_Wv")):
        d[f"l0a_{nm}T"] = np.ascontiguousarray(np.asarray(full[w]).T)
    d["l0a_bq"] = np.ascontiguousarray(np.asarray(full["L0_attn_bq"]).astype(f32).reshape(NKC, 128).T)
    d["l0a_bk"] = np.ascontiguousarray(np.asarray(full["L0_attn_bk"]).astype(f32).reshape(NKC, 128).T)
    d["l0a_bv"] = np.asarray(full["L0_attn_bv"]).reshape(1, D).astype(f32)

    d["l0f_wqkv"] = np.ascontiguousarray(np.asarray(full["L0_ffn_Wqkv"]))
    d["l0f_wo"] = np.ascontiguousarray(np.asarray(full["L0_ffn_Wo"]))
    d["l0f_w1"] = np.ascontiguousarray(np.asarray(full["L0_ffn_W1"]))
    d["l0f_w2"] = np.ascontiguousarray(np.asarray(full["L0_ffn_W2"]))
    pc = lambda a: np.ascontiguousarray(np.asarray(a).astype(f32).reshape(-1, NKC, 128).transpose(0, 2, 1))
    d["l0f_g"] = pc(full["L0_ffn_ln_g"])
    d["l0f_b"] = pc(full["L0_ffn_ln_b"])
    d["l0f_b2"] = pc(full["L0_ffn_b2"])
    b1 = np.asarray(full["L0_ffn_b1"]).astype(f32)
    NJ = cdiv(DI, 128)
    b1c = np.zeros((L0DEPTH, 128, 2 * NJ), f32)
    for j in range(NJ):
        sz = min(128, DI - j * 128)
        b1c[:, :sz, 2 * j] = b1[:, j * 128:j * 128 + sz]
        b1c[:, :sz, 2 * j + 1] = b1[:, DI + j * 128:DI + j * 128 + sz]
    d["l0f_b1c"] = b1c
    def selput(sc):
        out = np.zeros((L0DEPTH, 16, D), f32)
        for h in range(16):
            out[:, h, DH * h:DH * (h + 1)] = sc
        return out
    d["l0f_selq"] = selput(np.asarray(full["L0_ffn_q_scale"]).astype(f32))
    d["l0f_selk"] = selput(np.asarray(full["L0_ffn_k_scale"]).astype(f32))

    for i in (1, 2, 3, 4):
        d[f"l1a_w{i}"] = np.ascontiguousarray(np.asarray(full[f"L1_attn_W{i}"]))
        d[f"l1a_b{i}"] = np.ascontiguousarray(np.asarray(full[f"L1_attn_b{i}"]).astype(f32).reshape(NKC, 128).T)
    d["l1a_b3r"] = np.asarray(full["L1_attn_b3"]).reshape(1, D).astype(f32)

    d["l1f_w12T"] = np.ascontiguousarray(np.asarray(full["L1_ffn_w12"]).transpose(0, 2, 1))
    d["l1f_w3T"] = np.ascontiguousarray(np.asarray(full["L1_ffn_w3"]).transpose(0, 2, 1))
    d["l1f_sw1T"] = np.ascontiguousarray(np.asarray(full["L1_ffn_shared_w1"]).T)
    d["l1f_sw2T"] = np.ascontiguousarray(np.asarray(full["L1_ffn_shared_w2"]).T)
    d["l1f_sw3T"] = np.ascontiguousarray(np.asarray(full["L1_ffn_shared_w3"]).T)
    d["l1f_rwT"] = np.ascontiguousarray(np.asarray(full["L1_ffn_router_w"]).T.astype(f32))
    d["l1f_eb"] = np.asarray(full["L1_ffn_expert_bias"]).reshape(NEXP, 1).astype(f32)

    d["c_ident"] = np.eye(128, dtype=f32)
    sw = np.zeros((128, 128), f32)
    for i_ in range(128):
        sw[i_, (i_ // 64) * 64 + ((i_ % 64) + 32) % 64] = 1.0
    d["c_swap"] = sw
    d["c_ones"] = np.ones((128, 128), f32)
    ar = np.arange(128, dtype=f32)
    d["c_ms1"] = np.stack([ar, ar + 128.0], 1)
    d["c_me1"] = d["c_ms1"] + 257.0
    ms2 = np.zeros((128, 4), f32)
    me2 = np.zeros((128, 4), f32)
    for kc in range(4):
        kj = 128.0 * kc + ar
        ms2[:, kc] = np.maximum(0.0, kj - 256.0)
        me2[:, kc] = np.minimum(256.0, kj + 1.0)
    d["c_ms2"], d["c_me2"] = ms2, me2
    NEGM = np.float32(-1.0e30)
    mk1 = np.zeros((4, 128, 512), f32)
    for fw in range(2):
        for qc in range(2):
            qi = 128 * qc + np.arange(128)[:, None]
            kj = np.arange(512)[None, :]
            if fw:
                bad = (kj < 256) | (kj > 256 + qi)
            else:
                bad = (kj < qi) | (kj > qi + 256)
            mk1[2 * fw + qc][bad] = NEGM
    d["c_mk1"] = mk1
    mk2 = np.zeros((4, 128, 256), f32)
    for kt in range(4):
        kj = 128 * kt + np.arange(128)[:, None]
        qi = np.arange(256)[None, :]
        bad = (qi < kj - 256) | (qi > kj)
        mk2[kt][bad] = NEGM
    d["c_mk2"] = mk2
    bd = np.zeros((128, NKC, 16), f32)
    for c in range(NKC):
        bd[0:64, c, 2 * c] = 1.0
        bd[64:128, c, 2 * c + 1] = 1.0
    d["c_bdsel"] = bd.reshape(128, NKC * 16)
    return d


def np_f32(x):
    return np.ascontiguousarray(np.asarray(x, dtype=np.float32))


def build_program():
    nc = bass.Bass("TRN2", target_bir_lowering=False)
    TT = 512
    NT = T // TT
    NTO = TOWN // TT
    NJ = cdiv(DI, 128)
    NCH = TOWN // 1024

    dt = lambda n, s, d, k="ExternalInput": nc.dram_tensor(n, s, d, kind=k).ap()
    xin = dt("xin", [D, T], F32)
    tA = {"q": dt("tAq", [128, T], F32), "k": dt("tAk", [128, T], F32)}
    tD = {"q": dt("tDq", [128, T], F32), "k": dt("tDk", [128, T], F32)}
    l0a_wT = {n: dt(f"l0a_w{n}T", [D, D], F32) for n in "qkv"}
    l0a_bq = dt("l0a_bq", [128, NKC], F32)
    l0a_bk = dt("l0a_bk", [128, NKC], F32)
    l0a_bv = dt("l0a_bv", [1, D], F32)
    l0f_wqkv = dt("l0f_wqkv", [L0DEPTH, D, 3 * D], F32)
    l0f_wo = dt("l0f_wo", [L0DEPTH, D, D], F32)
    l0f_w1 = dt("l0f_w1", [L0DEPTH, D, W1OUT], F32)
    l0f_w2 = dt("l0f_w2", [L0DEPTH, DI, D], F32)
    l0f_g = dt("l0f_g", [L0DEPTH, 128, NKC], F32)
    l0f_b = dt("l0f_b", [L0DEPTH, 128, NKC], F32)
    l0f_b1c = dt("l0f_b1c", [L0DEPTH, 128, 2 * NJ], F32)
    l0f_b2 = dt("l0f_b2", [L0DEPTH, 128, NKC], F32)
    l0f_selq = dt("l0f_selq", [L0DEPTH, 16, D], F32)
    l0f_selk = dt("l0f_selk", [L0DEPTH, 16, D], F32)
    l1a_w = {i: dt(f"l1a_w{i}", [D, D], F32) for i in (1, 2, 3, 4)}
    l1a_b = {i: dt(f"l1a_b{i}", [128, NKC], F32) for i in (1, 2, 3, 4)}
    l1a_b3r = dt("l1a_b3r", [1, D], F32)
    l1f_w12T = dt("l1f_w12T", [NEXP, D, 2 * RH], F32)
    l1f_w3T = dt("l1f_w3T", [NEXP, RH, D], F32)
    l1f_sw1T = dt("l1f_sw1T", [D, SHID], F32)
    l1f_sw2T = dt("l1f_sw2T", [D, SHID], F32)
    l1f_sw3T = dt("l1f_sw3T", [SHID, D], F32)
    l1f_rwT = dt("l1f_rwT", [D, NEXP], F32)
    l1f_eb = dt("l1f_eb", [NEXP, 1], F32)
    c_ident = dt("c_ident", [128, 128], F32)
    c_swap = dt("c_swap", [128, 128], F32)
    c_ones = dt("c_ones", [128, 128], F32)
    c_ms1 = dt("c_ms1", [128, 2], F32)
    c_me1 = dt("c_me1", [128, 2], F32)
    c_ms2 = dt("c_ms2", [128, 4], F32)
    c_me2 = dt("c_me2", [128, 4], F32)
    c_bdsel = dt("c_bdsel", [128, NKC * 16], F32)
    c_mk1 = dt("c_mk1", [4, 128, 512], F32)
    c_mk2 = dt("c_mk2", [4, 128, 256], F32)
    out = dt("out", [D, TOWN], F32, k="ExternalOutput")
    ya = dt("ya", [D, T], F32, k="Internal")
    yb = dt("yb", [D, T], F32, k="Internal")
    a0h = dt("a0h", [D, TOWN], F32, k="Internal")
    x1h = dt("x1h", [D, TOWN], F32, k="Internal")
    x2ph = dt("x2ph", [D, TOWN], F32, k="Internal")
    xn1h = dt("xn1h", [D, TOWN], BF16, k="Internal")
    agk_in = dt("agk_in", [D, TOWN], BF16, k="Internal")
    agk_out = dt("agk_out", [4 * D, TOWN], BF16, k="Internal")
    agv_in = dt("agv_in", [TOWN, D], BF16, k="Internal")
    agv_out = dt("agv_out", [4 * TOWN, D], BF16, k="Internal")
    mk_in = dt("mk_in", [1, 1], F32, k="Internal")
    mscr = dt("mscr", [64 * 128], BF16, k="Internal")
    gatesd = dt("gatesd", [NEXP, TOWN], F32, k="Internal")
    lscr = dt("lscr", [32 * 256], BF16, k="Internal")
    lscr2 = dt("lscr2", [16 * 512], F32, k="Internal")
    augd = dt("augd", [1, TOWN], BF16, k="Internal")
    qd = dt("qd", [D, TOWN], BF16, k="Internal")
    mk_out = dt("mk_out", [1, 1], F32, k="Internal")

    def fm(dram, c0, w, nk=NKC):
        return dram[:, c0:c0 + w].rearrange("(c p) t -> p c t", c=nk)

    uid = [0]

    def sb(stk, shape, dtyp):
        uid[0] += 1
        return stk.enter_context(nc.sbuf_tensor(f"b{uid[0]}", shape, dtyp))

    def ps(stk, shape, dtyp=F32):
        uid[0] += 1
        return stk.enter_context(nc.psum_tensor(f"p{uid[0]}", shape, dtyp))

    with ExitStack() as big:
        ctx = Ctx(nc, big)
        rg = [[0, 1, 2, 3], [4, 5, 6, 7]]

        ident = sb(big, [128, 128], F32)
        identb = sb(big, [128, 128], BF16)
        swpb = sb(big, [128, 128], BF16)
        onesb = sb(big, [128, 128], BF16)
        onesfr = sb(big, [1, 128], F32)
        bdsel = sb(big, [128, NKC * 16], BF16)
        mk1 = sb(big, [128, 4, 512], BF16)
        mk2 = sb(big, [128, 4, 256], BF16)

        def phase(fn):
            fn()
            ctx.pend["y"].append([None, [("y", ctx.cnt["y"])], False, 1, False, "y"])
            ctx.pend["g"].append([None, [("gd", ctx.cnt["gd"])], False, 1, False, "g"])
            for e in ctx.ENGS:
                ctx.barrier_wait_all(e)
            with nc.Block() as blk:
                ctx.flush(blk)

        def ph_consts():
            ctx.dma("y", ident[:], c_ident, writes=["ident"])
            ctx.dma("g", identb[:], c_ident, writes=["identb"])
            ctx.dma("g", swpb[:], c_swap, writes=["swpb"])
            ctx.dma("g", onesb[:], c_ones, writes=["onesb"])
            ctx.dma("y", onesfr[:], c_ones[0:1, :], writes=["onesfr"])
            ctx.dma("g", bdsel[:], c_bdsel, writes=["bdsel"])
            ctx.dma("g", mk1[:], c_mk1.rearrange("v p x -> p v x"), writes=["mk1"])
            ctx.dma("g", mk2[:], c_mk2.rearrange("v p x -> p v x"), writes=["mk2"])
        phase(ph_consts)

        def ph_outer_ln():
            with ExitStack() as stk:
                ysb = sb(stk, [128, NKC, TT], F32)
                xb_pr = [sb(stk, [128, NKC, TT], BF16) for _ in range(2)]
                xf_pr = [sb(stk, [128, NKC, TT], F32) for _ in range(2)]
                sq = sb(stk, [128, TT], BF16)
                rowb = sb(stk, [1, 2 * TT], F32)
                pum = ps(stk, [1, TT])
                pq2 = ps(stk, [1, TT])
                pbc = ps(stk, [128, 1024])
                for i in range(NT):
                    c0 = i * TT
                    if True:
                        xb, xf = xb_pr[i % 2], xf_pr[i % 2]
                        ctx.dma("g", xb[:], fm(xin, c0, TT), writes=[xb.name])
                        ctx.dma("y", xf[:], fm(xin, c0, TT), writes=[xf.name])
                        for kc in range(NKC):
                            ctx.op("s", lambda E, kc=kc, xb=xb: E.activation(sq[:], xb[:, kc, :], AF.Square),
                                   reads=[xb.name], writes=[sq.name])
                            ctx.op("t", lambda E, kc=kc, xb=xb: E.matmul(
                                pum[:], onesb[:, 0:1], xb[:, kc, :],
                                start=(kc == 0), stop=(kc == NKC - 1), skip_group_check=True),
                                reads=[xb.name, "onesb"], writes=[pum.name])
                            ctx.op("t", lambda E, kc=kc: E.matmul(
                                pq2[:], onesb[:, 0:1], sq[:],
                                start=(kc == 0), stop=(kc == NKC - 1), skip_group_check=True),
                                reads=[sq.name, "onesb"], writes=[pq2.name])
                        mu = rowb[0:1, 0:TT]
                        var = rowb[0:1, TT:2 * TT]
                        rn = rowb.name
                        ctx.op("v", lambda E: E.tensor_scalar(mu, pum[:], 1.0 / D, None, op0=AluOpType.mult),
                               reads=[pum.name], writes=[rn])
                        ctx.op("v", lambda E: E.scalar_tensor_tensor(
                            var, mu, 1.0, mu, op0=AluOpType.mult, op1=AluOpType.mult),
                            reads=[rn], writes=[rn])
                        ctx.op("v", lambda E: E.scalar_tensor_tensor(
                            var, pq2[:], 1.0 / D, var, op0=AluOpType.mult, op1=AluOpType.subtract),
                            reads=[pq2.name, rn], writes=[rn])
                        ctx.op("v", lambda E: E.tensor_scalar(var, var, 1e-5, None, op0=AluOpType.add),
                               reads=[rn], writes=[rn])
                        ctx.op("v", lambda E: E.reciprocal(var, var), reads=[rn], writes=[rn])
                        ctx.op("s", lambda E: E.activation(var, var, AF.Sqrt),
                               reads=[rn], writes=[rn])
                        ctx.op("v", lambda E: E.scalar_tensor_tensor(
                            mu, mu, -1.0, var, op0=AluOpType.mult, op1=AluOpType.mult),
                            reads=[rn], writes=[rn])
                        ctx.op("t", lambda E: E.matmul(
                            pbc[:, 0:512], onesfr[0:1, 0:128], var,
                            start=True, stop=True, skip_group_check=True),
                            reads=["onesfr", rn], writes=[pbc.name])
                        ctx.op("t", lambda E: E.matmul(
                            pbc[:, 512:1024], onesfr[0:1, 0:128], mu,
                            start=True, stop=True, skip_group_check=True),
                            reads=["onesfr", rn], writes=[pbc.name])
                        for kc in range(NKC):
                            ctx.op("v", lambda E, kc=kc, xf=xf: E.tensor_tensor(
                                ysb[:, kc, :], xf[:, kc, :], pbc[:, 0:512], op=AluOpType.mult),
                                reads=[xf.name, pbc.name], writes=[ysb.name])
                            ctx.op("v", lambda E, kc=kc: E.tensor_tensor(
                                ysb[:, kc, :], ysb[:, kc, :], pbc[:, 512:1024], op=AluOpType.add),
                                reads=[ysb.name, pbc.name], writes=[ysb.name])
                        ctx.dma("y", fm(ya, c0, TT), ysb[:], reads=[ysb.name], writes=["ya"])
        phase(ph_outer_ln)



        def ph_l0_attn():
            with ExitStack() as stk:
                wq = sb(stk, [128, NKC, D], BF16)
                wk = sb(stk, [128, NKC, D], BF16)
                wv = sb(stk, [128, NKC, D], BF16)
                bqs = sb(stk, [128, NKC], F32)
                bks = sb(stk, [128, NKC], F32)
                bvr = sb(stk, [1, D], BF16)
                ctx.dma("g", wq[:], l0a_wT["q"].rearrange("(c p) e -> p c e", c=NKC), writes=[wq.name])
                ctx.dma("g", wk[:], l0a_wT["k"].rearrange("(c p) e -> p c e", c=NKC), writes=[wk.name])
                ctx.dma("g", wv[:], l0a_wT["v"].rearrange("(c p) e -> p c e", c=NKC), writes=[wv.name])
                ctx.dma("y", bqs[:], l0a_bq, writes=[bqs.name])
                ctx.dma("y", bks[:], l0a_bk, writes=[bks.name])
                ctx.dma("g", bvr[:], l0a_bv, writes=[bvr.name])
                cs = stk
                xc_pr = [sb(cs, [128, NKC, 1024], BF16) for _ in range(2)]
                qb = sb(cs, [128, NKC, 1024], BF16)
                kb = sb(cs, [128, NKC, 1024], BF16)
                vb = sb(cs, [128, NKC, 1024], BF16)
                mcols = sb(cs, [128, 16], F32)
                mc8 = sb(cs, [128, 8], F32)
                mrows = sb(cs, [8, 128], BF16)
                mflat = sb(cs, [1, 8 * 128], BF16)
                scr = sb(cs, [128, 512], F32)
                st2 = sb(cs, [128, 256], F32)
                pt = sb(cs, [128, 8, 256], BF16)
                lsb = sb(cs, [1, 256], F32)
                rbb = sb(cs, [128, 256], F32)
                a0sb = sb(cs, [128, NKC, 256], F32)
                pq = ps(cs, [128, 512])
                ps1 = pq
                pmx = ps(cs, [8, 512])
                pmt = pmx[0:8, 0:128]
                plv = pmx[0:1, 128:384]
                pstA = ps(cs, [128, 512])
                pstB = ps(cs, [128, 512])
                pst2 = (pstA, pstB)
                pav = ps(cs, [128, NKC, 256])
                for ch in range(TOWN // 1024):
                    if True:
                        c0 = HALO + 1024 * ch
                        xc = xc_pr[ch % 2]
                        ctx.dma("g", xc[:], fm(ya, c0, 1024), reads=["ya"], writes=[xc.name])
                        for (wsb, bias, dst) in ((wq, bqs, qb), (wk, bks, kb)):
                            for oc in range(NKC):
                                for nt in range(2):
                                    for kc in range(NKC):
                                        ctx.op("t", lambda E, oc=oc, nt=nt, kc=kc, wsb=wsb, xc=xc: E.matmul(
                                            pq[:], wsb[:, kc, oc * 128:oc * 128 + 128],
                                            xc[:, kc, nt * 512:nt * 512 + 512],
                                            start=(kc == 0), stop=(kc == NKC - 1), skip_group_check=True),
                                            reads=[wsb.name, xc.name], writes=[pq.name])
                                    ctx.op("s", lambda E, oc=oc, nt=nt, bias=bias, dst=dst: E.activation(
                                        dst[:, oc, nt * 512:nt * 512 + 512], pq[:], AF.Identity,
                                        bias=bias[:, oc:oc + 1]),
                                        reads=[pq.name, bias.name], writes=[dst.name])
                        for tch in range(8):
                            for nt in range(2):
                                for kc in range(NKC):
                                    ctx.op("t", lambda E, tch=tch, nt=nt, kc=kc, xc=xc: E.matmul(
                                        pq[:], xc[:, kc, tch * 128:tch * 128 + 128],
                                        wv[:, kc, nt * 512:nt * 512 + 512],
                                        start=(kc == 0), stop=False, skip_group_check=True),
                                        reads=[wv.name, xc.name], writes=[pq.name])
                                ctx.op("t", lambda E, nt=nt: E.matmul(
                                    pq[:], onesb[0:1, 0:128], bvr[0:1, nt * 512:nt * 512 + 512],
                                    start=False, stop=True, skip_group_check=True),
                                    reads=["onesb", bvr.name], writes=[pq.name])
                                ctx.op("s", lambda E, tch=tch, nt=nt: E.activation(
                                    vb[:, tch, nt * 512:nt * 512 + 512], pq[:], AF.Copy),
                                    reads=[pq.name], writes=[vb.name])
                        for qc in range(8):
                            for kt in range(2):
                                for ec in range(NKC):
                                    ctx.op("t", lambda E, qc=qc, kt=kt, ec=ec: E.matmul(
                                        ps1[:], qb[:, ec, qc * 128:qc * 128 + 128],
                                        kb[:, ec, kt * 512:kt * 512 + 512],
                                        start=(ec == 0), stop=(ec == NKC - 1), skip_group_check=True),
                                        reads=[qb.name, kb.name], writes=[ps1.name])
                                ctx.op("v", lambda E, qc=qc, kt=kt: E.reduce_max(
                                    mcols[:, 2 * qc + kt:2 * qc + kt + 1], ps1[:], axis=AX.X),
                                    reads=[ps1.name], writes=[mcols.name])
                        ctx.op("v", lambda E: E.tensor_tensor(
                            mc8[:], mcols[:].rearrange("p (a b) -> p a b", b=2)[:, :, 0],
                            mcols[:].rearrange("p (a b) -> p a b", b=2)[:, :, 1],
                            op=AluOpType.max),
                            reads=[mcols.name], writes=[mc8.name])
                        ctx.op("t", lambda E: E.transpose(pmt, mc8[:], ident[:]),
                               reads=[mc8.name, "ident"], writes=[pmx.name])
                        ctx.op("s", lambda E: E.activation(mrows[:], pmt, AF.Identity, scale=-1.0),
                               reads=[pmx.name], writes=[mrows.name])
                        ctx.dma("y", mscr[0:8 * 128].rearrange("(a b) -> a b", b=128), mrows[:],
                                reads=[mrows.name], writes=["mscr"])
                        ctx.dma("y", mflat[:], mscr[0:8 * 128].rearrange("(a b) -> a b", a=1),
                                reads=["mscr"], writes=[mflat.name])
                        for qq in range(4):
                            for kt in range(8):
                                pS = pst2[kt % 2]
                                for ec in range(NKC):
                                    ctx.op("t", lambda E, qq=qq, kt=kt, ec=ec, pS=pS: E.matmul(
                                        pS[:, 0:256], kb[:, ec, kt * 128:kt * 128 + 128],
                                        qb[:, ec, qq * 256:qq * 256 + 256],
                                        start=(ec == 0), stop=False, skip_group_check=True),
                                        reads=[qb.name, kb.name], writes=[pS.name])
                                for half in range(2):
                                    ctx.op("t", lambda E, qq=qq, half=half, pS=pS: E.matmul(
                                        pS[:, half * 128:half * 128 + 128],
                                        onesb[0:1, 0:128],
                                        mflat[0:1, 128 * (2 * qq + half):128 * (2 * qq + half) + 128],
                                        start=False, stop=(half == 1), skip_group_check=True),
                                        reads=["onesb", mflat.name], writes=[pS.name])
                                ctx.op("s", lambda E, kt=kt, pS=pS: E.activation(
                                    pt[:, kt, :], pS[:, 0:256], AF.Exp),
                                    reads=[pS.name], writes=[pt.name + str(kt)])
                                ctx.op("t", lambda E, kt=kt: E.matmul(
                                    plv, onesb[:, 0:1], pt[:, kt, :],
                                    start=(kt == 0), stop=(kt == 7), skip_group_check=True),
                                    reads=["onesb", pt.name + str(kt)], writes=[pmx.name])
                            ctx.op("v", lambda E: E.tensor_copy(lsb[:], plv),
                                   reads=[pmx.name], writes=[lsb.name])
                            ctx.op("v", lambda E: E.reciprocal(lsb[:], lsb[:]),
                                   reads=[lsb.name], writes=[lsb.name])
                            ctx.dma("y", lscr2[0:256].rearrange("(a b) -> a b", a=1), lsb[:],
                                    reads=[lsb.name], writes=["lscr2"])
                            ctx.dma("y", rbb[:],
                                    lscr2[0:256].rearrange("(a b) -> a b", a=1)
                                    .broadcast_to([128, 256]),
                                    reads=["lscr2"], writes=[rbb.name])
                            for dc in range(NKC):
                                for kt in range(8):
                                    ctx.op("t", lambda E, kt=kt, dc=dc: E.matmul(
                                        pav[:, dc, :], vb[:, kt, dc * 128:dc * 128 + 128],
                                        pt[:, kt, :], start=(kt == 0), stop=(kt == 7), skip_group_check=True),
                                        reads=[vb.name, pt.name + str(kt)], writes=[pav.name])
                                ctx.op("v", lambda E, dc=dc: E.scalar_tensor_tensor(
                                    a0sb[:, dc, :], pav[:, dc, :], 1.0, rbb[:],
                                    op0=AluOpType.mult, op1=AluOpType.mult),
                                    reads=[pav.name, rbb.name], writes=[a0sb.name])
                            ctx.dma("y", fm(a0h, 1024 * ch + 256 * qq, 256), a0sb[:],
                                    reads=[a0sb.name], writes=["a0h"])
        phase(ph_l0_attn)

        TTW = 512
        NTI = T // TTW

        def ph_inner_attn(l, src, dst, sname, dname):
            with ExitStack() as stk:
                wqkv = sb(stk, [128, NKC, 3 * D], BF16)
                wo = sb(stk, [128, NKC, D], BF16)
                selq = sb(stk, [16, D], F32)
                selk = sb(stk, [16, D], F32)
                ctx.dma("g", wqkv[:], l0f_wqkv[l].rearrange("(c p) e -> p c e", c=NKC),
                        writes=[wqkv.name])
                ctx.dma("g", wo[:], l0f_wo[l].rearrange("(c p) e -> p c e", c=NKC),
                        writes=[wo.name])
                ctx.dma("y", selq[:], l0f_selq[l], writes=[selq.name])
                ctx.dma("y", selk[:], l0f_selk[l], writes=[selk.name])
                vony = sb(stk, [128, 6, HEADS * 65], BF16)
                vony_v = vony[:].rearrange("p c (h u) -> p c h u", u=65)
                ctx.op("v", lambda E: E.memset(
                    vony[:].rearrange("p c (h u) -> p c h u", u=65)[:, :, :, 64:65], 1.0),
                    writes=[vony.name])
                ybf_pr = [sb(stk, [128, NKC, 769], BF16), sb(stk, [128, NKC, 769], BF16)]
                yf32 = sb(stk, [128, NKC, TTW], F32)
                tq_a = sb(stk, [128, TTW], BF16)
                tq_d = sb(stk, [128, TTW], BF16)
                tk_a = sb(stk, [128, 768], BF16)
                tk_d = sb(stk, [128, 768], BF16)
                qrot = sb(stk, [128, NKC, TTW], BF16)
                krot = sb(stk, [128, NKC, 768], BF16)
                qraw = sb(stk, [128, NKC, 768], BF16)
                scrb = sb(stk, [128, 768], BF16)
                sqq = scrb
                nsr = sb(stk, [16, 768], F32)
                tt1 = sb(stk, [128, 768], BF16)
                tt2 = sb(stk, [128, 768], BF16)
                mcols = sb(stk, [128, 64], F32)
                mrows = sb(stk, [64, 128], BF16)
                mflat = sb(stk, [1, 32 * 128], BF16)
                st2 = sb(stk, [128, 2, 256], F32)
                ptb = sb(stk, [128, 2, 256], BF16)
                lrow = sb(stk, [1, 32 * 256], BF16)
                lmat2 = sb(stk, [16, TTW], F32)
                rbb = sb(stk, [128, TTW], BF16)
                attc = qraw[:].rearrange("p c t -> p c t")[:, :, 0:TTW]
                attc_name = qraw.name
                yo = None
                pq = ps(stk, [128, 512])
                pun = ps(stk, [128, 1024])
                psq16 = pun[0:16, 0:768]
                prb = pun[0:128, 0:512]
                pmt = pun[0:64, 0:128]
                pstA = ps(stk, [128, 512])
                pstB = ps(stk, [128, 512])
                pst2 = (pstA, pstB)
                pavA = ps(stk, [65, 512])
                pavB = ps(stk, [65, 512])
                pswp = ps(stk, [128, 512])
                pav2 = (pavA, pavB)

                def proj_qk(i, which, ybf):
                    w = TTW if which == "q" else 768
                    dstb = qrot if which == "q" else krot
                    sel = selq if which == "q" else selk
                    ta = tq_a if which == "q" else tk_a
                    td = tq_d if which == "q" else tk_d
                    off = 1024 if which == "k" else 0
                    u0 = 257 if which == "q" else 1
                    nts = [(0, 512)] if which == "q" else [(0, 512), (512, 256)]
                    for oc in range(NKC):
                        for (n0, nw) in nts:
                            for kc in range(NKC):
                                lo = (u0 if kc < 4 else u0 - 1) + n0
                                ctx.op("t", lambda E, oc=oc, kc=kc, lo=lo, off=off, nw=nw: E.matmul(
                                    pq[:, 0:nw], wqkv[:, kc, off + oc * 128:off + oc * 128 + 128],
                                    ybf[:, kc, lo:lo + nw],
                                    start=(kc == 0), stop=(kc == NKC - 1), skip_group_check=True),
                                    reads=[wqkv.name, ybf.name], writes=[pq.name])
                            ctx.op("s", lambda E, n0=n0, nw=nw: E.activation(
                                sqq[:, n0:n0 + nw], pq[:, 0:nw], AF.Square),
                                reads=[pq.name], writes=[sqq.name])
                            ctx.op("s", lambda E, oc=oc, n0=n0, nw=nw: E.activation(
                                qraw[:, oc, n0:n0 + nw], pq[:, 0:nw], AF.Copy),
                                reads=[pq.name], writes=[qraw.name])
                            ctx.op("t", lambda E, oc=oc, n0=n0, nw=nw: E.matmul(
                                psq16[:, n0:n0 + nw], bdsel[:, oc * 16:oc * 16 + 16],
                                sqq[:, n0:n0 + nw],
                                start=(oc == 0), stop=(oc == NKC - 1), skip_group_check=True),
                                reads=[sqq.name, "bdsel"], writes=[pun.name])
                    ctx.op("s", lambda E, w=w: E.activation(nsr[:, 0:w], psq16[:, 0:w], AF.Sqrt),
                           reads=[pun.name], writes=[nsr.name])
                    ctx.op("v", lambda E, w=w: E.tensor_scalar_max(nsr[:, 0:w], nsr[:, 0:w], 1e-12),
                           reads=[nsr.name], writes=[nsr.name])
                    ctx.op("v", lambda E, w=w: E.reciprocal(nsr[:, 0:w], nsr[:, 0:w]),
                           reads=[nsr.name], writes=[nsr.name])
                    for oc in range(NKC):
                        for (n0, nw) in nts:
                            ctx.op("t", lambda E, oc=oc, n0=n0, nw=nw, sel=sel: E.matmul(
                                prb[:, 0:nw], sel[:, oc * 128:oc * 128 + 128], nsr[:, n0:n0 + nw],
                                start=True, stop=True, skip_group_check=True),
                                reads=[sel.name, nsr.name], writes=[pun.name])
                            ctx.op("v", lambda E, oc=oc, n0=n0, nw=nw: E.tensor_tensor(
                                tt1[:, n0:n0 + nw], qraw[:, oc, n0:n0 + nw], prb[:, 0:nw],
                                op=AluOpType.mult),
                                reads=[qraw.name, pun.name], writes=[tt1.name])
                        ctx.op("v", lambda E, w=w, ta=ta: E.tensor_tensor(
                            tt2[:, 0:w], tt1[:, 0:w], ta[:, 0:w], op=AluOpType.mult),
                            reads=[tt1.name, ta.name], writes=[tt2.name])
                        ctx.op("v", lambda E, w=w, td=td: E.tensor_tensor(
                            scrb[:, 0:w], tt1[:, 0:w], td[:, 0:w], op=AluOpType.mult),
                            reads=[tt1.name, td.name], writes=[scrb.name])
                        for (n0, nw) in nts:
                            ctx.op("t", lambda E, n0=n0, nw=nw: E.matmul(
                                pswp[:, 0:nw], identb[:], tt2[:, n0:n0 + nw],
                                start=True, stop=False, skip_group_check=True),
                                reads=["identb", tt2.name], writes=[pswp.name])
                            ctx.op("t", lambda E, n0=n0, nw=nw: E.matmul(
                                pswp[:, 0:nw], swpb[:], scrb[:, n0:n0 + nw],
                                start=False, stop=True, skip_group_check=True),
                                reads=["swpb", scrb.name], writes=[pswp.name])
                            ctx.op("s", lambda E, oc=oc, n0=n0, nw=nw, dstb=dstb: E.activation(
                                dstb[:, oc, n0:n0 + nw], pswp[:, 0:nw], AF.Copy),
                                reads=[pswp.name], writes=[dstb.name])

                for i in range(NTI):
                    c0 = i * TTW
                    ybf = ybf_pr[i % 2]
                    if i == 0:
                        ctx.op("v", lambda E, ybf=ybf: E.memset(ybf[:, :, 0:257], 0.0),
                               writes=[ybf.name])
                        ctx.dma("g", ybf[:, :, 257:769], fm(src, 0, TTW),
                                reads=[sname], writes=[ybf.name])
                        ctx.op("v", lambda E: E.memset(tk_a[:, 0:256], 0.0), writes=[tk_a.name])
                        ctx.op("v", lambda E: E.memset(tk_d[:, 0:256], 0.0), writes=[tk_d.name])
                        ctx.dma("g", tk_a[:, 256:768], tA["k"][:, 0:TTW], writes=[tk_a.name])
                        ctx.dma("g", tk_d[:, 256:768], tD["k"][:, 0:TTW], writes=[tk_d.name])
                    else:
                        ctx.dma("g", ybf[:], fm(src, c0 - 257, 769),
                                reads=[sname], writes=[ybf.name])
                        ctx.dma("g", tk_a[:], tA["k"][:, c0 - 256:c0 + 512], writes=[tk_a.name])
                        ctx.dma("g", tk_d[:], tD["k"][:, c0 - 256:c0 + 512], writes=[tk_d.name])
                    ctx.dma("y", yf32[:], fm(src, c0, TTW), reads=[sname], writes=[yf32.name])
                    ctx.dma("g", tq_a[:], tA["q"][:, c0:c0 + TTW], writes=[tq_a.name])
                    ctx.dma("g", tq_d[:], tD["q"][:, c0:c0 + TTW], writes=[tq_d.name])
                    proj_qk(i, "q", ybf)
                    proj_qk(i, "k", ybf)
                    for tch in range(6):
                        for nt in range(2):
                            for kc in range(NKC):
                                lo = (1 if kc < 4 else 0) + 128 * tch
                                ctx.op("t", lambda E, tch=tch, nt=nt, kc=kc, lo=lo, ybf=ybf: E.matmul(
                                    pq[:], ybf[:, kc, lo:lo + 128],
                                    wqkv[:, kc, 2048 + nt * 512:2048 + nt * 512 + 512],
                                    start=(kc == 0), stop=(kc == NKC - 1), skip_group_check=True),
                                    reads=[wqkv.name, ybf.name], writes=[pq.name])
                            ctx.op("s", lambda E, tch=tch, nt=nt: E.activation(
                                vony_v[:, tch, 8 * nt:8 * nt + 8, 0:64],
                                pq[:].rearrange("p (h u) -> p h u", u=64), AF.Copy),
                                reads=[pq.name], writes=[vony.name])
                    for h in range(HEADS):
                        hp, hc = 64 * (h % 2), h // 2
                        for w in range(2):
                            fw = (i == 0 and w == 0)
                            j2 = (h * 2 + w) * 2
                            for qc in range(2):
                                ctx.op("t", lambda E, h=h, w=w, qc=qc, hp=hp, hc=hc: E.matmul(
                                    pq[:], qrot[hp:hp + 64, hc, 256 * w + 128 * qc:256 * w + 128 * qc + 128],
                                    krot[hp:hp + 64, hc, 256 * w:256 * w + 512],
                                    start=True, stop=True, skip_group_check=True),
                                    reads=[qrot.name, krot.name], writes=[pq.name])
                                mv = 2 * (1 if fw else 0) + qc
                                ctx.op("v", lambda E, mv=mv: E.tensor_tensor(
                                    scrb[:, 0:512], pq[:], mk1[:, mv, :], op=AluOpType.add),
                                    reads=[pq.name, "mk1"], writes=[scrb.name])
                                ctx.op("v", lambda E, qc=qc, j2=j2: E.reduce_max(
                                    mcols[:, j2 + qc:j2 + qc + 1], scrb[:, 0:512], axis=AX.X),
                                    reads=[scrb.name], writes=[mcols.name])
                    ctx.op("t", lambda E: E.transpose(pmt, mcols[:], ident[:]),
                           reads=[mcols.name, "ident"], writes=[pun.name])
                    ctx.op("s", lambda E: E.activation(mrows[:], pmt, AF.Identity, scale=-1.0),
                           reads=[pun.name], writes=[mrows.name])
                    ctx.dma("y", mscr.rearrange("(a b) -> a b", b=128), mrows[:],
                            reads=[mrows.name], writes=["mscr"])
                    for h in range(HEADS):
                        hp, hc = 64 * (h % 2), h // 2
                        if h % 8 == 0:
                            ctx.dma("y", mflat[:],
                                    mscr[4096 * (h // 8):4096 * (h // 8) + 4096]
                                    .rearrange("(a b) -> a b", a=1),
                                    reads=["mscr"], writes=[mflat.name])
                        for w in range(2):
                            fw = (i == 0 and w == 0)
                            j2 = (h * 2 + w) * 2 - 32 * (h // 8)
                            pA = pav2[(h * 2 + w) % 2]
                            kts = range(2, 4) if fw else range(4)
                            first_kt = 2 if fw else 0
                            for kt in kts:
                                pS = pst2[kt % 2]
                                ctx.op("t", lambda E, h=h, w=w, kt=kt, hp=hp, hc=hc, pS=pS: E.matmul(
                                    pS[:, 0:256], krot[hp:hp + 64, hc, 256 * w + 128 * kt:256 * w + 128 * kt + 128],
                                    qrot[hp:hp + 64, hc, 256 * w:256 * w + 256],
                                    start=True, stop=False, skip_group_check=True),
                                    reads=[qrot.name, krot.name], writes=[pS.name])
                                for half in range(2):
                                    ctx.op("t", lambda E, j2=j2, half=half, pS=pS: E.matmul(
                                        pS[:, half * 128:half * 128 + 128],
                                        onesb[0:1, 0:128],
                                        mflat[0:1, 128 * (j2 + half):128 * (j2 + half) + 128],
                                        start=False, stop=(half == 1), skip_group_check=True),
                                        reads=["onesb", mflat.name], writes=[pS.name])
                                ctx.op("v", lambda E, kt=kt, pS=pS: E.tensor_tensor(
                                    st2[:, kt % 2, :], pS[:, 0:256], mk2[:, kt, :], op=AluOpType.add),
                                    reads=[pS.name, "mk2"], writes=[st2.name + str(kt % 2)])
                                ctx.op("s", lambda E, kt=kt: E.activation(
                                    ptb[:, kt % 2, :], st2[:, kt % 2, :], AF.Exp),
                                    reads=[st2.name + str(kt % 2)], writes=[ptb.name + str(kt % 2)])
                                ctx.op("t", lambda E, h=h, w=w, kt=kt, pA=pA, first_kt=first_kt: E.matmul(
                                    pA[:, 0:256], vony[:, 2 * w + kt, 65 * h:65 * h + 65],
                                    ptb[:, kt % 2, :],
                                    start=(kt == first_kt), stop=(kt == 3), skip_group_check=True),
                                    reads=[vony.name, ptb.name + str(kt % 2)], writes=[pA.name])
                            ctx.op("v", lambda E, h=h, w=w, pA=pA: E.tensor_copy(
                                lrow[0:1, 256 * (h * 2 + w):256 * (h * 2 + w) + 256],
                                pA[64:65, 0:256]),
                                reads=[pA.name], writes=[lrow.name])
                            ctx.op("s", lambda E, h=h, w=w, hp=hp, hc=hc, pA=pA: E.activation(
                                attc[hp:hp + 64, hc, 256 * w:256 * w + 256], pA[0:64, 0:256], AF.Copy),
                                reads=[pA.name], writes=[attc_name])
                    ctx.dma("y", lscr.rearrange("(a b) -> a b", a=1), lrow[:],
                            reads=[lrow.name], writes=["lscr"])
                    ctx.dma("g", lmat2[:],
                            lscr.rearrange("(h x) -> h x", h=16),
                            reads=["lscr"], writes=[lmat2.name])
                    ctx.op("v", lambda E: E.reciprocal(lmat2[:], lmat2[:]),
                           reads=[lmat2.name], writes=[lmat2.name])
                    ctx.dma("y", lscr2.rearrange("(h x) -> h x", h=16), lmat2[:],
                            reads=[lmat2.name], writes=["lscr2"])
                    for c in range(NKC):
                        lv = lscr2.rearrange("(h x) -> h x", h=16)
                        ctx.dma("g", rbb[0:64, :],
                                lv[2 * c:2 * c + 1, :].broadcast_to([64, TTW]),
                                reads=["lscr2"], writes=[rbb.name])
                        ctx.dma("g", rbb[64:128, :],
                                lv[2 * c + 1:2 * c + 2, :].broadcast_to([64, TTW]),
                                reads=["lscr2"], writes=[rbb.name])
                        ctx.op("v", lambda E, c=c: E.tensor_tensor(
                            attc[:, c, :], attc[:, c, :], rbb[:], op=AluOpType.mult),
                            reads=[attc_name, rbb.name], writes=[attc_name])
                    for oc in range(NKC):
                        for ic in range(NKC):
                            ctx.op("t", lambda E, oc=oc, ic=ic: E.matmul(
                                pq[:], wo[:, ic, oc * 128:oc * 128 + 128], attc[:, ic, :],
                                start=(ic == 0), stop=(ic == NKC - 1), skip_group_check=True),
                                reads=[wo.name, attc_name], writes=[pq.name])
                        ctx.op("v", lambda E, oc=oc: E.scalar_tensor_tensor(
                            yf32[:, oc, :], pq[:], 1.0, yf32[:, oc, :],
                            op0=AluOpType.mult, op1=AluOpType.add),
                            reads=[pq.name, yf32.name], writes=[yf32.name])
                    ctx.dma("y", fm(dst, c0, TTW), yf32[:], reads=[yf32.name], writes=[dname])

        def ph_inner_ffn(l, src, dst, sname, dname):
            NJ = cdiv(DI, 128)
            with ExitStack() as stk:
                w1 = sb(stk, [128, NKC, W1OUT], BF16)
                w2 = sb(stk, [128, NJ, D], BF16)
                b1c = sb(stk, [128, 2 * NJ], F32)
                b2c = sb(stk, [128, NKC], F32)
                gc = sb(stk, [128, NKC], F32)
                bc = sb(stk, [128, NKC], F32)
                ctx.dma("g", w1[:], l0f_w1[l].rearrange("(c p) e -> p c e", c=NKC),
                        writes=[w1.name])
                for j in range(NJ):
                    sz = min(128, DI - 128 * j)
                    ctx.dma("g", w2[0:sz, j, :], l0f_w2[l][128 * j:128 * j + sz, :],
                            writes=[w2.name])
                ctx.dma("y", b1c[:], l0f_b1c[l], writes=[b1c.name])
                ctx.dma("y", b2c[:], l0f_b2[l], writes=[b2c.name])
                ctx.dma("y", gc[:], l0f_g[l], writes=[gc.name])
                ctx.dma("y", bc[:], l0f_b[l], writes=[bc.name])
                ybf = sb(stk, [128, NKC, 513], BF16)
                yf32 = sb(stk, [128, NKC, TTW], F32)
                sq = sb(stk, [128, TTW], BF16)
                rowb = sb(stk, [1, 2 * TTW], F32)
                t1 = sb(stk, [128, TTW], F32)
                xnb = sb(stk, [128, NKC, TTW], BF16)
                pbc = ps(stk, [128, 1024])
                gel = sb(stk, [128, TTW], BF16)
                ab = sb(stk, [128, TTW], BF16)
                glu = sb(stk, [128, NJ, TTW], BF16)
                pa = ps(stk, [128, 512])
                pg = ps(stk, [128, 512])
                pum = ps(stk, [1, 512])
                pq2 = ps(stk, [1, 512])
                for i in range(NTI):
                    c0 = i * TTW
                    if i == 0:
                        ctx.op("v", lambda E: E.memset(ybf[:, :, 0:1], 0.0), writes=[ybf.name])
                        ctx.dma("g", ybf[:, :, 1:513], fm(src, 0, TTW),
                                reads=[sname], writes=[ybf.name])
                    else:
                        ctx.dma("g", ybf[:], fm(src, c0 - 1, 513),
                                reads=[sname], writes=[ybf.name])
                    ctx.dma("y", yf32[:], fm(src, c0, TTW), reads=[sname], writes=[yf32.name])
                    for kc in range(NKC):
                        lo = 1 if kc < 4 else 0
                        ctx.op("s", lambda E, kc=kc, lo=lo: E.activation(
                            sq[:], ybf[:, kc, lo:lo + TTW], AF.Square),
                            reads=[ybf.name], writes=[sq.name])
                        ctx.op("t", lambda E, kc=kc, lo=lo: E.matmul(
                            pum[:], onesb[:, 0:1], ybf[:, kc, lo:lo + TTW],
                            start=(kc == 0), stop=(kc == NKC - 1), skip_group_check=True),
                            reads=[ybf.name, "onesb"], writes=[pum.name])
                        ctx.op("t", lambda E, kc=kc: E.matmul(
                            pq2[:], onesb[:, 0:1], sq[:],
                            start=(kc == 0), stop=(kc == NKC - 1), skip_group_check=True),
                            reads=[sq.name, "onesb"], writes=[pq2.name])
                    mu = rowb[0:1, 0:TTW]
                    var = rowb[0:1, TTW:2 * TTW]
                    rn = rowb.name
                    ctx.op("v", lambda E: E.tensor_scalar(mu, pum[:], 1.0 / D, None, op0=AluOpType.mult),
                           reads=[pum.name], writes=[rn])
                    ctx.op("v", lambda E: E.scalar_tensor_tensor(
                        var, mu, 1.0, mu, op0=AluOpType.mult, op1=AluOpType.mult),
                        reads=[rn], writes=[rn])
                    ctx.op("v", lambda E: E.scalar_tensor_tensor(
                        var, pq2[:], 1.0 / D, var, op0=AluOpType.mult, op1=AluOpType.subtract),
                        reads=[pq2.name, rn], writes=[rn])
                    ctx.op("v", lambda E: E.tensor_scalar(var, var, 1e-5, None, op0=AluOpType.add),
                           reads=[rn], writes=[rn])
                    ctx.op("v", lambda E: E.reciprocal(var, var), reads=[rn], writes=[rn])
                    ctx.op("s", lambda E: E.activation(var, var, AF.Sqrt), reads=[rn], writes=[rn])
                    ctx.op("v", lambda E: E.scalar_tensor_tensor(
                        mu, mu, -1.0, var, op0=AluOpType.mult, op1=AluOpType.mult),
                        reads=[rn], writes=[rn])
                    ctx.op("t", lambda E: E.matmul(
                        pbc[:, 0:512], onesfr[0:1, 0:128], var,
                        start=True, stop=True, skip_group_check=True),
                        reads=["onesfr", rn], writes=[pbc.name])
                    ctx.op("t", lambda E: E.matmul(
                        pbc[:, 512:1024], onesfr[0:1, 0:128], mu,
                        start=True, stop=True, skip_group_check=True),
                        reads=["onesfr", rn], writes=[pbc.name])
                    for kc in range(NKC):
                        lo = 1 if kc < 4 else 0
                        ctx.op("v", lambda E, kc=kc, lo=lo: E.tensor_tensor(
                            t1[:], ybf[:, kc, lo:lo + TTW], pbc[:, 0:512], op=AluOpType.mult),
                            reads=[ybf.name, pbc.name], writes=[t1.name])
                        ctx.op("v", lambda E, kc=kc: E.tensor_tensor(
                            t1[:], t1[:], pbc[:, 512:1024], op=AluOpType.add),
                            reads=[t1.name, pbc.name], writes=[t1.name])
                        ctx.op("v", lambda E, kc=kc: E.tensor_scalar(
                            xnb[:, kc, :], t1[:], gc[:, kc:kc + 1], bc[:, kc:kc + 1],
                            op0=AluOpType.mult, op1=AluOpType.add),
                            reads=[t1.name, gc.name, bc.name], writes=[xnb.name])
                    for j in range(NJ):
                        sz = min(128, DI - 128 * j)
                        for kc in range(NKC):
                            ctx.op("t", lambda E, j=j, kc=kc, sz=sz: E.matmul(
                                pa[0:sz, :], w1[:, kc, 128 * j:128 * j + sz], xnb[:, kc, :],
                                start=(kc == 0), stop=(kc == NKC - 1), skip_group_check=True),
                                reads=[w1.name, xnb.name], writes=[pa.name])
                            ctx.op("t", lambda E, j=j, kc=kc, sz=sz: E.matmul(
                                pg[0:sz, :], w1[:, kc, DI + 128 * j:DI + 128 * j + sz], xnb[:, kc, :],
                                start=(kc == 0), stop=(kc == NKC - 1), skip_group_check=True),
                                reads=[w1.name, xnb.name], writes=[pg.name])
                        ctx.op("s", lambda E, j=j, sz=sz: E.activation(
                            gel[0:sz, :], pg[0:sz, :], AF.Gelu, bias=b1c[0:sz, 2 * j + 1:2 * j + 2]),
                            reads=[pg.name, b1c.name], writes=[gel.name])
                        ctx.op("s", lambda E, j=j, sz=sz: E.activation(
                            ab[0:sz, :], pa[0:sz, :], AF.Identity, bias=b1c[0:sz, 2 * j:2 * j + 1]),
                            reads=[pa.name, b1c.name], writes=[ab.name])
                        ctx.op("v", lambda E, j=j, sz=sz: E.tensor_tensor(
                            glu[0:sz, j, :], ab[0:sz, :], gel[0:sz, :], op=AluOpType.mult),
                            reads=[ab.name, gel.name], writes=[glu.name])
                    for oc in range(NKC):
                        for j in range(NJ):
                            sz = min(128, DI - 128 * j)
                            ctx.op("t", lambda E, oc=oc, j=j, sz=sz: E.matmul(
                                pa[:], w2[0:sz, j, oc * 128:oc * 128 + 128], glu[0:sz, j, :],
                                start=(j == 0), stop=(j == NJ - 1), skip_group_check=True),
                                reads=[w2.name, glu.name], writes=[pa.name])
                        ctx.op("v", lambda E, oc=oc: E.scalar_tensor_tensor(
                            yf32[:, oc, :], pa[:], b2c[:, oc:oc + 1], yf32[:, oc, :],
                            op0=AluOpType.add, op1=AluOpType.add),
                            reads=[pa.name, b2c.name, yf32.name], writes=[yf32.name])
                    ctx.dma("y", fm(dst, c0, TTW), yf32[:], reads=[yf32.name], writes=[dname])

        bufs = [(ya, "ya"), (yb, "yb")]
        cur = 0
        for l in range(L0DEPTH):
            src, sname = bufs[cur]
            dst, dname = bufs[1 - cur]
            phase(lambda l=l, src=src, dst=dst, sn=sname, dn=dname: ph_inner_attn(l, src, dst, sn, dn))
            cur = 1 - cur
            src, sname = bufs[cur]
            dst, dname = bufs[1 - cur]
            phase(lambda l=l, src=src, dst=dst, sn=sname, dn=dname: ph_inner_ffn(l, src, dst, sn, dn))
            cur = 1 - cur
        yfin, yfin_name = bufs[cur]

        NTO = TOWN // TTW
        mks = sb(big, [1, 1], F32)

        def ph_x1():
            with ExitStack() as stk:
                rwT = sb(stk, [128, NKC, NEXP], F32)
                ctx.dma("y", rwT[:], l1f_rwT.rearrange("(c p) e -> p c e", c=NKC), writes=[rwT.name])
                xi = sb(stk, [128, NKC, TTW], F32)
                a0t = sb(stk, [128, NKC, TTW], F32)
                yt = sb(stk, [128, NKC, TTW], F32)
                xb = sb(stk, [128, NKC, TTW], BF16)
                sq = sb(stk, [128, TTW], BF16)
                rowb = sb(stk, [1, 2 * TTW], F32)
                t1 = sb(stk, [128, TTW], F32)
                xnf = sb(stk, [128, NKC, TTW], F32)
                pbc = ps(stk, [128, 1024])
                xnb = sb(stk, [128, NKC, TTW], BF16)
                sgt = sb(stk, [128, 4, NEXP], F32)
                sgbt = sb(stk, [128, NEXP], F32)
                m1c = sb(stk, [128, 1], F32)
                mskt = sb(stk, [128, NEXP], F32)
                s2t = sb(stk, [128, NEXP], F32)
                gtt = sb(stk, [128, NEXP], F32)
                gout = sb(stk, [NEXP, TTW], F32)
                ebrep = sb(stk, [128, NEXP], F32)
                pum = ps(stk, [1, 512])
                pq2 = ps(stk, [1, 512])
                prt = ps(stk, [128, 512])
                pgt = ps(stk, [NEXP, 512])
                ctx.dma("y", ebrep[:],
                        l1f_eb.rearrange("e 1 -> 1 e").broadcast_to([128, NEXP]),
                        writes=[ebrep.name])
                for i in range(NTO):
                    c0 = i * TTW
                    if True:
                        ctx.dma("y", xi[:], fm(xin, HALO + c0, TTW), writes=[xi.name])
                        ctx.dma("y", a0t[:], fm(a0h, c0, TTW), reads=["a0h"], writes=[a0t.name])
                        ctx.dma("y", yt[:], fm(yfin, HALO + c0, TTW), reads=[yfin_name], writes=[yt.name])
                        for kc in range(NKC):
                            ctx.op("v", lambda E, kc=kc: E.tensor_tensor(
                                xi[:, kc, :], xi[:, kc, :], a0t[:, kc, :], op=AluOpType.add),
                                reads=[xi.name, a0t.name], writes=[xi.name])
                            ctx.op("v", lambda E, kc=kc: E.tensor_tensor(
                                xi[:, kc, :], xi[:, kc, :], yt[:, kc, :], op=AluOpType.add),
                                reads=[xi.name, yt.name], writes=[xi.name])
                            ctx.op("s", lambda E, kc=kc: E.activation(
                                xb[:, kc, :], xi[:, kc, :], AF.Copy),
                                reads=[xi.name], writes=[xb.name])
                        ctx.dma("y", fm(x1h, c0, TTW), xi[:], reads=[xi.name], writes=["x1h"])
                        for kc in range(NKC):
                            ctx.op("s", lambda E, kc=kc, xb=xb: E.activation(sq[:], xb[:, kc, :], AF.Square),
                                   reads=[xb.name], writes=[sq.name])
                            ctx.op("t", lambda E, kc=kc, xb=xb: E.matmul(
                                pum[:], onesb[:, 0:1], xb[:, kc, :],
                                start=(kc == 0), stop=(kc == NKC - 1), skip_group_check=True),
                                reads=[xb.name, "onesb"], writes=[pum.name])
                            ctx.op("t", lambda E, kc=kc: E.matmul(
                                pq2[:], onesb[:, 0:1], sq[:],
                                start=(kc == 0), stop=(kc == NKC - 1), skip_group_check=True),
                                reads=[sq.name, "onesb"], writes=[pq2.name])
                        mu = rowb[0:1, 0:TTW]
                        var = rowb[0:1, TTW:2 * TTW]
                        rn = rowb.name
                        ctx.op("v", lambda E: E.tensor_scalar(mu, pum[:], 1.0 / D, None, op0=AluOpType.mult),
                               reads=[pum.name], writes=[rn])
                        ctx.op("v", lambda E: E.scalar_tensor_tensor(
                            var, mu, 1.0, mu, op0=AluOpType.mult, op1=AluOpType.mult),
                            reads=[rn], writes=[rn])
                        ctx.op("v", lambda E: E.scalar_tensor_tensor(
                            var, pq2[:], 1.0 / D, var, op0=AluOpType.mult, op1=AluOpType.subtract),
                            reads=[pq2.name, rn], writes=[rn])
                        ctx.op("v", lambda E: E.tensor_scalar(var, var, 1e-5, None, op0=AluOpType.add),
                               reads=[rn], writes=[rn])
                        ctx.op("v", lambda E: E.reciprocal(var, var), reads=[rn], writes=[rn])
                        ctx.op("s", lambda E: E.activation(var, var, AF.Sqrt), reads=[rn], writes=[rn])
                        ctx.op("v", lambda E: E.scalar_tensor_tensor(
                            mu, mu, -1.0, var, op0=AluOpType.mult, op1=AluOpType.mult),
                            reads=[rn], writes=[rn])
                        ctx.op("t", lambda E: E.matmul(
                            pbc[:, 0:512], onesfr[0:1, 0:128], var,
                            start=True, stop=True, skip_group_check=True),
                            reads=["onesfr", rn], writes=[pbc.name])
                        ctx.op("t", lambda E: E.matmul(
                            pbc[:, 512:1024], onesfr[0:1, 0:128], mu,
                            start=True, stop=True, skip_group_check=True),
                            reads=["onesfr", rn], writes=[pbc.name])
                        for kc in range(NKC):
                            ctx.op("v", lambda E, kc=kc: E.tensor_tensor(
                                t1[:], xi[:, kc, :], pbc[:, 0:512], op=AluOpType.mult),
                                reads=[xi.name, pbc.name], writes=[t1.name])
                            ctx.op("v", lambda E, kc=kc: E.tensor_tensor(
                                xnf[:, kc, :], t1[:], pbc[:, 512:1024], op=AluOpType.add),
                                reads=[t1.name, pbc.name], writes=[xnf.name])
                            ctx.op("s", lambda E, kc=kc: E.activation(
                                xnb[:, kc, :], xnf[:, kc, :], AF.Copy),
                                reads=[xnf.name], writes=[xnb.name])
                        ctx.dma("y", fm(xn1h, c0, TTW), xnb[:], reads=[xnb.name], writes=["xn1h"])
                        for tc in range(4):
                            for kc in range(NKC):
                                ctx.op("t", lambda E, tc=tc, kc=kc: E.matmul(
                                    prt[:, 0:NEXP], xnf[:, kc, tc * 128:tc * 128 + 128],
                                    rwT[:, kc, :],
                                    start=(kc == 0), stop=(kc == NKC - 1), skip_group_check=True),
                                    reads=[rwT.name, xnf.name], writes=[prt.name])
                            ctx.op("s", lambda E, tc=tc: E.activation(
                                sgt[:, tc, :], prt[:, 0:NEXP], AF.Sigmoid),
                                reads=[prt.name], writes=[sgt.name])
                        for tc in range(4):
                            ctx.op("v", lambda E, tc=tc: E.tensor_tensor(
                                sgbt[:], sgt[:, tc, :], ebrep[:], op=AluOpType.add),
                                reads=[sgt.name, ebrep.name], writes=[sgbt.name])
                            ctx.op("v", lambda E: E.reduce_max(m1c[:], sgbt[:], axis=AX.X),
                                   reads=[sgbt.name], writes=[m1c.name])
                            ctx.op("v", lambda E: E.tensor_scalar(
                                mskt[:], sgbt[:], m1c[:], None, op0=AluOpType.is_ge),
                                reads=[sgbt.name, m1c.name], writes=[mskt.name])
                            ctx.op("v", lambda E: E.scalar_tensor_tensor(
                                s2t[:], mskt[:], -1e30, sgbt[:],
                                op0=AluOpType.mult, op1=AluOpType.add),
                                reads=[mskt.name, sgbt.name], writes=[s2t.name])
                            ctx.op("v", lambda E: E.reduce_max(m1c[:], s2t[:], axis=AX.X),
                                   reads=[s2t.name], writes=[m1c.name])
                            ctx.op("v", lambda E: E.tensor_scalar(
                                mskt[:], sgbt[:], m1c[:], None, op0=AluOpType.is_ge),
                                reads=[sgbt.name, m1c.name], writes=[mskt.name])
                            ctx.op("v", lambda E, tc=tc: E.tensor_tensor(
                                gtt[:], sgt[:, tc, :], mskt[:], op=AluOpType.mult),
                                reads=[sgt.name, mskt.name], writes=[gtt.name])
                            ctx.op("t", lambda E, tc=tc: E.transpose(
                                pgt[0:NEXP, 0:128], gtt[:], ident[:]),
                                reads=[gtt.name, "ident"], writes=[pgt.name])
                            ctx.op("s", lambda E, tc=tc: E.activation(
                                gout[:, tc * 128:tc * 128 + 128], pgt[0:NEXP, 0:128], AF.Copy),
                                reads=[pgt.name], writes=[gout.name])
                        ctx.dma("y", gatesd[:, c0:c0 + TTW], gout[:],
                                reads=[gout.name], writes=["gatesd"])
        phase(ph_x1)


        def ph_l1kv():
            with ExitStack() as stk:
                w1s = sb(stk, [128, NKC, D], BF16)
                w2s = sb(stk, [128, NKC, D], BF16)
                w3s = sb(stk, [128, NKC, D], BF16)
                b1c = sb(stk, [128, NKC], F32)
                b2c = sb(stk, [128, NKC], F32)
                b3r = sb(stk, [1, D], BF16)
                mkrun = sb(stk, [1, 1], F32)
                t11 = sb(stk, [1, 1], F32)
                nkrow = sb(stk, [1, TTW], F32)
                ctx.dma("g", w1s[:], l1a_w[1].rearrange("(c p) e -> p c e", c=NKC), writes=[w1s.name])
                ctx.dma("g", w2s[:], l1a_w[2].rearrange("(c p) e -> p c e", c=NKC), writes=[w2s.name])
                ctx.dma("g", w3s[:], l1a_w[3].rearrange("(c p) e -> p c e", c=NKC), writes=[w3s.name])
                ctx.dma("y", b1c[:], l1a_b[1], writes=[b1c.name])
                ctx.dma("y", b2c[:], l1a_b[2], writes=[b2c.name])
                ctx.dma("g", b3r[:], l1a_b3r, writes=[b3r.name])
                ctx.op("v", lambda E: E.memset(mkrun[:], NEG), writes=[mkrun.name])
                xnt_pr = [sb(stk, [128, NKC, TTW], BF16) for _ in range(2)]
                kfb = sb(stk, [128, TTW], BF16)
                sqk = sb(stk, [128, TTW], BF16)
                vtb = sb(stk, [128, 4, D], BF16)
                pk = ps(stk, [128, 512])
                pnk = ps(stk, [1, 512])
                for i in range(NTO):
                    c0 = i * TTW
                    if True:
                        xnt = xnt_pr[i % 2]
                        ctx.dma("y", xnt[:], fm(xn1h, c0, TTW), reads=["xn1h"], writes=[xnt.name])
                        for oc in range(NKC):
                            for kc in range(NKC):
                                ctx.op("t", lambda E, oc=oc, kc=kc, xnt=xnt: E.matmul(
                                    pk[:], w1s[:, kc, oc * 128:oc * 128 + 128], xnt[:, kc, :],
                                    start=(kc == 0), stop=(kc == NKC - 1), skip_group_check=True),
                                    reads=[w1s.name, xnt.name], writes=[pk.name])
                            ctx.op("s", lambda E, oc=oc: E.activation(
                                kfb[:], pk[:], AF.Identity, bias=b1c[:, oc:oc + 1]),
                                reads=[pk.name, b1c.name], writes=[kfb.name])
                            ctx.dma("y", qd[oc * 128:oc * 128 + 128, c0:c0 + TTW], kfb[:],
                                    reads=[kfb.name], writes=["qd"])
                        for oc in range(NKC):
                            for kc in range(NKC):
                                ctx.op("t", lambda E, oc=oc, kc=kc, xnt=xnt: E.matmul(
                                    pk[:], w2s[:, kc, oc * 128:oc * 128 + 128], xnt[:, kc, :],
                                    start=(kc == 0), stop=(kc == NKC - 1), skip_group_check=True),
                                    reads=[w2s.name, xnt.name], writes=[pk.name])
                            ctx.op("s", lambda E, oc=oc: E.activation(
                                kfb[:], pk[:], AF.Identity, bias=b2c[:, oc:oc + 1]),
                                reads=[pk.name, b2c.name], writes=[kfb.name])
                            ctx.dma("y", agk_in[oc * 128:oc * 128 + 128, c0:c0 + TTW], kfb[:],
                                    reads=[kfb.name], writes=["agk_in"])
                            ctx.op("s", lambda E, oc=oc: E.activation(sqk[:], kfb[:], AF.Square),
                                   reads=[kfb.name], writes=[sqk.name])
                            ctx.op("t", lambda E, oc=oc: E.matmul(
                                pnk[:], onesb[:, 0:1], sqk[:],
                                start=(oc == 0), stop=(oc == NKC - 1), skip_group_check=True),
                                reads=[sqk.name, "onesb"], writes=[pnk.name])
                        ctx.op("v", lambda E: E.tensor_copy(nkrow[:], pnk[:]),
                               reads=[pnk.name], writes=[nkrow.name])
                        ctx.op("v", lambda E: E.reduce_max(t11[:], nkrow[:], axis=AX.X),
                               reads=[nkrow.name], writes=[t11.name])
                        ctx.op("v", lambda E: E.tensor_tensor(
                            mkrun[:], mkrun[:], t11[:], op=AluOpType.max),
                            reads=[mkrun.name, t11.name], writes=[mkrun.name])
                        for tch in range(4):
                            for nt in range(2):
                                for kc in range(NKC):
                                    ctx.op("t", lambda E, tch=tch, nt=nt, kc=kc, xnt=xnt: E.matmul(
                                        pk[:], xnt[:, kc, tch * 128:tch * 128 + 128],
                                        w3s[:, kc, nt * 512:nt * 512 + 512],
                                        start=(kc == 0), stop=False, skip_group_check=True),
                                        reads=[w3s.name, xnt.name], writes=[pk.name])
                                ctx.op("t", lambda E, nt=nt: E.matmul(
                                    pk[:], onesb[0:1, 0:128], b3r[0:1, nt * 512:nt * 512 + 512],
                                    start=False, stop=True, skip_group_check=True),
                                    reads=["onesb", b3r.name], writes=[pk.name])
                                ctx.op("s", lambda E, tch=tch, nt=nt: E.activation(
                                    vtb[:, tch, nt * 512:nt * 512 + 512], pk[:], AF.Copy),
                                    reads=[pk.name], writes=[vtb.name])
                        for tch in range(4):
                            ctx.dma("y", agv_in[c0 + tch * 128:c0 + tch * 128 + 128, :],
                                    vtb[:, tch, :], reads=[vtb.name], writes=["agv_in"])
                ctx.op("s", lambda E: E.activation(mkrun[:], mkrun[:], AF.Sqrt),
                       reads=[mkrun.name], writes=[mkrun.name])
                ctx.dma("y", mk_in, mkrun[:], reads=[mkrun.name], writes=["mk_in"])
        phase(ph_l1kv)

        def ph_coll():
            ctx.op("g", lambda E: E.collective_compute(
                "AllGather", mybir.AluOpType.bypass, replica_groups=rg,
                ins=[agk_in], outs=[agk_out]),
                reads=["agk_in"], writes=["agk_out"], coll=True)
            ctx.op("g", lambda E: E.collective_compute(
                "AllGather", mybir.AluOpType.bypass, replica_groups=rg,
                ins=[agv_in], outs=[agv_out]),
                reads=["agv_in"], writes=["agv_out"], coll=True)
            ctx.op("g", lambda E: E.collective_compute(
                "AllReduce", mybir.AluOpType.max, replica_groups=rg,
                ins=[mk_in], outs=[mk_out]),
                reads=["mk_in"], writes=["mk_out"], coll=True)
        phase(ph_coll)

        def ph_ub():
            with ExitStack() as stk:
                sqq = sb(stk, [128, TTW], BF16)
                qt = sb(stk, [128, NKC, TTW], BF16)
                nq = sb(stk, [1, TTW], F32)
                augsb = sb(stk, [1, TTW], BF16)
                c960 = sb(stk, [1, TTW], F32)
                pnq = ps(stk, [1, 512])
                ctx.dma("y", mks[:], mk_out, reads=["mk_out"], writes=["mks"])
                ctx.op("v", lambda E: E.memset(c960[:], 32.0 * UB_SHIFT), writes=[c960.name])
                for ct in range(NTO):
                    c0 = ct * TTW
                    ctx.dma("y", qt[:], fm(qd, c0, TTW), reads=["qd"], writes=[qt.name])
                    for oc in range(NKC):
                        ctx.op("s", lambda E, oc=oc: E.activation(
                            sqq[:], qt[:, oc, :], AF.Square),
                            reads=[qt.name], writes=[sqq.name])
                        ctx.op("t", lambda E, oc=oc: E.matmul(
                            pnq[:], onesb[:, 0:1], sqq[:],
                            start=(oc == 0), stop=(oc == NKC - 1), skip_group_check=True),
                            reads=[sqq.name, "onesb"], writes=[pnq.name])
                    ctx.op("s", lambda E: E.activation(nq[:], pnq[:], AF.Sqrt),
                           reads=[pnq.name], writes=[nq.name])
                    ctx.op("v", lambda E: E.scalar_tensor_tensor(
                        nq[:], nq[:], mks[0:1, 0:1], c960[:],
                        op0=AluOpType.mult, op1=AluOpType.subtract),
                        reads=[nq.name, "mks", c960.name], writes=[nq.name])
                    ctx.op("s", lambda E, c0=c0: E.activation(
                        augsb[:], nq[:], AF.Identity, scale=-1.0),
                        reads=[nq.name], writes=[augsb.name])
                    ctx.dma("y", augd[0:1, c0:c0 + TTW], augsb[:],
                            reads=[augsb.name], writes=["augd"])
        phase(ph_ub)

        def ph_l1attn():
            QH = TOWN // 2
            NQB = QH // 256
            NKG = S // 1024
            with ExitStack() as stk:
                w4s = sb(stk, [128, NKC, D], BF16)
                b4c = sb(stk, [128, NKC], F32)
                ctx.dma("g", w4s[:], l1a_w[4].rearrange("(c p) e -> p c e", c=NKC), writes=[w4s.name])
                ctx.dma("y", b4c[:], l1a_b[4], writes=[b4c.name])
                acc = sb(stk, [128, NKC, QH], F32)
                lacc = sb(stk, [1, QH], F32)
                lrb = sb(stk, [128, QH], BF16)
                att1 = sb(stk, [128, NKC, TTW], BF16)
                qbuf_pr = [sb(stk, [128, NKC, 256], BF16) for _ in range(2)]
                ptq = sb(stk, [128, 8, 256], BF16)
                kgt_pr = [sb(stk, [128, NKC, 1024], BF16) for _ in range(2)]
                vgt_pr = [sb(stk, [128, NKC, 1024], BF16) for _ in range(2)]
                x1t = sb(stk, [128, NKC, TTW], F32)
                augh = sb(stk, [1, QH], BF16)
                pstA = ps(stk, [128, 512])
                pstB = ps(stk, [128, 512])
                pav = ps(stk, [128, NKC, 256])
                plq = ps(stk, [1, 512])
                x2sb = x1t
                pst2 = (pstA, pstB)
                for qh in range(2):
                    q0 = qh * QH
                    if True:
                        ctx.op("v", lambda E: E.memset(acc[:], 0.0), writes=[acc.name])
                        ctx.dma("y", augh[:], augd[0:1, q0:q0 + QH],
                                reads=["augd"], writes=[augh.name])
                        ctx.op("v", lambda E: E.memset(lacc[:], 0.0), writes=[lacc.name])
                        for kg in range(NKG):
                            rblk = (1024 * kg) // TOWN
                            coff = 1024 * kg - TOWN * rblk
                            kgt, vgt = kgt_pr[kg % 2], vgt_pr[kg % 2]
                            ctx.dma("g", kgt[:],
                                    agk_out[D * rblk:D * (rblk + 1), coff:coff + 1024]
                                    .rearrange("(c p) t -> p c t", c=NKC),
                                    reads=["agk_out"], writes=[kgt.name])
                            ctx.dma("g", vgt[:],
                                    agv_out[1024 * kg:1024 * kg + 1024, :]
                                    .rearrange("(c p) e -> p c e", c=NKC),
                                    reads=["agv_out"], writes=[vgt.name])
                            for qb in range(NQB):
                                qc0 = q0 + qb * 256
                                qbuf = qbuf_pr[qb % 2]
                                ctx.dma("y", qbuf[:], fm(qd, qc0, 256),
                                        reads=["qd"], writes=[qbuf.name])
                                for kt in range(8):
                                    pS = pst2[kt % 2]
                                    for ec in range(NKC):
                                        ctx.op("t", lambda E, kt=kt, ec=ec, pS=pS, kgt=kgt, qbuf=qbuf: E.matmul(
                                            pS[:, 0:256], kgt[:, ec, kt * 128:kt * 128 + 128],
                                            qbuf[:, ec, :],
                                            start=(ec == 0), stop=False, skip_group_check=True),
                                            reads=[kgt.name, qbuf.name], writes=[pS.name])
                                    ctx.op("t", lambda E, qc0=qc0, q0=q0, pS=pS: E.matmul(
                                        pS[:, 0:256], onesb[0:1, 0:128],
                                        augh[0:1, qc0 - q0:qc0 - q0 + 256],
                                        start=False, stop=True, skip_group_check=True),
                                        reads=["onesb", augh.name], writes=[pS.name])
                                    ctx.op("s", lambda E, kt=kt, pS=pS: E.activation(
                                        ptq[:, kt, :], pS[:, 0:256], AF.Exp, scale=L1_SCALE),
                                        reads=[pS.name], writes=[ptq.name + str(kt)])
                                    ctx.op("t", lambda E, kt=kt: E.matmul(
                                        plq[0:1, 0:256], onesb[:, 0:1], ptq[:, kt, :],
                                        start=(kt == 0), stop=(kt == 7), skip_group_check=True),
                                        reads=["onesb", ptq.name + str(kt)], writes=[plq.name])
                                for dc in range(NKC):
                                    for kt in range(8):
                                        ctx.op("t", lambda E, kt=kt, dc=dc, vgt=vgt: E.matmul(
                                            pav[:, dc, :], vgt[:, kt, dc * 128:dc * 128 + 128],
                                            ptq[:, kt, :], start=(kt == 0), stop=(kt == 7), skip_group_check=True),
                                            reads=[vgt.name, ptq.name + str(kt)], writes=[pav.name])
                                    ctx.op("v", lambda E, dc=dc, qb=qb: E.scalar_tensor_tensor(
                                        acc[:, dc, qb * 256:qb * 256 + 256], pav[:, dc, :], 1.0,
                                        acc[:, dc, qb * 256:qb * 256 + 256],
                                        op0=AluOpType.mult, op1=AluOpType.add),
                                        reads=[pav.name, acc.name], writes=[acc.name])
                                qq = qb * 256
                                ctx.op("v", lambda E, qq=qq: E.scalar_tensor_tensor(
                                    lacc[0:1, qq:qq + 256], plq[0:1, 0:256], 1.0, lacc[0:1, qq:qq + 256],
                                    op0=AluOpType.mult, op1=AluOpType.add),
                                    reads=[plq.name, lacc.name], writes=[lacc.name])
                        ctx.op("v", lambda E: E.reciprocal(lacc[:], lacc[:]),
                               reads=[lacc.name], writes=[lacc.name])
                        ctx.dma("y", lscr2[0:QH].rearrange("(a b) -> a b", a=1), lacc[:],
                                reads=[lacc.name], writes=["lscr2"])
                        ctx.dma("g", lrb[:],
                                lscr2[0:QH].rearrange("(a b) -> a b", a=1)
                                .broadcast_to([128, QH]),
                                reads=["lscr2"], writes=[lrb.name])
                        for ct in range(QH // TTW):
                            c0 = q0 + ct * TTW
                            for dc in range(NKC):
                                ctx.op("v", lambda E, dc=dc, ct=ct: E.tensor_tensor(
                                    att1[:, dc, :], acc[:, dc, ct * TTW:ct * TTW + TTW],
                                    lrb[:, ct * TTW:ct * TTW + TTW], op=AluOpType.mult),
                                    reads=[acc.name, lrb.name], writes=[att1.name])
                            ctx.dma("y", x1t[:], fm(x1h, c0, TTW), reads=["x1h"], writes=[x1t.name])
                            for oc in range(NKC):
                                for ic in range(NKC):
                                    ctx.op("t", lambda E, oc=oc, ic=ic: E.matmul(
                                        pst2[0][:, 0:512],
                                        w4s[:, ic, oc * 128:oc * 128 + 128],
                                        att1[:, ic, :],
                                        start=(ic == 0), stop=(ic == NKC - 1), skip_group_check=True),
                                        reads=[w4s.name, att1.name], writes=[pstA.name])
                                ctx.op("v", lambda E, oc=oc: E.scalar_tensor_tensor(
                                    x2sb[:, oc, :], pst2[0][:, 0:512], b4c[:, oc:oc + 1],
                                    x1t[:, oc, :], op0=AluOpType.add, op1=AluOpType.add),
                                    reads=[pstA.name, b4c.name, x1t.name], writes=[x1t.name])
                            for oc in range(NKC):
                                ctx.dma("y", x2ph[oc * 128:oc * 128 + 128, c0:c0 + TTW],
                                        x2sb[:, oc, :], reads=[x1t.name], writes=[f"x2ph{oc}"])
        phase(ph_l1attn)

        def ph_moe():
            NTG = 2
            TG = TOWN // NTG
            NTT = TG // TTW
            parts = [("sq", q) for q in range(SHID // RH)] + [("e", e) for e in range(NEXP)]
            with ExitStack() as stk:
                pht = ps(stk, [128, NTT, 512])
                wsA = sb(stk, [128, NKC, 2 * RH], BF16)
                w3p = sb(stk, [128, NKC, D], BF16)
                xng_pr = [sb(stk, [128, NKC, TG], BF16) for _ in range(2)]
                h1s = sb(stk, [128, NTT, 512], BF16)
                glu8 = sb(stk, [128, NKC, NTT, 512], BF16)
                osb = sb(stk, [128, 512], F32)
                gbuf = sb(stk, [128, NTT, 512], F32)
                for (pk_, pidx) in parts:
                    if True:
                        if pk_ == "sq":
                            wa = wsA[:, :, 0:RH]
                            wb = wsA[:, :, RH:2 * RH]
                            ctx.dma("g", wa, l1f_sw1T[:, RH * pidx:RH * (pidx + 1)]
                                    .rearrange("(c p) e -> p c e", c=NKC), writes=[wsA.name])
                            ctx.dma("g", wb, l1f_sw2T[:, RH * pidx:RH * (pidx + 1)]
                                    .rearrange("(c p) e -> p c e", c=NKC), writes=[wsA.name])
                            ctx.dma("g", w3p[:], l1f_sw3T[RH * pidx:RH * (pidx + 1), :]
                                    .rearrange("(c p) e -> p c e", c=NKC), writes=[w3p.name])
                        else:
                            wa = wsA[:]
                            wb = None
                            ctx.dma("g", wsA[:], l1f_w12T[pidx]
                                    .rearrange("(c p) e -> p c e", c=NKC), writes=[wsA.name])
                            ctx.dma("g", w3p[:], l1f_w3T[pidx]
                                    .rearrange("(c p) e -> p c e", c=NKC), writes=[w3p.name])
                        for tg in range(NTG):
                            if True:
                                xng = xng_pr[tg % 2]
                                ctx.dma("y", xng[:], fm(xn1h, tg * TG, TG),
                                        reads=["xn1h"], writes=[xng.name])
                                if pk_ == "e":
                                    for t in range(NTT):
                                        ctx.dma("y", gbuf[:, t, :],
                                                gatesd[pidx:pidx + 1,
                                                       tg * TG + t * 512:tg * TG + t * 512 + 512]
                                                .broadcast_to([128, 512]),
                                                reads=["gatesd"], writes=[gbuf.name])
                                for oc in range(NKC):
                                    for kc in range(NKC):
                                        for t in range(NTT):
                                            ctx.op("t", lambda E, oc=oc, kc=kc, t=t, wa=wa, xng=xng: E.matmul(
                                                pht[:, t, :], wa[:, kc, oc * 128:oc * 128 + 128],
                                                xng[:, kc, t * 512:t * 512 + 512],
                                                start=(kc == 0), stop=(kc == NKC - 1),
                                                skip_group_check=True),
                                                reads=[wsA.name, xng.name], writes=[pht.name])
                                    for t in range(NTT):
                                        ctx.op("s", lambda E, t=t: E.activation(
                                            h1s[:, t, :], pht[:, t, :], AF.Silu),
                                            reads=[pht.name], writes=[h1s.name])
                                    woff = RH if pk_ == "e" else 0
                                    wsec = wa if pk_ == "e" else wb
                                    for kc in range(NKC):
                                        for t in range(NTT):
                                            ctx.op("t", lambda E, oc=oc, kc=kc, t=t, wsec=wsec, woff=woff, xng=xng: E.matmul(
                                                pht[:, t, :], wsec[:, kc, woff + oc * 128:woff + oc * 128 + 128],
                                                xng[:, kc, t * 512:t * 512 + 512],
                                                start=(kc == 0), stop=(kc == NKC - 1),
                                                skip_group_check=True),
                                                reads=[wsA.name, xng.name], writes=[pht.name])
                                    for t in range(NTT):
                                        ctx.op("v", lambda E, oc=oc, t=t: E.scalar_tensor_tensor(
                                            glu8[:, oc, t, :], pht[:, t, :], 1.0, h1s[:, t, :],
                                            op0=AluOpType.mult, op1=AluOpType.mult),
                                            reads=[pht.name, h1s.name], writes=[glu8.name])
                                        if pk_ == "e":
                                            ctx.op("v", lambda E, oc=oc, t=t: E.tensor_tensor(
                                                glu8[:, oc, t, :], glu8[:, oc, t, :], gbuf[:, t, :],
                                                op=AluOpType.mult),
                                                reads=[glu8.name, gbuf.name], writes=[glu8.name])
                                for oc2 in range(NKC):
                                    for hcc in range(NKC):
                                        for t in range(NTT):
                                            ctx.op("t", lambda E, oc2=oc2, hcc=hcc, t=t: E.matmul(
                                                pht[:, t, :], w3p[:, hcc, oc2 * 128:oc2 * 128 + 128],
                                                glu8[:, hcc, t, :],
                                                start=(hcc == 0), stop=(hcc == NKC - 1),
                                                skip_group_check=True),
                                                reads=[w3p.name, glu8.name], writes=[pht.name])
                                    for t in range(NTT):
                                        ctx.op("v", lambda E, t=t: E.tensor_copy(
                                            osb[:], pht[:, t, :]),
                                            reads=[pht.name], writes=[osb.name])
                                        c0 = tg * TG + t * 512
                                        ctx.dma("g", x2ph[oc2 * 128:oc2 * 128 + 128, c0:c0 + TTW],
                                                osb[:], reads=[osb.name], writes=[f"x2ph{oc2}"],
                                                accum_op=AluOpType.add)
        phase(ph_moe)

        def ph_out():
            with ExitStack() as stk:
                ot = sb(stk, [128, NKC, TTW], F32)
                for i in range(NTO):
                    c0 = i * TTW
                    ctx.dma("y", ot[:], fm(x2ph, c0, TTW),
                            reads=[f"x2ph{oc}" for oc in range(NKC)], writes=[ot.name])
                    ctx.dma("y", fm(out, c0, TTW), ot[:], reads=[ot.name], writes=["out"])
            t = ctx.cnt["y"]
            ctx.pend["y"].append([None, [("y", t)], False, 1, False, "y"])
            ctx.marks["y"].append((len(ctx.pend["y"]) - 1, (), ()))
            ctx.hwm[("y", "y")] = t
        phase(ph_out)

    return nc


def kernel(**inputs):
    nc = build_program()
    in_maps = [host_inputs(c, inputs) for c in range(NCORES)]
    res = run_bass_kernel_spmd(nc, in_maps, core_ids=list(range(NCORES)))
    outp = np.zeros((B, S, D), np.float32)
    for c in range(NCORES):
        b, r = c // 4, c % 4
        outp[b, r * TOWN:(r + 1) * TOWN, :] = np.asarray(res.results[c]["out"]).T
    return outp


# version 08
# speedup vs baseline: 1.1251x; 1.1100x over previous
import numpy as np
import math
from contextlib import ExitStack

import concourse.bass as bass
import concourse.mybir as mybir
from concourse.bass_utils import run_bass_kernel_spmd
from concourse.alu_op_type import AluOpType
from concourse import library_config

F32 = mybir.dt.float32
BF16 = mybir.dt.bfloat16
AF = mybir.ActivationFunctionType
AX = mybir.AxisListType

NCORES = 8
B = 2
S = 16384
D = 1024
NKC = D // 128

L0DEPTH = 6
HEADS = 16
DH = 64
WIN = 256
DI = 2730
W1OUT = 2 * DI
QK_SCALE = 8.0

NEXP = 8
RH = 1024
SHID = 4096
L1_SCALE = 1.0 / 32.0
UB_SHIFT = 30.0

NEG = -3.0e38

TOWN = 4096
HALO = 2048
T = TOWN + HALO
ENG_ATTR = {"t": "tensor", "v": "vector", "s": "scalar", "g": "gpsimd", "y": "sync"}


def set_config(town=None, halo=None, seq=None, l0depth=None):
    global TOWN, HALO, T, S, L0DEPTH
    if town is not None:
        TOWN = town
    if halo is not None:
        HALO = halo
    if seq is not None:
        S = seq
    if l0depth is not None:
        L0DEPTH = l0depth
    T = TOWN + HALO


class Ctx:
    ENGS = ("t", "v", "s", "g", "y")

    def __init__(self, nc, stack):
        self.nc = nc
        self.cnt = {e: 0 for e in self.ENGS}
        self.pend = {e: [] for e in self.ENGS}
        self.marks = {e: [] for e in self.ENGS}
        self.hwm = {}
        self.writer = {}
        self.readers = {}
        self.undrained = {e: set() for e in self.ENGS}
        self.unread = {e: set() for e in self.ENGS}
        self.sem = {e: stack.enter_context(nc.semaphore(f"sem_{e}"))
                    for e in self.ENGS + ("gd",)}
        self.cnt["gd"] = 0
        self.need = {"y": set(), "gd": set()}
        self.flushed = {"y": 0, "gd": 0}

    def _ticket(self, eng):
        if eng == "gd":
            return self.cnt["gd"]
        if self.pend[eng] and self.marks[eng]:
            last = None
            for rec in reversed(self.pend[eng]):
                if rec[0] is not None:
                    last = rec
                    break
            if last is not None and not last[4]:
                last[4] = True
                self.cnt[eng] += last[3]
            tick = self.cnt[eng]
            for _, ws, rs in self.marks[eng]:
                for b in ws:
                    w = self.writer.get(b)
                    if w is not None and w[0] == eng and w[1] is None:
                        self.writer[b] = (eng, tick, w[2])
                for b in rs:
                    r = self.readers.get(b, {}).get(eng)
                    if r is not None and r[0] is None:
                        self.readers[b][eng] = (tick, r[1])
            self.marks[eng] = []
        return self.cnt[eng]

    def op(self, eng, fn, reads=(), writes=(), dma=False, coll=False):
        waits = []
        drain = False

        def tick_of(psk, t):
            v = t if t is not None else self._ticket(psk)
            if psk in ("y", "gd") and v > self.flushed[psk]:
                self.need.setdefault(psk, set()).add(v)
            return v

        own = lambda sk: sk[0] == eng
        for b in reads:
            w = self.writer.get(b)
            if w is None:
                continue
            wsk, wtick, wdma = w
            if own(wsk) and not wdma:
                if b in self.undrained[eng]:
                    drain = True
            else:
                waits.append((wsk, tick_of(wsk, wtick)))
        for b in writes:
            for rsk, (rtick, rdma) in list(self.readers.get(b, {}).items()):
                if not own(rsk) or rdma:
                    waits.append((rsk, tick_of(rsk, rtick)))
                elif b in self.unread[eng]:
                    drain = True
            w = self.writer.get(b)
            if w is not None:
                wsk, wtick, wdma = w
                if not own(wsk) or wdma:
                    waits.append((wsk, tick_of(wsk, wtick)))
        fwaits = []
        for peng, tick in waits:
            if peng in ("y", "gd") and tick <= self.flushed[peng]:
                continue
            key = (eng, peng)
            if self.hwm.get(key, 0) < tick:
                self.hwm[key] = tick
                fwaits.append((peng, tick))
        if drain:
            self.undrained[eng].clear()
            self.unread[eng].clear()
        quant = 16 if dma else 1
        sealed = dma or coll
        semkey = "gd" if (dma and eng == "g") else eng
        self.pend[eng].append([fn, fwaits, drain, quant, sealed, semkey])
        if sealed:
            self.cnt[semkey] += quant
            tick = self.cnt[semkey]
        else:
            tick = None
        for b in writes:
            self.writer[b] = (semkey, tick, dma)
            self.readers.pop(b, None)
            if not dma:
                self.undrained[eng].add(b)
        for b in reads:
            self.readers.setdefault(b, {})[semkey] = (tick, dma)
            if not dma:
                self.unread[eng].add(b)
        if not sealed:
            self.marks[eng].append((len(self.pend[eng]) - 1, tuple(writes), tuple(reads)))

    def dma(self, q, out, in_, reads=(), writes=(), **kw):
        self.op(q, lambda E: E.dma_start(out=out, in_=in_, **kw),
                reads=reads, writes=writes, dma=True)

    def barrier_wait_all(self, eng):
        for p in self.ENGS + ("gd",):
            if p == eng:
                continue
            t = self._ticket(p)
            key = (eng, p)
            if t and self.hwm.get(key, 0) < t:
                self.hwm[key] = t
                self.pend[eng].append([None, [(p, t)], False, 1, False, eng])
                self.marks[eng].append((len(self.pend[eng]) - 1, (), ()))

    def flush(self, block):
        for e in self.ENGS:
            self._ticket(e)
        sem = self.sem
        post = {}
        for e in ("y", "g"):
            tot = dict(self.flushed)
            for idx, rec in enumerate(self.pend[e]):
                sk = rec[5]
                if sk in ("y", "gd") and rec[4] and rec[0] is not None:
                    tot[sk] = tot.get(sk, 0) + rec[3]
                    if tot[sk] in self.need.get(sk, ()):
                        post[(e, idx)] = (sk, tot[sk])
            for sk in ("y", "gd"):
                if sk in tot:
                    self.flushed[sk] = tot[sk]
        for e in self.ENGS:
            ops = self.pend[e]
            if not ops:
                continue
            def run(engobj, ops=ops, e=e, post=post):
                for idx, (fn, fwaits, drain, quant, inc, semkey) in enumerate(ops):
                    for psk, tick in fwaits:
                        engobj.wait_ge(sem[psk], tick)
                    if drain:
                        engobj.drain()
                    if fn is None:
                        if inc:
                            engobj.nop().then_inc(sem[semkey], quant)
                    else:
                        inst = fn(engobj)
                        if inc:
                            inst.then_inc(sem[semkey], quant)
                    pw = post.get((e, idx))
                    if pw is not None:
                        engobj.wait_ge(sem[pw[0]], pw[1])
            getattr(block, ENG_ATTR[e])(run)
            self.pend[e] = []
            self.undrained[e].clear()
            self.unread[e].clear()


def cdiv(a, b):
    return (a + b - 1) // b


def host_inputs(core, full):
    b, r = core // 4, core % 4
    f32 = np.float32
    d = {}
    x = np.asarray(full["L0_attn_x"])
    start = r * TOWN
    lo = start - HALO
    sl = np.zeros((T, D), f32)
    src_lo = max(lo, 0)
    sl[src_lo - lo:, :] = x[b, src_lo:start + TOWN, :]
    d["xin"] = np.ascontiguousarray(sl.T)

    pos = np.clip(np.arange(lo, start + TOWN), 0, S - 1).astype(np.float64)
    half = np.arange(0, DH, 2, dtype=np.float64)
    inv_freq = 1.0 / 10000.0 ** (half / DH)
    freqs = pos[:, None] * inv_freq[None, :]
    base_scale = (half + 0.4 * DH) / (1.4 * DH)
    power = (pos - S // 2) / (WIN // 2)
    scale = base_scale[None, :] ** power[:, None]
    cos, sin = np.cos(freqs), np.sin(freqs)
    def build(scl):
        A32 = (cos * scl).T
        B32 = (sin * scl).T
        A64 = np.concatenate([A32, A32], 0)
        D64 = np.concatenate([B32, -B32], 0)
        A = np.concatenate([A64, A64], 0).astype(f32)
        Dt = np.concatenate([D64, D64], 0).astype(f32)
        return np.ascontiguousarray(A), np.ascontiguousarray(Dt)
    d["tAq"], d["tDq"] = build(scale)
    d["tAk"], d["tDk"] = build(QK_SCALE / scale)

    for nm, w in (("wq", "L0_attn_Wq"), ("wk", "L0_attn_Wk"), ("wv", "L0_attn_Wv")):
        d[f"l0a_{nm}T"] = np.ascontiguousarray(np.asarray(full[w]).T)
    d["l0a_bq"] = np.ascontiguousarray(np.asarray(full["L0_attn_bq"]).astype(f32).reshape(NKC, 128).T)
    d["l0a_bk"] = np.ascontiguousarray(np.asarray(full["L0_attn_bk"]).astype(f32).reshape(NKC, 128).T)
    d["l0a_bv"] = np.asarray(full["L0_attn_bv"]).reshape(1, D).astype(f32)

    d["l0f_wqkv"] = np.ascontiguousarray(np.asarray(full["L0_ffn_Wqkv"]))
    d["l0f_wo"] = np.ascontiguousarray(np.asarray(full["L0_ffn_Wo"]))
    d["l0f_w1"] = np.ascontiguousarray(np.asarray(full["L0_ffn_W1"]))
    d["l0f_w2"] = np.ascontiguousarray(np.asarray(full["L0_ffn_W2"]))
    pc = lambda a: np.ascontiguousarray(np.asarray(a).astype(f32).reshape(-1, NKC, 128).transpose(0, 2, 1))
    d["l0f_g"] = pc(full["L0_ffn_ln_g"])
    d["l0f_b"] = pc(full["L0_ffn_ln_b"])
    d["l0f_b2"] = pc(full["L0_ffn_b2"])
    b1 = np.asarray(full["L0_ffn_b1"]).astype(f32)
    NJ = cdiv(DI, 128)
    b1c = np.zeros((L0DEPTH, 128, 2 * NJ), f32)
    for j in range(NJ):
        sz = min(128, DI - j * 128)
        b1c[:, :sz, 2 * j] = b1[:, j * 128:j * 128 + sz]
        b1c[:, :sz, 2 * j + 1] = b1[:, DI + j * 128:DI + j * 128 + sz]
    d["l0f_b1c"] = b1c
    def selput(sc):
        out = np.zeros((L0DEPTH, 16, D), f32)
        for h in range(16):
            out[:, h, DH * h:DH * (h + 1)] = sc
        return out
    d["l0f_selq"] = selput(np.asarray(full["L0_ffn_q_scale"]).astype(f32))
    d["l0f_selk"] = selput(np.asarray(full["L0_ffn_k_scale"]).astype(f32))

    for i in (1, 2, 3, 4):
        d[f"l1a_w{i}"] = np.ascontiguousarray(np.asarray(full[f"L1_attn_W{i}"]))
        d[f"l1a_b{i}"] = np.ascontiguousarray(np.asarray(full[f"L1_attn_b{i}"]).astype(f32).reshape(NKC, 128).T)
    d["l1a_b3r"] = np.asarray(full["L1_attn_b3"]).reshape(1, D).astype(f32)

    d["l1f_w12T"] = np.ascontiguousarray(np.asarray(full["L1_ffn_w12"]).transpose(0, 2, 1))
    d["l1f_w3T"] = np.ascontiguousarray(np.asarray(full["L1_ffn_w3"]).transpose(0, 2, 1))
    d["l1f_sw1T"] = np.ascontiguousarray(np.asarray(full["L1_ffn_shared_w1"]).T)
    d["l1f_sw2T"] = np.ascontiguousarray(np.asarray(full["L1_ffn_shared_w2"]).T)
    d["l1f_sw3T"] = np.ascontiguousarray(np.asarray(full["L1_ffn_shared_w3"]).T)
    d["l1f_rwT"] = np.ascontiguousarray(np.asarray(full["L1_ffn_router_w"]).T.astype(f32))
    d["l1f_eb"] = np.asarray(full["L1_ffn_expert_bias"]).reshape(NEXP, 1).astype(f32)

    d["c_ident"] = np.eye(128, dtype=f32)
    sw = np.zeros((128, 128), f32)
    for i_ in range(128):
        sw[i_, (i_ // 64) * 64 + ((i_ % 64) + 32) % 64] = 1.0
    d["c_swap"] = sw
    d["c_ones"] = np.ones((128, 128), f32)
    ar = np.arange(128, dtype=f32)
    d["c_ms1"] = np.stack([ar, ar + 128.0], 1)
    d["c_me1"] = d["c_ms1"] + 257.0
    ms2 = np.zeros((128, 4), f32)
    me2 = np.zeros((128, 4), f32)
    for kc in range(4):
        kj = 128.0 * kc + ar
        ms2[:, kc] = np.maximum(0.0, kj - 256.0)
        me2[:, kc] = np.minimum(256.0, kj + 1.0)
    d["c_ms2"], d["c_me2"] = ms2, me2
    NEGM = np.float32(-1.0e30)
    mk1 = np.zeros((4, 128, 512), f32)
    for fw in range(2):
        for qc in range(2):
            qi = 128 * qc + np.arange(128)[:, None]
            kj = np.arange(512)[None, :]
            if fw:
                bad = (kj < 256) | (kj > 256 + qi)
            else:
                bad = (kj < qi) | (kj > qi + 256)
            mk1[2 * fw + qc][bad] = NEGM
    d["c_mk1"] = mk1
    mk2 = np.zeros((4, 128, 256), f32)
    for kt in range(4):
        kj = 128 * kt + np.arange(128)[:, None]
        qi = np.arange(256)[None, :]
        bad = (qi < kj - 256) | (qi > kj)
        mk2[kt][bad] = NEGM
    d["c_mk2"] = mk2
    bd = np.zeros((128, NKC, 16), f32)
    for c in range(NKC):
        bd[0:64, c, 2 * c] = 1.0
        bd[64:128, c, 2 * c + 1] = 1.0
    d["c_bdsel"] = bd.reshape(128, NKC * 16)
    return d


def np_f32(x):
    return np.ascontiguousarray(np.asarray(x, dtype=np.float32))


def build_program():
    nc = bass.Bass("TRN2", target_bir_lowering=False)
    TT = 512
    NT = T // TT
    NTO = TOWN // TT
    NJ = cdiv(DI, 128)
    NCH = TOWN // 1024

    dt = lambda n, s, d, k="ExternalInput": nc.dram_tensor(n, s, d, kind=k).ap()
    xin = dt("xin", [D, T], F32)
    tA = {"q": dt("tAq", [128, T], F32), "k": dt("tAk", [128, T], F32)}
    tD = {"q": dt("tDq", [128, T], F32), "k": dt("tDk", [128, T], F32)}
    l0a_wT = {n: dt(f"l0a_w{n}T", [D, D], F32) for n in "qkv"}
    l0a_bq = dt("l0a_bq", [128, NKC], F32)
    l0a_bk = dt("l0a_bk", [128, NKC], F32)
    l0a_bv = dt("l0a_bv", [1, D], F32)
    l0f_wqkv = dt("l0f_wqkv", [L0DEPTH, D, 3 * D], F32)
    l0f_wo = dt("l0f_wo", [L0DEPTH, D, D], F32)
    l0f_w1 = dt("l0f_w1", [L0DEPTH, D, W1OUT], F32)
    l0f_w2 = dt("l0f_w2", [L0DEPTH, DI, D], F32)
    l0f_g = dt("l0f_g", [L0DEPTH, 128, NKC], F32)
    l0f_b = dt("l0f_b", [L0DEPTH, 128, NKC], F32)
    l0f_b1c = dt("l0f_b1c", [L0DEPTH, 128, 2 * NJ], F32)
    l0f_b2 = dt("l0f_b2", [L0DEPTH, 128, NKC], F32)
    l0f_selq = dt("l0f_selq", [L0DEPTH, 16, D], F32)
    l0f_selk = dt("l0f_selk", [L0DEPTH, 16, D], F32)
    l1a_w = {i: dt(f"l1a_w{i}", [D, D], F32) for i in (1, 2, 3, 4)}
    l1a_b = {i: dt(f"l1a_b{i}", [128, NKC], F32) for i in (1, 2, 3, 4)}
    l1a_b3r = dt("l1a_b3r", [1, D], F32)
    l1f_w12T = dt("l1f_w12T", [NEXP, D, 2 * RH], F32)
    l1f_w3T = dt("l1f_w3T", [NEXP, RH, D], F32)
    l1f_sw1T = dt("l1f_sw1T", [D, SHID], F32)
    l1f_sw2T = dt("l1f_sw2T", [D, SHID], F32)
    l1f_sw3T = dt("l1f_sw3T", [SHID, D], F32)
    l1f_rwT = dt("l1f_rwT", [D, NEXP], F32)
    l1f_eb = dt("l1f_eb", [NEXP, 1], F32)
    c_ident = dt("c_ident", [128, 128], F32)
    c_swap = dt("c_swap", [128, 128], F32)
    c_ones = dt("c_ones", [128, 128], F32)
    c_ms1 = dt("c_ms1", [128, 2], F32)
    c_me1 = dt("c_me1", [128, 2], F32)
    c_ms2 = dt("c_ms2", [128, 4], F32)
    c_me2 = dt("c_me2", [128, 4], F32)
    c_bdsel = dt("c_bdsel", [128, NKC * 16], F32)
    c_mk1 = dt("c_mk1", [4, 128, 512], F32)
    c_mk2 = dt("c_mk2", [4, 128, 256], F32)
    out = dt("out", [D, TOWN], F32, k="ExternalOutput")
    ya = dt("ya", [D, T], F32, k="Internal")
    yb = dt("yb", [D, T], F32, k="Internal")
    a0h = dt("a0h", [D, TOWN], F32, k="Internal")
    x1h = dt("x1h", [D, TOWN], F32, k="Internal")
    x2ph = dt("x2ph", [D, TOWN], F32, k="Internal")
    xn1h = dt("xn1h", [D, TOWN], BF16, k="Internal")
    agk_in = dt("agk_in", [D, TOWN], BF16, k="Internal")
    agk_out = dt("agk_out", [4 * D, TOWN], BF16, k="Internal")
    agv_in = dt("agv_in", [TOWN, D], BF16, k="Internal")
    agv_out = dt("agv_out", [4 * TOWN, D], BF16, k="Internal")
    mk_in = dt("mk_in", [1, 1], F32, k="Internal")
    mscr = dt("mscr", [64 * 128], BF16, k="Internal")
    gatesd = dt("gatesd", [NEXP, TOWN], F32, k="Internal")
    lscr = dt("lscr", [32 * 256], BF16, k="Internal")
    lscr2 = dt("lscr2", [16 * 512], F32, k="Internal")
    augd = dt("augd", [1, TOWN], BF16, k="Internal")
    qd = dt("qd", [D, TOWN], BF16, k="Internal")
    mk_out = dt("mk_out", [1, 1], F32, k="Internal")

    def fm(dram, c0, w, nk=NKC):
        return dram[:, c0:c0 + w].rearrange("(c p) t -> p c t", c=nk)

    uid = [0]

    def sb(stk, shape, dtyp):
        uid[0] += 1
        return stk.enter_context(nc.sbuf_tensor(f"b{uid[0]}", shape, dtyp))

    def ps(stk, shape, dtyp=F32):
        uid[0] += 1
        return stk.enter_context(nc.psum_tensor(f"p{uid[0]}", shape, dtyp))

    with ExitStack() as big:
        ctx = Ctx(nc, big)
        rg = [[0, 1, 2, 3], [4, 5, 6, 7]]

        ident = sb(big, [128, 128], F32)
        identb = sb(big, [128, 128], BF16)
        swpb = sb(big, [128, 128], BF16)
        onesb = sb(big, [128, 128], BF16)
        onesfr = sb(big, [1, 128], F32)
        bdsel = sb(big, [128, NKC * 16], BF16)
        mk1 = sb(big, [128, 4, 512], BF16)
        mk2 = sb(big, [128, 4, 256], BF16)

        def phase(fn):
            fn()
            ctx.pend["y"].append([None, [("y", ctx.cnt["y"])], False, 1, False, "y"])
            ctx.pend["g"].append([None, [("gd", ctx.cnt["gd"])], False, 1, False, "g"])
            for e in ctx.ENGS:
                ctx.barrier_wait_all(e)
            with nc.Block() as blk:
                ctx.flush(blk)

        def ph_consts():
            ctx.dma("y", ident[:], c_ident, writes=["ident"])
            ctx.dma("g", identb[:], c_ident, writes=["identb"])
            ctx.dma("g", swpb[:], c_swap, writes=["swpb"])
            ctx.dma("g", onesb[:], c_ones, writes=["onesb"])
            ctx.dma("y", onesfr[:], c_ones[0:1, :], writes=["onesfr"])
            ctx.dma("g", bdsel[:], c_bdsel, writes=["bdsel"])
            ctx.dma("g", mk1[:], c_mk1.rearrange("v p x -> p v x"), writes=["mk1"])
            ctx.dma("g", mk2[:], c_mk2.rearrange("v p x -> p v x"), writes=["mk2"])
        phase(ph_consts)

        def ph_outer_ln():
            with ExitStack() as stk:
                ysb = sb(stk, [128, NKC, TT], F32)
                xb_pr = [sb(stk, [128, NKC, TT], BF16) for _ in range(2)]
                xf_pr = [sb(stk, [128, NKC, TT], F32) for _ in range(2)]
                sq = sb(stk, [128, TT], BF16)
                rowb = sb(stk, [1, 2 * TT], F32)
                pum = ps(stk, [1, TT])
                pq2 = ps(stk, [1, TT])
                pbc = ps(stk, [128, 1024])
                for i in range(NT):
                    c0 = i * TT
                    if True:
                        xb, xf = xb_pr[i % 2], xf_pr[i % 2]
                        ctx.dma("g", xb[:], fm(xin, c0, TT), writes=[xb.name])
                        ctx.dma("y", xf[:], fm(xin, c0, TT), writes=[xf.name])
                        for kc in range(NKC):
                            ctx.op("s", lambda E, kc=kc, xb=xb: E.activation(sq[:], xb[:, kc, :], AF.Square),
                                   reads=[xb.name], writes=[sq.name])
                            ctx.op("t", lambda E, kc=kc, xb=xb: E.matmul(
                                pum[:], onesb[:, 0:1], xb[:, kc, :],
                                start=(kc == 0), stop=(kc == NKC - 1), skip_group_check=True),
                                reads=[xb.name, "onesb"], writes=[pum.name])
                            ctx.op("t", lambda E, kc=kc: E.matmul(
                                pq2[:], onesb[:, 0:1], sq[:],
                                start=(kc == 0), stop=(kc == NKC - 1), skip_group_check=True),
                                reads=[sq.name, "onesb"], writes=[pq2.name])
                        mu = rowb[0:1, 0:TT]
                        var = rowb[0:1, TT:2 * TT]
                        rn = rowb.name
                        ctx.op("v", lambda E: E.tensor_scalar(mu, pum[:], 1.0 / D, None, op0=AluOpType.mult),
                               reads=[pum.name], writes=[rn])
                        ctx.op("v", lambda E: E.scalar_tensor_tensor(
                            var, mu, 1.0, mu, op0=AluOpType.mult, op1=AluOpType.mult),
                            reads=[rn], writes=[rn])
                        ctx.op("v", lambda E: E.scalar_tensor_tensor(
                            var, pq2[:], 1.0 / D, var, op0=AluOpType.mult, op1=AluOpType.subtract),
                            reads=[pq2.name, rn], writes=[rn])
                        ctx.op("v", lambda E: E.tensor_scalar(var, var, 1e-5, None, op0=AluOpType.add),
                               reads=[rn], writes=[rn])
                        ctx.op("v", lambda E: E.reciprocal(var, var), reads=[rn], writes=[rn])
                        ctx.op("s", lambda E: E.activation(var, var, AF.Sqrt),
                               reads=[rn], writes=[rn])
                        ctx.op("v", lambda E: E.scalar_tensor_tensor(
                            mu, mu, -1.0, var, op0=AluOpType.mult, op1=AluOpType.mult),
                            reads=[rn], writes=[rn])
                        ctx.op("t", lambda E: E.matmul(
                            pbc[:, 0:512], onesfr[0:1, 0:128], var,
                            start=True, stop=True, skip_group_check=True),
                            reads=["onesfr", rn], writes=[pbc.name])
                        ctx.op("t", lambda E: E.matmul(
                            pbc[:, 512:1024], onesfr[0:1, 0:128], mu,
                            start=True, stop=True, skip_group_check=True),
                            reads=["onesfr", rn], writes=[pbc.name])
                        for kc in range(NKC):
                            ctx.op("v", lambda E, kc=kc, xf=xf: E.tensor_tensor(
                                ysb[:, kc, :], xf[:, kc, :], pbc[:, 0:512], op=AluOpType.mult),
                                reads=[xf.name, pbc.name], writes=[ysb.name])
                            ctx.op("v", lambda E, kc=kc: E.tensor_tensor(
                                ysb[:, kc, :], ysb[:, kc, :], pbc[:, 512:1024], op=AluOpType.add),
                                reads=[ysb.name, pbc.name], writes=[ysb.name])
                        ctx.dma("y", fm(ya, c0, TT), ysb[:], reads=[ysb.name], writes=["ya"])
        phase(ph_outer_ln)



        def ph_l0_attn():
            with ExitStack() as stk:
                wq = sb(stk, [128, NKC, D], BF16)
                wk = sb(stk, [128, NKC, D], BF16)
                wv = sb(stk, [128, NKC, D], BF16)
                bqs = sb(stk, [128, NKC], F32)
                bks = sb(stk, [128, NKC], F32)
                bvr = sb(stk, [1, D], BF16)
                ctx.dma("g", wq[:], l0a_wT["q"].rearrange("(c p) e -> p c e", c=NKC), writes=[wq.name])
                ctx.dma("g", wk[:], l0a_wT["k"].rearrange("(c p) e -> p c e", c=NKC), writes=[wk.name])
                ctx.dma("g", wv[:], l0a_wT["v"].rearrange("(c p) e -> p c e", c=NKC), writes=[wv.name])
                ctx.dma("y", bqs[:], l0a_bq, writes=[bqs.name])
                ctx.dma("y", bks[:], l0a_bk, writes=[bks.name])
                ctx.dma("g", bvr[:], l0a_bv, writes=[bvr.name])
                cs = stk
                xc_pr = [sb(cs, [128, NKC, 1024], BF16) for _ in range(2)]
                qb = sb(cs, [128, NKC, 1024], BF16)
                kb = sb(cs, [128, NKC, 1024], BF16)
                vb = sb(cs, [128, NKC, 1024], BF16)
                mcols = sb(cs, [128, 16], F32)
                mc8 = sb(cs, [128, 8], F32)
                mrows = sb(cs, [8, 128], BF16)
                mflat = sb(cs, [1, 8 * 128], BF16)
                scr = sb(cs, [128, 512], F32)
                st2 = sb(cs, [128, 256], F32)
                pt = sb(cs, [128, 8, 256], BF16)
                lsb = sb(cs, [1, 256], F32)
                rbb = sb(cs, [128, 256], F32)
                a0sb = sb(cs, [128, NKC, 256], F32)
                pq = ps(cs, [128, 512])
                ps1 = pq
                pmx = ps(cs, [8, 512])
                pmt = pmx[0:8, 0:128]
                plv = pmx[0:1, 128:384]
                pstA = ps(cs, [128, 512])
                pstB = ps(cs, [128, 512])
                pst2 = (pstA, pstB)
                pav = ps(cs, [128, NKC, 256])
                for ch in range(TOWN // 1024):
                    if True:
                        c0 = HALO + 1024 * ch
                        xc = xc_pr[ch % 2]
                        ctx.dma("g", xc[:], fm(ya, c0, 1024), reads=["ya"], writes=[xc.name])
                        for (wsb, bias, dst) in ((wq, bqs, qb), (wk, bks, kb)):
                            for oc in range(NKC):
                                for nt in range(2):
                                    pQ = (pq, pstA)[(oc * 2 + nt) % 2]
                                    for kc in range(NKC):
                                        ctx.op("t", lambda E, oc=oc, nt=nt, kc=kc, wsb=wsb, xc=xc, pQ=pQ: E.matmul(
                                            pQ[:], wsb[:, kc, oc * 128:oc * 128 + 128],
                                            xc[:, kc, nt * 512:nt * 512 + 512],
                                            start=(kc == 0), stop=(kc == NKC - 1), skip_group_check=True),
                                            reads=[wsb.name, xc.name], writes=[pQ.name])
                                    ctx.op("s", lambda E, oc=oc, nt=nt, bias=bias, dst=dst, pQ=pQ: E.activation(
                                        dst[:, oc, nt * 512:nt * 512 + 512], pQ[:], AF.Identity,
                                        bias=bias[:, oc:oc + 1]),
                                        reads=[pQ.name, bias.name], writes=[dst.name])
                        for tch in range(8):
                            for nt in range(2):
                                pQ = (pq, pstA)[(tch * 2 + nt) % 2]
                                for kc in range(NKC):
                                    ctx.op("t", lambda E, tch=tch, nt=nt, kc=kc, xc=xc, pQ=pQ: E.matmul(
                                        pQ[:], xc[:, kc, tch * 128:tch * 128 + 128],
                                        wv[:, kc, nt * 512:nt * 512 + 512],
                                        start=(kc == 0), stop=False, skip_group_check=True),
                                        reads=[wv.name, xc.name], writes=[pQ.name])
                                ctx.op("t", lambda E, nt=nt, pQ=pQ: E.matmul(
                                    pQ[:], onesb[0:1, 0:128], bvr[0:1, nt * 512:nt * 512 + 512],
                                    start=False, stop=True, skip_group_check=True),
                                    reads=["onesb", bvr.name], writes=[pQ.name])
                                ctx.op("s", lambda E, tch=tch, nt=nt, pQ=pQ: E.activation(
                                    vb[:, tch, nt * 512:nt * 512 + 512], pQ[:], AF.Copy),
                                    reads=[pQ.name], writes=[vb.name])
                        for qc in range(8):
                            for kt in range(2):
                                pP1 = pst2[(qc * 2 + kt) % 2]
                                for ec in range(NKC):
                                    ctx.op("t", lambda E, qc=qc, kt=kt, ec=ec, pP1=pP1: E.matmul(
                                        pP1[:], qb[:, ec, qc * 128:qc * 128 + 128],
                                        kb[:, ec, kt * 512:kt * 512 + 512],
                                        start=(ec == 0), stop=(ec == NKC - 1), skip_group_check=True),
                                        reads=[qb.name, kb.name], writes=[pP1.name])
                                ctx.op("v", lambda E, qc=qc, kt=kt, pP1=pP1: E.reduce_max(
                                    mcols[:, 2 * qc + kt:2 * qc + kt + 1], pP1[:], axis=AX.X),
                                    reads=[pP1.name], writes=[mcols.name])
                        ctx.op("v", lambda E: E.tensor_tensor(
                            mc8[:], mcols[:].rearrange("p (a b) -> p a b", b=2)[:, :, 0],
                            mcols[:].rearrange("p (a b) -> p a b", b=2)[:, :, 1],
                            op=AluOpType.max),
                            reads=[mcols.name], writes=[mc8.name])
                        ctx.op("t", lambda E: E.transpose(pmt, mc8[:], ident[:]),
                               reads=[mc8.name, "ident"], writes=[pmx.name])
                        ctx.op("s", lambda E: E.activation(mrows[:], pmt, AF.Identity, scale=-1.0),
                               reads=[pmx.name], writes=[mrows.name])
                        ctx.dma("y", mscr[0:8 * 128].rearrange("(a b) -> a b", b=128), mrows[:],
                                reads=[mrows.name], writes=["mscr"])
                        ctx.dma("y", mflat[:], mscr[0:8 * 128].rearrange("(a b) -> a b", a=1),
                                reads=["mscr"], writes=[mflat.name])
                        for qq in range(4):
                            for kt in range(8):
                                pS = pst2[kt % 2]
                                for ec in range(NKC):
                                    ctx.op("t", lambda E, qq=qq, kt=kt, ec=ec, pS=pS: E.matmul(
                                        pS[:, 0:256], kb[:, ec, kt * 128:kt * 128 + 128],
                                        qb[:, ec, qq * 256:qq * 256 + 256],
                                        start=(ec == 0), stop=False, skip_group_check=True),
                                        reads=[qb.name, kb.name], writes=[pS.name])
                                for half in range(2):
                                    ctx.op("t", lambda E, qq=qq, half=half, pS=pS: E.matmul(
                                        pS[:, half * 128:half * 128 + 128],
                                        onesb[0:1, 0:128],
                                        mflat[0:1, 128 * (2 * qq + half):128 * (2 * qq + half) + 128],
                                        start=False, stop=(half == 1), skip_group_check=True),
                                        reads=["onesb", mflat.name], writes=[pS.name])
                                ctx.op("s", lambda E, kt=kt, pS=pS: E.activation(
                                    pt[:, kt, :], pS[:, 0:256], AF.Exp),
                                    reads=[pS.name], writes=[pt.name + str(kt)])
                            for kt in range(8):
                                ctx.op("t", lambda E, kt=kt: E.matmul(
                                    plv, onesb[:, 0:1], pt[:, kt, :],
                                    start=(kt == 0), stop=(kt == 7), skip_group_check=True),
                                    reads=["onesb", pt.name + str(kt)], writes=[pmx.name])
                            ctx.op("v", lambda E: E.tensor_copy(lsb[:], plv),
                                   reads=[pmx.name], writes=[lsb.name])
                            ctx.op("v", lambda E: E.reciprocal(lsb[:], lsb[:]),
                                   reads=[lsb.name], writes=[lsb.name])
                            ctx.dma("y", lscr2[0:256].rearrange("(a b) -> a b", a=1), lsb[:],
                                    reads=[lsb.name], writes=["lscr2"])
                            ctx.dma("y", rbb[:],
                                    lscr2[0:256].rearrange("(a b) -> a b", a=1)
                                    .broadcast_to([128, 256]),
                                    reads=["lscr2"], writes=[rbb.name])
                            for dc in range(NKC):
                                for kt in range(8):
                                    ctx.op("t", lambda E, kt=kt, dc=dc: E.matmul(
                                        pav[:, dc, :], vb[:, kt, dc * 128:dc * 128 + 128],
                                        pt[:, kt, :], start=(kt == 0), stop=(kt == 7), skip_group_check=True),
                                        reads=[vb.name, pt.name + str(kt)], writes=[pav.name])
                                ctx.op("v", lambda E, dc=dc: E.scalar_tensor_tensor(
                                    a0sb[:, dc, :], pav[:, dc, :], 1.0, rbb[:],
                                    op0=AluOpType.mult, op1=AluOpType.mult),
                                    reads=[pav.name, rbb.name], writes=[a0sb.name])
                            ctx.dma("y", fm(a0h, 1024 * ch + 256 * qq, 256), a0sb[:],
                                    reads=[a0sb.name], writes=["a0h"])
        phase(ph_l0_attn)

        TTW = 512
        NTI = T // TTW

        def ph_inner_attn(l, src, dst, sname, dname):
            with ExitStack() as stk:
                wqkv = sb(stk, [128, NKC, 3 * D], BF16)
                wo = sb(stk, [128, NKC, D], BF16)
                selq = sb(stk, [16, D], F32)
                selk = sb(stk, [16, D], F32)
                ctx.dma("g", wqkv[:], l0f_wqkv[l].rearrange("(c p) e -> p c e", c=NKC),
                        writes=[wqkv.name])
                ctx.dma("g", wo[:], l0f_wo[l].rearrange("(c p) e -> p c e", c=NKC),
                        writes=[wo.name])
                ctx.dma("y", selq[:], l0f_selq[l], writes=[selq.name])
                ctx.dma("y", selk[:], l0f_selk[l], writes=[selk.name])
                vony = sb(stk, [128, 6, HEADS * 65], BF16)
                vony_v = vony[:].rearrange("p c (h u) -> p c h u", u=65)
                ctx.op("v", lambda E: E.memset(
                    vony[:].rearrange("p c (h u) -> p c h u", u=65)[:, :, :, 64:65], 1.0),
                    writes=[vony.name])
                ybf_pr = [sb(stk, [128, NKC, 769], BF16), sb(stk, [128, NKC, 769], BF16)]
                yf32 = sb(stk, [128, NKC, TTW], F32)
                tq_a = sb(stk, [128, TTW], BF16)
                tq_d = sb(stk, [128, TTW], BF16)
                tk_a = sb(stk, [128, 768], BF16)
                tk_d = sb(stk, [128, 768], BF16)
                qrot = sb(stk, [128, NKC, TTW], BF16)
                krot = sb(stk, [128, NKC, 768], BF16)
                qraw = sb(stk, [128, NKC, 768], BF16)
                scrb = sb(stk, [128, 768], BF16)
                sqq = scrb
                nsr = sb(stk, [16, 768], F32)
                tt1 = sb(stk, [128, 768], BF16)
                tt2 = sb(stk, [128, 768], BF16)
                mcols = sb(stk, [128, 64], F32)
                mrows = sb(stk, [64, 128], BF16)
                mflat = sb(stk, [1, 32 * 128], BF16)
                st2 = sb(stk, [128, 2, 256], F32)
                ptb = sb(stk, [128, 2, 256], BF16)
                lrow = sb(stk, [1, 32 * 256], BF16)
                lmat2 = sb(stk, [16, TTW], F32)
                rbb = sb(stk, [128, TTW], BF16)
                attc = qraw[:].rearrange("p c t -> p c t")[:, :, 0:TTW]
                attc_name = qraw.name
                yo = None
                pq = ps(stk, [128, 512])
                pun = ps(stk, [128, 1024])
                psq16 = pun[0:16, 0:768]
                prb = pun[0:128, 0:512]
                pmt = pun[0:64, 0:128]
                pstA = ps(stk, [128, 512])
                pstB = ps(stk, [128, 512])
                pst2 = (pstA, pstB)
                pavA = ps(stk, [65, 512])
                pavB = ps(stk, [65, 512])
                pswp = ps(stk, [128, 512])
                pav2 = (pavA, pavB)

                def proj_qk(i, which, ybf):
                    w = TTW if which == "q" else 768
                    dstb = qrot if which == "q" else krot
                    sel = selq if which == "q" else selk
                    ta = tq_a if which == "q" else tk_a
                    td = tq_d if which == "q" else tk_d
                    off = 1024 if which == "k" else 0
                    u0 = 257 if which == "q" else 1
                    nts = [(0, 512)] if which == "q" else [(0, 512), (512, 256)]
                    for oc in range(NKC):
                        for (n0, nw) in nts:
                            pQ = (pq, pswp)[oc % 2]
                            for kc in range(NKC):
                                lo = (u0 if kc < 4 else u0 - 1) + n0
                                ctx.op("t", lambda E, oc=oc, kc=kc, lo=lo, off=off, nw=nw, ybf=ybf, pQ=pQ: E.matmul(
                                    pQ[:, 0:nw], wqkv[:, kc, off + oc * 128:off + oc * 128 + 128],
                                    ybf[:, kc, lo:lo + nw],
                                    start=(kc == 0), stop=(kc == NKC - 1), skip_group_check=True),
                                    reads=[wqkv.name, ybf.name], writes=[pQ.name])
                            ctx.op("s", lambda E, n0=n0, nw=nw, pQ=pQ: E.activation(
                                sqq[:, n0:n0 + nw], pQ[:, 0:nw], AF.Square),
                                reads=[pQ.name], writes=[sqq.name])
                            ctx.op("s", lambda E, oc=oc, n0=n0, nw=nw, pQ=pQ: E.activation(
                                qraw[:, oc, n0:n0 + nw], pQ[:, 0:nw], AF.Copy),
                                reads=[pQ.name], writes=[qraw.name])
                            ctx.op("t", lambda E, oc=oc, n0=n0, nw=nw: E.matmul(
                                psq16[:, n0:n0 + nw], bdsel[:, oc * 16:oc * 16 + 16],
                                sqq[:, n0:n0 + nw],
                                start=(oc == 0), stop=(oc == NKC - 1), skip_group_check=True),
                                reads=[sqq.name, "bdsel"], writes=[pun.name])
                    ctx.op("s", lambda E, w=w: E.activation(nsr[:, 0:w], psq16[:, 0:w], AF.Sqrt),
                           reads=[pun.name], writes=[nsr.name])
                    ctx.op("v", lambda E, w=w: E.tensor_scalar_max(nsr[:, 0:w], nsr[:, 0:w], 1e-12),
                           reads=[nsr.name], writes=[nsr.name])
                    ctx.op("v", lambda E, w=w: E.reciprocal(nsr[:, 0:w], nsr[:, 0:w]),
                           reads=[nsr.name], writes=[nsr.name])
                    for oc in range(NKC):
                        for (n0, nw) in nts:
                            ctx.op("t", lambda E, oc=oc, n0=n0, nw=nw, sel=sel: E.matmul(
                                prb[:, 0:nw], sel[:, oc * 128:oc * 128 + 128], nsr[:, n0:n0 + nw],
                                start=True, stop=True, skip_group_check=True),
                                reads=[sel.name, nsr.name], writes=[pun.name])
                            ctx.op("v", lambda E, oc=oc, n0=n0, nw=nw: E.tensor_tensor(
                                tt1[:, n0:n0 + nw], qraw[:, oc, n0:n0 + nw], prb[:, 0:nw],
                                op=AluOpType.mult),
                                reads=[qraw.name, pun.name], writes=[tt1.name])
                        ctx.op("v", lambda E, w=w, ta=ta: E.tensor_tensor(
                            tt2[:, 0:w], tt1[:, 0:w], ta[:, 0:w], op=AluOpType.mult),
                            reads=[tt1.name, ta.name], writes=[tt2.name])
                        ctx.op("v", lambda E, w=w, td=td: E.tensor_tensor(
                            scrb[:, 0:w], tt1[:, 0:w], td[:, 0:w], op=AluOpType.mult),
                            reads=[tt1.name, td.name], writes=[scrb.name])
                        for (n0, nw) in nts:
                            ctx.op("t", lambda E, n0=n0, nw=nw: E.matmul(
                                pswp[:, 0:nw], identb[:], tt2[:, n0:n0 + nw],
                                start=True, stop=False, skip_group_check=True),
                                reads=["identb", tt2.name], writes=[pswp.name])
                            ctx.op("t", lambda E, n0=n0, nw=nw: E.matmul(
                                pswp[:, 0:nw], swpb[:], scrb[:, n0:n0 + nw],
                                start=False, stop=True, skip_group_check=True),
                                reads=["swpb", scrb.name], writes=[pswp.name])
                            ctx.op("s", lambda E, oc=oc, n0=n0, nw=nw, dstb=dstb: E.activation(
                                dstb[:, oc, n0:n0 + nw], pswp[:, 0:nw], AF.Copy),
                                reads=[pswp.name], writes=[dstb.name])

                for i in range(NTI):
                    c0 = i * TTW
                    ybf = ybf_pr[i % 2]
                    if i == 0:
                        ctx.op("v", lambda E, ybf=ybf: E.memset(ybf[:, :, 0:257], 0.0),
                               writes=[ybf.name])
                        ctx.dma("g", ybf[:, :, 257:769], fm(src, 0, TTW),
                                reads=[sname], writes=[ybf.name])
                        ctx.op("v", lambda E: E.memset(tk_a[:, 0:256], 0.0), writes=[tk_a.name])
                        ctx.op("v", lambda E: E.memset(tk_d[:, 0:256], 0.0), writes=[tk_d.name])
                        ctx.dma("g", tk_a[:, 256:768], tA["k"][:, 0:TTW], writes=[tk_a.name])
                        ctx.dma("g", tk_d[:, 256:768], tD["k"][:, 0:TTW], writes=[tk_d.name])
                    else:
                        ctx.dma("g", ybf[:], fm(src, c0 - 257, 769),
                                reads=[sname], writes=[ybf.name])
                        ctx.dma("g", tk_a[:], tA["k"][:, c0 - 256:c0 + 512], writes=[tk_a.name])
                        ctx.dma("g", tk_d[:], tD["k"][:, c0 - 256:c0 + 512], writes=[tk_d.name])
                    ctx.dma("y", yf32[:], fm(src, c0, TTW), reads=[sname], writes=[yf32.name])
                    ctx.dma("g", tq_a[:], tA["q"][:, c0:c0 + TTW], writes=[tq_a.name])
                    ctx.dma("g", tq_d[:], tD["q"][:, c0:c0 + TTW], writes=[tq_d.name])
                    proj_qk(i, "q", ybf)
                    proj_qk(i, "k", ybf)
                    for tch in range(6):
                        for nt in range(2):
                            pQ = (pq, pswp)[(tch * 2 + nt) % 2]
                            for kc in range(NKC):
                                lo = (1 if kc < 4 else 0) + 128 * tch
                                ctx.op("t", lambda E, tch=tch, nt=nt, kc=kc, lo=lo, ybf=ybf, pQ=pQ: E.matmul(
                                    pQ[:], ybf[:, kc, lo:lo + 128],
                                    wqkv[:, kc, 2048 + nt * 512:2048 + nt * 512 + 512],
                                    start=(kc == 0), stop=(kc == NKC - 1), skip_group_check=True),
                                    reads=[wqkv.name, ybf.name], writes=[pQ.name])
                            ctx.op("s", lambda E, tch=tch, nt=nt, pQ=pQ: E.activation(
                                vony_v[:, tch, 8 * nt:8 * nt + 8, 0:64],
                                pQ[:].rearrange("p (h u) -> p h u", u=64), AF.Copy),
                                reads=[pQ.name], writes=[vony.name])
                    for h in range(HEADS):
                        hp, hc = 64 * (h % 2), h // 2
                        for w in range(2):
                            fw = (i == 0 and w == 0)
                            j2 = (h * 2 + w) * 2
                            for qc in range(2):
                                pP1 = pst2[(h * 4 + w * 2 + qc) % 2]
                                ctx.op("t", lambda E, h=h, w=w, qc=qc, hp=hp, hc=hc, pP1=pP1: E.matmul(
                                    pP1[:], qrot[hp:hp + 64, hc, 256 * w + 128 * qc:256 * w + 128 * qc + 128],
                                    krot[hp:hp + 64, hc, 256 * w:256 * w + 512],
                                    start=True, stop=True, skip_group_check=True),
                                    reads=[qrot.name, krot.name], writes=[pP1.name])
                                mv = 2 * (1 if fw else 0) + qc
                                ctx.op("v", lambda E, mv=mv, pP1=pP1: E.tensor_tensor(
                                    scrb[:, 0:512], pP1[:], mk1[:, mv, :], op=AluOpType.add),
                                    reads=[pP1.name, "mk1"], writes=[scrb.name])
                                ctx.op("v", lambda E, qc=qc, j2=j2: E.reduce_max(
                                    mcols[:, j2 + qc:j2 + qc + 1], scrb[:, 0:512], axis=AX.X),
                                    reads=[scrb.name], writes=[mcols.name])
                    ctx.op("t", lambda E: E.transpose(pmt, mcols[:], ident[:]),
                           reads=[mcols.name, "ident"], writes=[pun.name])
                    ctx.op("s", lambda E: E.activation(mrows[:], pmt, AF.Identity, scale=-1.0),
                           reads=[pun.name], writes=[mrows.name])
                    ctx.dma("y", mscr.rearrange("(a b) -> a b", b=128), mrows[:],
                            reads=[mrows.name], writes=["mscr"])
                    for h in range(HEADS):
                        hp, hc = 64 * (h % 2), h // 2
                        if h % 8 == 0:
                            ctx.dma("y", mflat[:],
                                    mscr[4096 * (h // 8):4096 * (h // 8) + 4096]
                                    .rearrange("(a b) -> a b", a=1),
                                    reads=["mscr"], writes=[mflat.name])
                        for w in range(2):
                            fw = (i == 0 and w == 0)
                            j2 = (h * 2 + w) * 2 - 32 * (h // 8)
                            pA = pav2[(h * 2 + w) % 2]
                            kts = range(2, 4) if fw else range(4)
                            first_kt = 2 if fw else 0
                            for kt in kts:
                                pS = pst2[kt % 2]
                                ctx.op("t", lambda E, h=h, w=w, kt=kt, hp=hp, hc=hc, pS=pS: E.matmul(
                                    pS[:, 0:256], krot[hp:hp + 64, hc, 256 * w + 128 * kt:256 * w + 128 * kt + 128],
                                    qrot[hp:hp + 64, hc, 256 * w:256 * w + 256],
                                    start=True, stop=False, skip_group_check=True),
                                    reads=[qrot.name, krot.name], writes=[pS.name])
                                for half in range(2):
                                    ctx.op("t", lambda E, j2=j2, half=half, pS=pS: E.matmul(
                                        pS[:, half * 128:half * 128 + 128],
                                        onesb[0:1, 0:128],
                                        mflat[0:1, 128 * (j2 + half):128 * (j2 + half) + 128],
                                        start=False, stop=(half == 1), skip_group_check=True),
                                        reads=["onesb", mflat.name], writes=[pS.name])
                                ctx.op("v", lambda E, kt=kt, pS=pS: E.tensor_tensor(
                                    st2[:, kt % 2, :], pS[:, 0:256], mk2[:, kt, :], op=AluOpType.add),
                                    reads=[pS.name, "mk2"], writes=[st2.name + str(kt % 2)])
                                ctx.op("s", lambda E, kt=kt: E.activation(
                                    ptb[:, kt % 2, :], st2[:, kt % 2, :], AF.Exp),
                                    reads=[st2.name + str(kt % 2)], writes=[ptb.name + str(kt % 2)])
                                if kt > first_kt:
                                    ctx.op("t", lambda E, h=h, w=w, kt=kt - 1, pA=pA, first_kt=first_kt: E.matmul(
                                        pA[:, 0:256], vony[:, 2 * w + kt, 65 * h:65 * h + 65],
                                        ptb[:, kt % 2, :],
                                        start=(kt == first_kt), stop=(kt == 3), skip_group_check=True),
                                        reads=[vony.name, ptb.name + str(kt % 2)], writes=[pA.name])
                            for kt in (3,):
                                ctx.op("t", lambda E, h=h, w=w, kt=kt, pA=pA, first_kt=first_kt: E.matmul(
                                    pA[:, 0:256], vony[:, 2 * w + kt, 65 * h:65 * h + 65],
                                    ptb[:, kt % 2, :],
                                    start=(kt == first_kt), stop=(kt == 3), skip_group_check=True),
                                    reads=[vony.name, ptb.name + str(kt % 2)], writes=[pA.name])
                            ctx.op("v", lambda E, h=h, w=w, pA=pA: E.tensor_copy(
                                lrow[0:1, 256 * (h * 2 + w):256 * (h * 2 + w) + 256],
                                pA[64:65, 0:256]),
                                reads=[pA.name], writes=[lrow.name])
                            ctx.op("s", lambda E, h=h, w=w, hp=hp, hc=hc, pA=pA: E.activation(
                                attc[hp:hp + 64, hc, 256 * w:256 * w + 256], pA[0:64, 0:256], AF.Copy),
                                reads=[pA.name], writes=[attc_name])
                    ctx.dma("y", lscr.rearrange("(a b) -> a b", a=1), lrow[:],
                            reads=[lrow.name], writes=["lscr"])
                    ctx.dma("g", lmat2[:],
                            lscr.rearrange("(h x) -> h x", h=16),
                            reads=["lscr"], writes=[lmat2.name])
                    ctx.op("v", lambda E: E.reciprocal(lmat2[:], lmat2[:]),
                           reads=[lmat2.name], writes=[lmat2.name])
                    ctx.dma("y", lscr2.rearrange("(h x) -> h x", h=16), lmat2[:],
                            reads=[lmat2.name], writes=["lscr2"])
                    for c in range(NKC):
                        lv = lscr2.rearrange("(h x) -> h x", h=16)
                        ctx.dma("g", rbb[0:64, :],
                                lv[2 * c:2 * c + 1, :].broadcast_to([64, TTW]),
                                reads=["lscr2"], writes=[rbb.name])
                        ctx.dma("g", rbb[64:128, :],
                                lv[2 * c + 1:2 * c + 2, :].broadcast_to([64, TTW]),
                                reads=["lscr2"], writes=[rbb.name])
                        ctx.op("v", lambda E, c=c: E.tensor_tensor(
                            attc[:, c, :], attc[:, c, :], rbb[:], op=AluOpType.mult),
                            reads=[attc_name, rbb.name], writes=[attc_name])
                    for oc in range(NKC):
                        for ic in range(NKC):
                            ctx.op("t", lambda E, oc=oc, ic=ic: E.matmul(
                                pq[:], wo[:, ic, oc * 128:oc * 128 + 128], attc[:, ic, :],
                                start=(ic == 0), stop=(ic == NKC - 1), skip_group_check=True),
                                reads=[wo.name, attc_name], writes=[pq.name])
                        ctx.op("v", lambda E, oc=oc: E.scalar_tensor_tensor(
                            yf32[:, oc, :], pq[:], 1.0, yf32[:, oc, :],
                            op0=AluOpType.mult, op1=AluOpType.add),
                            reads=[pq.name, yf32.name], writes=[yf32.name])
                    ctx.dma("y", fm(dst, c0, TTW), yf32[:], reads=[yf32.name], writes=[dname])

        def ph_inner_ffn(l, src, dst, sname, dname):
            NJ = cdiv(DI, 128)
            with ExitStack() as stk:
                w1 = sb(stk, [128, NKC, W1OUT], BF16)
                w2 = sb(stk, [128, NJ, D], BF16)
                b1c = sb(stk, [128, 2 * NJ], F32)
                b2c = sb(stk, [128, NKC], F32)
                gc = sb(stk, [128, NKC], F32)
                bc = sb(stk, [128, NKC], F32)
                ctx.dma("g", w1[:], l0f_w1[l].rearrange("(c p) e -> p c e", c=NKC),
                        writes=[w1.name])
                for j in range(NJ):
                    sz = min(128, DI - 128 * j)
                    ctx.dma("g", w2[0:sz, j, :], l0f_w2[l][128 * j:128 * j + sz, :],
                            writes=[w2.name])
                ctx.dma("y", b1c[:], l0f_b1c[l], writes=[b1c.name])
                ctx.dma("y", b2c[:], l0f_b2[l], writes=[b2c.name])
                ctx.dma("y", gc[:], l0f_g[l], writes=[gc.name])
                ctx.dma("y", bc[:], l0f_b[l], writes=[bc.name])
                ybf = sb(stk, [128, NKC, 513], BF16)
                yf32 = sb(stk, [128, NKC, TTW], F32)
                sq = sb(stk, [128, TTW], BF16)
                rowb = sb(stk, [1, 2 * TTW], F32)
                t1 = sb(stk, [128, TTW], F32)
                xnb = sb(stk, [128, NKC, TTW], BF16)
                pbc = ps(stk, [128, 1024])
                gel = sb(stk, [128, TTW], BF16)
                ab = sb(stk, [128, TTW], BF16)
                glu = sb(stk, [128, NJ, TTW], BF16)
                pa = ps(stk, [128, 512])
                pg = ps(stk, [128, 512])
                pum = ps(stk, [1, 512])
                pq2 = ps(stk, [1, 512])
                for i in range(NTI):
                    c0 = i * TTW
                    if i == 0:
                        ctx.op("v", lambda E: E.memset(ybf[:, :, 0:1], 0.0), writes=[ybf.name])
                        ctx.dma("g", ybf[:, :, 1:513], fm(src, 0, TTW),
                                reads=[sname], writes=[ybf.name])
                    else:
                        ctx.dma("g", ybf[:], fm(src, c0 - 1, 513),
                                reads=[sname], writes=[ybf.name])
                    ctx.dma("y", yf32[:], fm(src, c0, TTW), reads=[sname], writes=[yf32.name])
                    for kc in range(NKC):
                        lo = 1 if kc < 4 else 0
                        ctx.op("s", lambda E, kc=kc, lo=lo: E.activation(
                            sq[:], ybf[:, kc, lo:lo + TTW], AF.Square),
                            reads=[ybf.name], writes=[sq.name])
                        ctx.op("t", lambda E, kc=kc, lo=lo: E.matmul(
                            pum[:], onesb[:, 0:1], ybf[:, kc, lo:lo + TTW],
                            start=(kc == 0), stop=(kc == NKC - 1), skip_group_check=True),
                            reads=[ybf.name, "onesb"], writes=[pum.name])
                        ctx.op("t", lambda E, kc=kc: E.matmul(
                            pq2[:], onesb[:, 0:1], sq[:],
                            start=(kc == 0), stop=(kc == NKC - 1), skip_group_check=True),
                            reads=[sq.name, "onesb"], writes=[pq2.name])
                    mu = rowb[0:1, 0:TTW]
                    var = rowb[0:1, TTW:2 * TTW]
                    rn = rowb.name
                    ctx.op("v", lambda E: E.tensor_scalar(mu, pum[:], 1.0 / D, None, op0=AluOpType.mult),
                           reads=[pum.name], writes=[rn])
                    ctx.op("v", lambda E: E.scalar_tensor_tensor(
                        var, mu, 1.0, mu, op0=AluOpType.mult, op1=AluOpType.mult),
                        reads=[rn], writes=[rn])
                    ctx.op("v", lambda E: E.scalar_tensor_tensor(
                        var, pq2[:], 1.0 / D, var, op0=AluOpType.mult, op1=AluOpType.subtract),
                        reads=[pq2.name, rn], writes=[rn])
                    ctx.op("v", lambda E: E.tensor_scalar(var, var, 1e-5, None, op0=AluOpType.add),
                           reads=[rn], writes=[rn])
                    ctx.op("v", lambda E: E.reciprocal(var, var), reads=[rn], writes=[rn])
                    ctx.op("s", lambda E: E.activation(var, var, AF.Sqrt), reads=[rn], writes=[rn])
                    ctx.op("v", lambda E: E.scalar_tensor_tensor(
                        mu, mu, -1.0, var, op0=AluOpType.mult, op1=AluOpType.mult),
                        reads=[rn], writes=[rn])
                    ctx.op("t", lambda E: E.matmul(
                        pbc[:, 0:512], onesfr[0:1, 0:128], var,
                        start=True, stop=True, skip_group_check=True),
                        reads=["onesfr", rn], writes=[pbc.name])
                    ctx.op("t", lambda E: E.matmul(
                        pbc[:, 512:1024], onesfr[0:1, 0:128], mu,
                        start=True, stop=True, skip_group_check=True),
                        reads=["onesfr", rn], writes=[pbc.name])
                    for kc in range(NKC):
                        lo = 1 if kc < 4 else 0
                        ctx.op("v", lambda E, kc=kc, lo=lo: E.tensor_tensor(
                            t1[:], ybf[:, kc, lo:lo + TTW], pbc[:, 0:512], op=AluOpType.mult),
                            reads=[ybf.name, pbc.name], writes=[t1.name])
                        ctx.op("v", lambda E, kc=kc: E.tensor_tensor(
                            t1[:], t1[:], pbc[:, 512:1024], op=AluOpType.add),
                            reads=[t1.name, pbc.name], writes=[t1.name])
                        ctx.op("v", lambda E, kc=kc: E.tensor_scalar(
                            xnb[:, kc, :], t1[:], gc[:, kc:kc + 1], bc[:, kc:kc + 1],
                            op0=AluOpType.mult, op1=AluOpType.add),
                            reads=[t1.name, gc.name, bc.name], writes=[xnb.name])
                    for j in range(NJ):
                        sz = min(128, DI - 128 * j)
                        for kc in range(NKC):
                            ctx.op("t", lambda E, j=j, kc=kc, sz=sz: E.matmul(
                                pa[0:sz, :], w1[:, kc, 128 * j:128 * j + sz], xnb[:, kc, :],
                                start=(kc == 0), stop=(kc == NKC - 1), skip_group_check=True),
                                reads=[w1.name, xnb.name], writes=[pa.name])
                            ctx.op("t", lambda E, j=j, kc=kc, sz=sz: E.matmul(
                                pg[0:sz, :], w1[:, kc, DI + 128 * j:DI + 128 * j + sz], xnb[:, kc, :],
                                start=(kc == 0), stop=(kc == NKC - 1), skip_group_check=True),
                                reads=[w1.name, xnb.name], writes=[pg.name])
                        ctx.op("s", lambda E, j=j, sz=sz: E.activation(
                            gel[0:sz, :], pg[0:sz, :], AF.Gelu, bias=b1c[0:sz, 2 * j + 1:2 * j + 2]),
                            reads=[pg.name, b1c.name], writes=[gel.name])
                        ctx.op("s", lambda E, j=j, sz=sz: E.activation(
                            ab[0:sz, :], pa[0:sz, :], AF.Identity, bias=b1c[0:sz, 2 * j:2 * j + 1]),
                            reads=[pa.name, b1c.name], writes=[ab.name])
                        ctx.op("v", lambda E, j=j, sz=sz: E.tensor_tensor(
                            glu[0:sz, j, :], ab[0:sz, :], gel[0:sz, :], op=AluOpType.mult),
                            reads=[ab.name, gel.name], writes=[glu.name])
                    for oc in range(NKC):
                        for j in range(NJ):
                            sz = min(128, DI - 128 * j)
                            ctx.op("t", lambda E, oc=oc, j=j, sz=sz: E.matmul(
                                pa[:], w2[0:sz, j, oc * 128:oc * 128 + 128], glu[0:sz, j, :],
                                start=(j == 0), stop=(j == NJ - 1), skip_group_check=True),
                                reads=[w2.name, glu.name], writes=[pa.name])
                        ctx.op("v", lambda E, oc=oc: E.scalar_tensor_tensor(
                            yf32[:, oc, :], pa[:], b2c[:, oc:oc + 1], yf32[:, oc, :],
                            op0=AluOpType.add, op1=AluOpType.add),
                            reads=[pa.name, b2c.name, yf32.name], writes=[yf32.name])
                    ctx.dma("y", fm(dst, c0, TTW), yf32[:], reads=[yf32.name], writes=[dname])

        bufs = [(ya, "ya"), (yb, "yb")]
        cur = 0
        for l in range(L0DEPTH):
            src, sname = bufs[cur]
            dst, dname = bufs[1 - cur]
            phase(lambda l=l, src=src, dst=dst, sn=sname, dn=dname: ph_inner_attn(l, src, dst, sn, dn))
            cur = 1 - cur
            src, sname = bufs[cur]
            dst, dname = bufs[1 - cur]
            phase(lambda l=l, src=src, dst=dst, sn=sname, dn=dname: ph_inner_ffn(l, src, dst, sn, dn))
            cur = 1 - cur
        yfin, yfin_name = bufs[cur]

        NTO = TOWN // TTW
        mks = sb(big, [1, 1], F32)

        def ph_x1():
            with ExitStack() as stk:
                rwT = sb(stk, [128, NKC, NEXP], F32)
                ctx.dma("y", rwT[:], l1f_rwT.rearrange("(c p) e -> p c e", c=NKC), writes=[rwT.name])
                xi = sb(stk, [128, NKC, TTW], F32)
                a0t = sb(stk, [128, NKC, TTW], F32)
                yt = sb(stk, [128, NKC, TTW], F32)
                xb = sb(stk, [128, NKC, TTW], BF16)
                sq = sb(stk, [128, TTW], BF16)
                rowb = sb(stk, [1, 2 * TTW], F32)
                t1 = sb(stk, [128, TTW], F32)
                xnf = sb(stk, [128, NKC, TTW], F32)
                pbc = ps(stk, [128, 1024])
                xnb = sb(stk, [128, NKC, TTW], BF16)
                sgt = sb(stk, [128, 4, NEXP], F32)
                sgbt = sb(stk, [128, NEXP], F32)
                m1c = sb(stk, [128, 1], F32)
                mskt = sb(stk, [128, NEXP], F32)
                s2t = sb(stk, [128, NEXP], F32)
                gtt = sb(stk, [128, NEXP], F32)
                gout = sb(stk, [NEXP, TTW], F32)
                ebrep = sb(stk, [128, NEXP], F32)
                pum = ps(stk, [1, 512])
                pq2 = ps(stk, [1, 512])
                prt = ps(stk, [128, 512])
                pgt = ps(stk, [NEXP, 512])
                ctx.dma("y", ebrep[:],
                        l1f_eb.rearrange("e 1 -> 1 e").broadcast_to([128, NEXP]),
                        writes=[ebrep.name])
                for i in range(NTO):
                    c0 = i * TTW
                    if True:
                        ctx.dma("y", xi[:], fm(xin, HALO + c0, TTW), writes=[xi.name])
                        ctx.dma("y", a0t[:], fm(a0h, c0, TTW), reads=["a0h"], writes=[a0t.name])
                        ctx.dma("y", yt[:], fm(yfin, HALO + c0, TTW), reads=[yfin_name], writes=[yt.name])
                        for kc in range(NKC):
                            ctx.op("v", lambda E, kc=kc: E.tensor_tensor(
                                xi[:, kc, :], xi[:, kc, :], a0t[:, kc, :], op=AluOpType.add),
                                reads=[xi.name, a0t.name], writes=[xi.name])
                            ctx.op("v", lambda E, kc=kc: E.tensor_tensor(
                                xi[:, kc, :], xi[:, kc, :], yt[:, kc, :], op=AluOpType.add),
                                reads=[xi.name, yt.name], writes=[xi.name])
                            ctx.op("s", lambda E, kc=kc: E.activation(
                                xb[:, kc, :], xi[:, kc, :], AF.Copy),
                                reads=[xi.name], writes=[xb.name])
                        ctx.dma("y", fm(x1h, c0, TTW), xi[:], reads=[xi.name], writes=["x1h"])
                        for kc in range(NKC):
                            ctx.op("s", lambda E, kc=kc, xb=xb: E.activation(sq[:], xb[:, kc, :], AF.Square),
                                   reads=[xb.name], writes=[sq.name])
                            ctx.op("t", lambda E, kc=kc, xb=xb: E.matmul(
                                pum[:], onesb[:, 0:1], xb[:, kc, :],
                                start=(kc == 0), stop=(kc == NKC - 1), skip_group_check=True),
                                reads=[xb.name, "onesb"], writes=[pum.name])
                            ctx.op("t", lambda E, kc=kc: E.matmul(
                                pq2[:], onesb[:, 0:1], sq[:],
                                start=(kc == 0), stop=(kc == NKC - 1), skip_group_check=True),
                                reads=[sq.name, "onesb"], writes=[pq2.name])
                        mu = rowb[0:1, 0:TTW]
                        var = rowb[0:1, TTW:2 * TTW]
                        rn = rowb.name
                        ctx.op("v", lambda E: E.tensor_scalar(mu, pum[:], 1.0 / D, None, op0=AluOpType.mult),
                               reads=[pum.name], writes=[rn])
                        ctx.op("v", lambda E: E.scalar_tensor_tensor(
                            var, mu, 1.0, mu, op0=AluOpType.mult, op1=AluOpType.mult),
                            reads=[rn], writes=[rn])
                        ctx.op("v", lambda E: E.scalar_tensor_tensor(
                            var, pq2[:], 1.0 / D, var, op0=AluOpType.mult, op1=AluOpType.subtract),
                            reads=[pq2.name, rn], writes=[rn])
                        ctx.op("v", lambda E: E.tensor_scalar(var, var, 1e-5, None, op0=AluOpType.add),
                               reads=[rn], writes=[rn])
                        ctx.op("v", lambda E: E.reciprocal(var, var), reads=[rn], writes=[rn])
                        ctx.op("s", lambda E: E.activation(var, var, AF.Sqrt), reads=[rn], writes=[rn])
                        ctx.op("v", lambda E: E.scalar_tensor_tensor(
                            mu, mu, -1.0, var, op0=AluOpType.mult, op1=AluOpType.mult),
                            reads=[rn], writes=[rn])
                        ctx.op("t", lambda E: E.matmul(
                            pbc[:, 0:512], onesfr[0:1, 0:128], var,
                            start=True, stop=True, skip_group_check=True),
                            reads=["onesfr", rn], writes=[pbc.name])
                        ctx.op("t", lambda E: E.matmul(
                            pbc[:, 512:1024], onesfr[0:1, 0:128], mu,
                            start=True, stop=True, skip_group_check=True),
                            reads=["onesfr", rn], writes=[pbc.name])
                        for kc in range(NKC):
                            ctx.op("v", lambda E, kc=kc: E.tensor_tensor(
                                t1[:], xi[:, kc, :], pbc[:, 0:512], op=AluOpType.mult),
                                reads=[xi.name, pbc.name], writes=[t1.name])
                            ctx.op("v", lambda E, kc=kc: E.tensor_tensor(
                                xnf[:, kc, :], t1[:], pbc[:, 512:1024], op=AluOpType.add),
                                reads=[t1.name, pbc.name], writes=[xnf.name])
                            ctx.op("s", lambda E, kc=kc: E.activation(
                                xnb[:, kc, :], xnf[:, kc, :], AF.Copy),
                                reads=[xnf.name], writes=[xnb.name])
                        ctx.dma("y", fm(xn1h, c0, TTW), xnb[:], reads=[xnb.name], writes=["xn1h"])
                        for tc in range(4):
                            for kc in range(NKC):
                                ctx.op("t", lambda E, tc=tc, kc=kc: E.matmul(
                                    prt[:, 0:NEXP], xnf[:, kc, tc * 128:tc * 128 + 128],
                                    rwT[:, kc, :],
                                    start=(kc == 0), stop=(kc == NKC - 1), skip_group_check=True),
                                    reads=[rwT.name, xnf.name], writes=[prt.name])
                            ctx.op("s", lambda E, tc=tc: E.activation(
                                sgt[:, tc, :], prt[:, 0:NEXP], AF.Sigmoid),
                                reads=[prt.name], writes=[sgt.name])
                        for tc in range(4):
                            ctx.op("v", lambda E, tc=tc: E.tensor_tensor(
                                sgbt[:], sgt[:, tc, :], ebrep[:], op=AluOpType.add),
                                reads=[sgt.name, ebrep.name], writes=[sgbt.name])
                            ctx.op("v", lambda E: E.reduce_max(m1c[:], sgbt[:], axis=AX.X),
                                   reads=[sgbt.name], writes=[m1c.name])
                            ctx.op("v", lambda E: E.tensor_scalar(
                                mskt[:], sgbt[:], m1c[:], None, op0=AluOpType.is_ge),
                                reads=[sgbt.name, m1c.name], writes=[mskt.name])
                            ctx.op("v", lambda E: E.scalar_tensor_tensor(
                                s2t[:], mskt[:], -1e30, sgbt[:],
                                op0=AluOpType.mult, op1=AluOpType.add),
                                reads=[mskt.name, sgbt.name], writes=[s2t.name])
                            ctx.op("v", lambda E: E.reduce_max(m1c[:], s2t[:], axis=AX.X),
                                   reads=[s2t.name], writes=[m1c.name])
                            ctx.op("v", lambda E: E.tensor_scalar(
                                mskt[:], sgbt[:], m1c[:], None, op0=AluOpType.is_ge),
                                reads=[sgbt.name, m1c.name], writes=[mskt.name])
                            ctx.op("v", lambda E, tc=tc: E.tensor_tensor(
                                gtt[:], sgt[:, tc, :], mskt[:], op=AluOpType.mult),
                                reads=[sgt.name, mskt.name], writes=[gtt.name])
                            ctx.op("t", lambda E, tc=tc: E.transpose(
                                pgt[0:NEXP, 0:128], gtt[:], ident[:]),
                                reads=[gtt.name, "ident"], writes=[pgt.name])
                            ctx.op("s", lambda E, tc=tc: E.activation(
                                gout[:, tc * 128:tc * 128 + 128], pgt[0:NEXP, 0:128], AF.Copy),
                                reads=[pgt.name], writes=[gout.name])
                        ctx.dma("y", gatesd[:, c0:c0 + TTW], gout[:],
                                reads=[gout.name], writes=["gatesd"])
        phase(ph_x1)


        def ph_l1kv():
            with ExitStack() as stk:
                w1s = sb(stk, [128, NKC, D], BF16)
                w2s = sb(stk, [128, NKC, D], BF16)
                w3s = sb(stk, [128, NKC, D], BF16)
                b1c = sb(stk, [128, NKC], F32)
                b2c = sb(stk, [128, NKC], F32)
                b3r = sb(stk, [1, D], BF16)
                mkrun = sb(stk, [1, 1], F32)
                t11 = sb(stk, [1, 1], F32)
                nkrow = sb(stk, [1, TTW], F32)
                ctx.dma("g", w1s[:], l1a_w[1].rearrange("(c p) e -> p c e", c=NKC), writes=[w1s.name])
                ctx.dma("g", w2s[:], l1a_w[2].rearrange("(c p) e -> p c e", c=NKC), writes=[w2s.name])
                ctx.dma("g", w3s[:], l1a_w[3].rearrange("(c p) e -> p c e", c=NKC), writes=[w3s.name])
                ctx.dma("y", b1c[:], l1a_b[1], writes=[b1c.name])
                ctx.dma("y", b2c[:], l1a_b[2], writes=[b2c.name])
                ctx.dma("g", b3r[:], l1a_b3r, writes=[b3r.name])
                ctx.op("v", lambda E: E.memset(mkrun[:], NEG), writes=[mkrun.name])
                xnt_pr = [sb(stk, [128, NKC, TTW], BF16) for _ in range(2)]
                kfb = sb(stk, [128, TTW], BF16)
                sqk = sb(stk, [128, TTW], BF16)
                vtb = sb(stk, [128, 4, D], BF16)
                pk = ps(stk, [128, 512])
                pk2 = ps(stk, [128, 512])
                pnk = ps(stk, [1, 512])
                for i in range(NTO):
                    c0 = i * TTW
                    if True:
                        xnt = xnt_pr[i % 2]
                        ctx.dma("y", xnt[:], fm(xn1h, c0, TTW), reads=["xn1h"], writes=[xnt.name])
                        for oc in range(NKC):
                            pK = (pk, pk2)[oc % 2]
                            kQ = (kfb, sqk)[oc % 2]
                            for kc in range(NKC):
                                ctx.op("t", lambda E, oc=oc, kc=kc, xnt=xnt, pK=pK: E.matmul(
                                    pK[:], w1s[:, kc, oc * 128:oc * 128 + 128], xnt[:, kc, :],
                                    start=(kc == 0), stop=(kc == NKC - 1), skip_group_check=True),
                                    reads=[w1s.name, xnt.name], writes=[pK.name])
                            ctx.op("s", lambda E, oc=oc, pK=pK, kQ=kQ: E.activation(
                                kQ[:], pK[:], AF.Identity, bias=b1c[:, oc:oc + 1]),
                                reads=[pK.name, b1c.name], writes=[kQ.name])
                            ctx.dma("y", qd[oc * 128:oc * 128 + 128, c0:c0 + TTW], kQ[:],
                                    reads=[kQ.name], writes=["qd"])
                        for oc in range(NKC):
                            for kc in range(NKC):
                                ctx.op("t", lambda E, oc=oc, kc=kc, xnt=xnt: E.matmul(
                                    pk[:], w2s[:, kc, oc * 128:oc * 128 + 128], xnt[:, kc, :],
                                    start=(kc == 0), stop=(kc == NKC - 1), skip_group_check=True),
                                    reads=[w2s.name, xnt.name], writes=[pk.name])
                            ctx.op("s", lambda E, oc=oc: E.activation(
                                kfb[:], pk[:], AF.Identity, bias=b2c[:, oc:oc + 1]),
                                reads=[pk.name, b2c.name], writes=[kfb.name])
                            ctx.dma("y", agk_in[oc * 128:oc * 128 + 128, c0:c0 + TTW], kfb[:],
                                    reads=[kfb.name], writes=["agk_in"])
                            ctx.op("s", lambda E, oc=oc: E.activation(sqk[:], kfb[:], AF.Square),
                                   reads=[kfb.name], writes=[sqk.name])
                            ctx.op("t", lambda E, oc=oc: E.matmul(
                                pnk[:], onesb[:, 0:1], sqk[:],
                                start=(oc == 0), stop=(oc == NKC - 1), skip_group_check=True),
                                reads=[sqk.name, "onesb"], writes=[pnk.name])
                        ctx.op("v", lambda E: E.tensor_copy(nkrow[:], pnk[:]),
                               reads=[pnk.name], writes=[nkrow.name])
                        ctx.op("v", lambda E: E.reduce_max(t11[:], nkrow[:], axis=AX.X),
                               reads=[nkrow.name], writes=[t11.name])
                        ctx.op("v", lambda E: E.tensor_tensor(
                            mkrun[:], mkrun[:], t11[:], op=AluOpType.max),
                            reads=[mkrun.name, t11.name], writes=[mkrun.name])
                        for tch in range(4):
                            for nt in range(2):
                                for kc in range(NKC):
                                    ctx.op("t", lambda E, tch=tch, nt=nt, kc=kc, xnt=xnt: E.matmul(
                                        pk[:], xnt[:, kc, tch * 128:tch * 128 + 128],
                                        w3s[:, kc, nt * 512:nt * 512 + 512],
                                        start=(kc == 0), stop=False, skip_group_check=True),
                                        reads=[w3s.name, xnt.name], writes=[pk.name])
                                ctx.op("t", lambda E, nt=nt: E.matmul(
                                    pk[:], onesb[0:1, 0:128], b3r[0:1, nt * 512:nt * 512 + 512],
                                    start=False, stop=True, skip_group_check=True),
                                    reads=["onesb", b3r.name], writes=[pk.name])
                                ctx.op("s", lambda E, tch=tch, nt=nt: E.activation(
                                    vtb[:, tch, nt * 512:nt * 512 + 512], pk[:], AF.Copy),
                                    reads=[pk.name], writes=[vtb.name])
                        for tch in range(4):
                            ctx.dma("y", agv_in[c0 + tch * 128:c0 + tch * 128 + 128, :],
                                    vtb[:, tch, :], reads=[vtb.name], writes=["agv_in"])
                ctx.op("s", lambda E: E.activation(mkrun[:], mkrun[:], AF.Sqrt),
                       reads=[mkrun.name], writes=[mkrun.name])
                ctx.dma("y", mk_in, mkrun[:], reads=[mkrun.name], writes=["mk_in"])
        phase(ph_l1kv)

        def ph_coll():
            ctx.op("g", lambda E: E.collective_compute(
                "AllGather", mybir.AluOpType.bypass, replica_groups=rg,
                ins=[agk_in], outs=[agk_out]),
                reads=["agk_in"], writes=["agk_out"], coll=True)
            ctx.op("g", lambda E: E.collective_compute(
                "AllGather", mybir.AluOpType.bypass, replica_groups=rg,
                ins=[agv_in], outs=[agv_out]),
                reads=["agv_in"], writes=["agv_out"], coll=True)
            ctx.op("g", lambda E: E.collective_compute(
                "AllReduce", mybir.AluOpType.max, replica_groups=rg,
                ins=[mk_in], outs=[mk_out]),
                reads=["mk_in"], writes=["mk_out"], coll=True)
        phase(ph_coll)

        def ph_ub():
            with ExitStack() as stk:
                sqq = sb(stk, [128, TTW], BF16)
                qt = sb(stk, [128, NKC, TTW], BF16)
                nq = sb(stk, [1, TTW], F32)
                augsb = sb(stk, [1, TTW], BF16)
                c960 = sb(stk, [1, TTW], F32)
                pnq = ps(stk, [1, 512])
                ctx.dma("y", mks[:], mk_out, reads=["mk_out"], writes=["mks"])
                ctx.op("v", lambda E: E.memset(c960[:], 32.0 * UB_SHIFT), writes=[c960.name])
                for ct in range(NTO):
                    c0 = ct * TTW
                    ctx.dma("y", qt[:], fm(qd, c0, TTW), reads=["qd"], writes=[qt.name])
                    for oc in range(NKC):
                        ctx.op("s", lambda E, oc=oc: E.activation(
                            sqq[:], qt[:, oc, :], AF.Square),
                            reads=[qt.name], writes=[sqq.name])
                        ctx.op("t", lambda E, oc=oc: E.matmul(
                            pnq[:], onesb[:, 0:1], sqq[:],
                            start=(oc == 0), stop=(oc == NKC - 1), skip_group_check=True),
                            reads=[sqq.name, "onesb"], writes=[pnq.name])
                    ctx.op("s", lambda E: E.activation(nq[:], pnq[:], AF.Sqrt),
                           reads=[pnq.name], writes=[nq.name])
                    ctx.op("v", lambda E: E.scalar_tensor_tensor(
                        nq[:], nq[:], mks[0:1, 0:1], c960[:],
                        op0=AluOpType.mult, op1=AluOpType.subtract),
                        reads=[nq.name, "mks", c960.name], writes=[nq.name])
                    ctx.op("s", lambda E, c0=c0: E.activation(
                        augsb[:], nq[:], AF.Identity, scale=-1.0),
                        reads=[nq.name], writes=[augsb.name])
                    ctx.dma("y", augd[0:1, c0:c0 + TTW], augsb[:],
                            reads=[augsb.name], writes=["augd"])
        phase(ph_ub)

        def ph_l1attn():
            QH = TOWN // 2
            NQB = QH // 256
            NKG = S // 1024
            with ExitStack() as stk:
                w4s = sb(stk, [128, NKC, D], BF16)
                b4c = sb(stk, [128, NKC], F32)
                ctx.dma("g", w4s[:], l1a_w[4].rearrange("(c p) e -> p c e", c=NKC), writes=[w4s.name])
                ctx.dma("y", b4c[:], l1a_b[4], writes=[b4c.name])
                acc = sb(stk, [128, NKC, QH], F32)
                lacc = sb(stk, [1, QH], F32)
                lrb = sb(stk, [128, QH], BF16)
                att1 = sb(stk, [128, NKC, TTW], BF16)
                qbuf_pr = [sb(stk, [128, NKC, 256], BF16) for _ in range(2)]
                ptq = sb(stk, [128, 8, 256], BF16)
                kgt_pr = [sb(stk, [128, NKC, 1024], BF16) for _ in range(2)]
                vgt_pr = [sb(stk, [128, NKC, 1024], BF16) for _ in range(2)]
                x1t = sb(stk, [128, NKC, TTW], F32)
                augh = sb(stk, [1, QH], BF16)
                pstA = ps(stk, [128, 512])
                pstB = ps(stk, [128, 512])
                pav = ps(stk, [128, NKC, 256])
                plq = ps(stk, [1, 512])
                x2sb = x1t
                pst2 = (pstA, pstB)
                for qh in range(2):
                    q0 = qh * QH
                    if True:
                        ctx.op("v", lambda E: E.memset(acc[:], 0.0), writes=[acc.name])
                        ctx.dma("y", augh[:], augd[0:1, q0:q0 + QH],
                                reads=["augd"], writes=[augh.name])
                        ctx.op("v", lambda E: E.memset(lacc[:], 0.0), writes=[lacc.name])
                        for kg in range(NKG):
                            rblk = (1024 * kg) // TOWN
                            coff = 1024 * kg - TOWN * rblk
                            kgt, vgt = kgt_pr[kg % 2], vgt_pr[kg % 2]
                            ctx.dma("g", kgt[:],
                                    agk_out[D * rblk:D * (rblk + 1), coff:coff + 1024]
                                    .rearrange("(c p) t -> p c t", c=NKC),
                                    reads=["agk_out"], writes=[kgt.name])
                            ctx.dma("g", vgt[:],
                                    agv_out[1024 * kg:1024 * kg + 1024, :]
                                    .rearrange("(c p) e -> p c e", c=NKC),
                                    reads=["agv_out"], writes=[vgt.name])
                            for qb in range(NQB):
                                qc0 = q0 + qb * 256
                                qbuf = qbuf_pr[qb % 2]
                                ctx.dma("y", qbuf[:], fm(qd, qc0, 256),
                                        reads=["qd"], writes=[qbuf.name])
                                for kt in range(8):
                                    pS = pst2[kt % 2]
                                    for ec in range(NKC):
                                        ctx.op("t", lambda E, kt=kt, ec=ec, pS=pS, kgt=kgt, qbuf=qbuf: E.matmul(
                                            pS[:, 0:256], kgt[:, ec, kt * 128:kt * 128 + 128],
                                            qbuf[:, ec, :],
                                            start=(ec == 0), stop=False, skip_group_check=True),
                                            reads=[kgt.name, qbuf.name], writes=[pS.name])
                                    ctx.op("t", lambda E, qc0=qc0, q0=q0, pS=pS: E.matmul(
                                        pS[:, 0:256], onesb[0:1, 0:128],
                                        augh[0:1, qc0 - q0:qc0 - q0 + 256],
                                        start=False, stop=True, skip_group_check=True),
                                        reads=["onesb", augh.name], writes=[pS.name])
                                    ctx.op("s", lambda E, kt=kt, pS=pS: E.activation(
                                        ptq[:, kt, :], pS[:, 0:256], AF.Exp, scale=L1_SCALE),
                                        reads=[pS.name], writes=[ptq.name + str(kt)])
                                for kt in range(8):
                                    ctx.op("t", lambda E, kt=kt: E.matmul(
                                        plq[0:1, 0:256], onesb[:, 0:1], ptq[:, kt, :],
                                        start=(kt == 0), stop=(kt == 7), skip_group_check=True),
                                        reads=["onesb", ptq.name + str(kt)], writes=[plq.name])
                                for dc in range(NKC):
                                    for kt in range(8):
                                        ctx.op("t", lambda E, kt=kt, dc=dc, vgt=vgt: E.matmul(
                                            pav[:, dc, :], vgt[:, kt, dc * 128:dc * 128 + 128],
                                            ptq[:, kt, :], start=(kt == 0), stop=(kt == 7), skip_group_check=True),
                                            reads=[vgt.name, ptq.name + str(kt)], writes=[pav.name])
                                    ctx.op("v", lambda E, dc=dc, qb=qb: E.scalar_tensor_tensor(
                                        acc[:, dc, qb * 256:qb * 256 + 256], pav[:, dc, :], 1.0,
                                        acc[:, dc, qb * 256:qb * 256 + 256],
                                        op0=AluOpType.mult, op1=AluOpType.add),
                                        reads=[pav.name, acc.name], writes=[acc.name])
                                qq = qb * 256
                                ctx.op("v", lambda E, qq=qq: E.scalar_tensor_tensor(
                                    lacc[0:1, qq:qq + 256], plq[0:1, 0:256], 1.0, lacc[0:1, qq:qq + 256],
                                    op0=AluOpType.mult, op1=AluOpType.add),
                                    reads=[plq.name, lacc.name], writes=[lacc.name])
                        ctx.op("v", lambda E: E.reciprocal(lacc[:], lacc[:]),
                               reads=[lacc.name], writes=[lacc.name])
                        ctx.dma("y", lscr2[0:QH].rearrange("(a b) -> a b", a=1), lacc[:],
                                reads=[lacc.name], writes=["lscr2"])
                        ctx.dma("g", lrb[:],
                                lscr2[0:QH].rearrange("(a b) -> a b", a=1)
                                .broadcast_to([128, QH]),
                                reads=["lscr2"], writes=[lrb.name])
                        for ct in range(QH // TTW):
                            c0 = q0 + ct * TTW
                            for dc in range(NKC):
                                ctx.op("v", lambda E, dc=dc, ct=ct: E.tensor_tensor(
                                    att1[:, dc, :], acc[:, dc, ct * TTW:ct * TTW + TTW],
                                    lrb[:, ct * TTW:ct * TTW + TTW], op=AluOpType.mult),
                                    reads=[acc.name, lrb.name], writes=[att1.name])
                            ctx.dma("y", x1t[:], fm(x1h, c0, TTW), reads=["x1h"], writes=[x1t.name])
                            for oc in range(NKC):
                                for ic in range(NKC):
                                    ctx.op("t", lambda E, oc=oc, ic=ic: E.matmul(
                                        pst2[0][:, 0:512],
                                        w4s[:, ic, oc * 128:oc * 128 + 128],
                                        att1[:, ic, :],
                                        start=(ic == 0), stop=(ic == NKC - 1), skip_group_check=True),
                                        reads=[w4s.name, att1.name], writes=[pstA.name])
                                ctx.op("v", lambda E, oc=oc: E.scalar_tensor_tensor(
                                    x2sb[:, oc, :], pst2[0][:, 0:512], b4c[:, oc:oc + 1],
                                    x1t[:, oc, :], op0=AluOpType.add, op1=AluOpType.add),
                                    reads=[pstA.name, b4c.name, x1t.name], writes=[x1t.name])
                            for oc in range(NKC):
                                ctx.dma("y", x2ph[oc * 128:oc * 128 + 128, c0:c0 + TTW],
                                        x2sb[:, oc, :], reads=[x1t.name], writes=[f"x2ph{oc}"])
        phase(ph_l1attn)

        def ph_moe():
            NTG = 2
            TG = TOWN // NTG
            NTT = TG // TTW
            parts = [("sq", q) for q in range(SHID // RH)] + [("e", e) for e in range(NEXP)]
            with ExitStack() as stk:
                pht = ps(stk, [128, NTT, 512])
                wsA = sb(stk, [128, NKC, 2 * RH], BF16)
                w3p = sb(stk, [128, NKC, D], BF16)
                xng_pr = [sb(stk, [128, NKC, TG], BF16) for _ in range(2)]
                h1s = sb(stk, [128, NTT, 512], BF16)
                glu8 = sb(stk, [128, NKC, NTT, 512], BF16)
                osb = sb(stk, [128, 512], F32)
                gbuf = sb(stk, [128, NTT, 512], F32)
                for (pk_, pidx) in parts:
                    if True:
                        if pk_ == "sq":
                            wa = wsA[:, :, 0:RH]
                            wb = wsA[:, :, RH:2 * RH]
                            ctx.dma("g", wa, l1f_sw1T[:, RH * pidx:RH * (pidx + 1)]
                                    .rearrange("(c p) e -> p c e", c=NKC), writes=[wsA.name])
                            ctx.dma("g", wb, l1f_sw2T[:, RH * pidx:RH * (pidx + 1)]
                                    .rearrange("(c p) e -> p c e", c=NKC), writes=[wsA.name])
                            ctx.dma("g", w3p[:], l1f_sw3T[RH * pidx:RH * (pidx + 1), :]
                                    .rearrange("(c p) e -> p c e", c=NKC), writes=[w3p.name])
                        else:
                            wa = wsA[:]
                            wb = None
                            ctx.dma("g", wsA[:], l1f_w12T[pidx]
                                    .rearrange("(c p) e -> p c e", c=NKC), writes=[wsA.name])
                            ctx.dma("g", w3p[:], l1f_w3T[pidx]
                                    .rearrange("(c p) e -> p c e", c=NKC), writes=[w3p.name])
                        for tg in range(NTG):
                            if True:
                                xng = xng_pr[tg % 2]
                                ctx.dma("y", xng[:], fm(xn1h, tg * TG, TG),
                                        reads=["xn1h"], writes=[xng.name])
                                if pk_ == "e":
                                    for t in range(NTT):
                                        ctx.dma("y", gbuf[:, t, :],
                                                gatesd[pidx:pidx + 1,
                                                       tg * TG + t * 512:tg * TG + t * 512 + 512]
                                                .broadcast_to([128, 512]),
                                                reads=["gatesd"], writes=[gbuf.name])
                                for oc in range(NKC):
                                    for kc in range(NKC):
                                        for t in range(NTT):
                                            ctx.op("t", lambda E, oc=oc, kc=kc, t=t, wa=wa, xng=xng: E.matmul(
                                                pht[:, t, :], wa[:, kc, oc * 128:oc * 128 + 128],
                                                xng[:, kc, t * 512:t * 512 + 512],
                                                start=(kc == 0), stop=(kc == NKC - 1),
                                                skip_group_check=True),
                                                reads=[wsA.name, xng.name], writes=[pht.name])
                                    for t in range(NTT):
                                        ctx.op("s", lambda E, t=t: E.activation(
                                            h1s[:, t, :], pht[:, t, :], AF.Silu),
                                            reads=[pht.name], writes=[h1s.name])
                                    woff = RH if pk_ == "e" else 0
                                    wsec = wa if pk_ == "e" else wb
                                    for kc in range(NKC):
                                        for t in range(NTT):
                                            ctx.op("t", lambda E, oc=oc, kc=kc, t=t, wsec=wsec, woff=woff, xng=xng: E.matmul(
                                                pht[:, t, :], wsec[:, kc, woff + oc * 128:woff + oc * 128 + 128],
                                                xng[:, kc, t * 512:t * 512 + 512],
                                                start=(kc == 0), stop=(kc == NKC - 1),
                                                skip_group_check=True),
                                                reads=[wsA.name, xng.name], writes=[pht.name])
                                    for t in range(NTT):
                                        ctx.op("v", lambda E, oc=oc, t=t: E.scalar_tensor_tensor(
                                            glu8[:, oc, t, :], pht[:, t, :], 1.0, h1s[:, t, :],
                                            op0=AluOpType.mult, op1=AluOpType.mult),
                                            reads=[pht.name, h1s.name], writes=[glu8.name])
                                        if pk_ == "e":
                                            ctx.op("v", lambda E, oc=oc, t=t: E.tensor_tensor(
                                                glu8[:, oc, t, :], glu8[:, oc, t, :], gbuf[:, t, :],
                                                op=AluOpType.mult),
                                                reads=[glu8.name, gbuf.name], writes=[glu8.name])
                                for oc2 in range(NKC):
                                    for hcc in range(NKC):
                                        for t in range(NTT):
                                            ctx.op("t", lambda E, oc2=oc2, hcc=hcc, t=t: E.matmul(
                                                pht[:, t, :], w3p[:, hcc, oc2 * 128:oc2 * 128 + 128],
                                                glu8[:, hcc, t, :],
                                                start=(hcc == 0), stop=(hcc == NKC - 1),
                                                skip_group_check=True),
                                                reads=[w3p.name, glu8.name], writes=[pht.name])
                                    for t in range(NTT):
                                        ctx.op("v", lambda E, t=t: E.tensor_copy(
                                            osb[:], pht[:, t, :]),
                                            reads=[pht.name], writes=[osb.name])
                                        c0 = tg * TG + t * 512
                                        ctx.dma("g", x2ph[oc2 * 128:oc2 * 128 + 128, c0:c0 + TTW],
                                                osb[:], reads=[osb.name], writes=[f"x2ph{oc2}"],
                                                accum_op=AluOpType.add)
        phase(ph_moe)

        def ph_out():
            with ExitStack() as stk:
                ot = sb(stk, [128, NKC, TTW], F32)
                for i in range(NTO):
                    c0 = i * TTW
                    ctx.dma("y", ot[:], fm(x2ph, c0, TTW),
                            reads=[f"x2ph{oc}" for oc in range(NKC)], writes=[ot.name])
                    ctx.dma("y", fm(out, c0, TTW), ot[:], reads=[ot.name], writes=["out"])
            t = ctx.cnt["y"]
            ctx.pend["y"].append([None, [("y", t)], False, 1, False, "y"])
            ctx.marks["y"].append((len(ctx.pend["y"]) - 1, (), ()))
            ctx.hwm[("y", "y")] = t
        phase(ph_out)

    return nc


def kernel(**inputs):
    nc = build_program()
    in_maps = [host_inputs(c, inputs) for c in range(NCORES)]
    res = run_bass_kernel_spmd(nc, in_maps, core_ids=list(range(NCORES)))
    outp = np.zeros((B, S, D), np.float32)
    for c in range(NCORES):
        b, r = c // 4, c % 4
        outp[b, r * TOWN:(r + 1) * TOWN, :] = np.asarray(res.results[c]["out"]).T
    return outp
